# version 06
# baseline (speedup 1.0000x reference)
import numpy as np
import ml_dtypes
from contextlib import ExitStack
import concourse.bass as bass
import concourse.mybir as mybir
from concourse.bass_utils import run_bass_kernel_spmd

F32 = mybir.dt.float32
BF16 = mybir.dt.bfloat16
I32 = mybir.dt.int32
I8 = mybir.dt.int8
AF = mybir.ActivationFunctionType
OP = mybir.AluOpType
BF16NP = ml_dtypes.bfloat16

NCORES = 8
T = 8192
TC = T // NCORES
D = 4096
DSL = D // NCORES
NST = 16
DTR = 128
DFF = 11008
NJ = DFF // 128
ND = D // 128
SC = 512
NSC = T // SC
NDN = DSL * NST // 128
EPS = 1e-5

NF4 = [-1.0, -0.6961928009986877, -0.5250730514526367, -0.39491748809814453,
       -0.28444138169288635, -0.18477343022823334, -0.09105003625154495, 0.0,
       0.07958029955625534, 0.16093020141124725, 0.2461123019456863,
       0.33791524171829224, 0.44070982933044434, 0.5626170039176941,
       0.7229568362236023, 1.0]

ENGS = ("sync", "gpsimd", "vector", "scalar", "tensor")


class Prog:

    def __init__(self, sems):
        self.queues = {e: [] for e in ENGS}
        self.sems = sems
        self.C = {k: 0 for k in sems}
        self.mark = {}

    def emit(self, eng, fn, waits=(), inc=None, inc_n=1):
        if inc is not None:
            self.C[inc] += inc_n
        wl = tuple((self.sems[n], v) for (n, v) in waits if v > 0)
        io = (self.sems[inc], inc_n) if inc is not None else None

        def thunk(e, fn=fn, wl=wl, io=io):
            for (s, v) in wl:
                e.wait_ge(s, v)
            r = fn(e)
            if io is not None:
                s, n = io
                if r is None:
                    e.sem_inc(s, n)
                else:
                    r.then_inc(s, n)
        self.queues[eng].append(thunk)
        return self.C[inc] if inc is not None else None

    def dma(self, eng, dst, src, waits=(), name=None):
        assert name is not None
        return self.emit(eng, lambda e, d=dst, s=src: e.dma_start(out=d, in_=s),
                         waits=waits, inc=name, inc_n=16)

    def snap(self, label, name):
        self.mark[label] = (name, self.C[name])

    def at(self, label):
        if label not in self.mark:
            return ()
        n, v = self.mark[label]
        return ((n, v),)


DEBUG_MAMBA = False


def build_forward(stop_after="full"):
    nc = bass.Bass("TRN2", target_bir_lowering=False, debug=False)

    x_tok = nc.dram_tensor("x_tok", [TC, D], F32, kind="ExternalInput").ap()
    x_ds = nc.dram_tensor("x_ds", [T, DSL], F32, kind="ExternalInput").ap()
    xpw = nc.dram_tensor("xpw", [DTR + 2 * NST, D], F32, kind="ExternalInput").ap()
    dtw = nc.dram_tensor("dtw", [DSL, DTR], F32, kind="ExternalInput").ap()
    dtb = nc.dram_tensor("dtb", [DSL, 1], F32, kind="ExternalInput").ap()
    a_pre = nc.dram_tensor("a_pre", [128, NDN], F32, kind="ExternalInput").ap()
    dvec = nc.dram_tensor("dvec", [1, D], F32, kind="ExternalInput").ap()
    e_cat = nc.dram_tensor("e_cat", [16, 128, 128], BF16, kind="ExternalInput").ap()
    f_cat = nc.dram_tensor("f_cat", [16, 128, 128], BF16, kind="ExternalInput").ap()
    g16 = nc.dram_tensor("g16", [16, 128], BF16, kind="ExternalInput").ap()
    eye_bf = nc.dram_tensor("eye_bf", [128, 128], BF16, kind="ExternalInput").ap()
    eye_f32i = nc.dram_tensor("eye_f32", [128, 128], F32, kind="ExternalInput").ap()
    full = stop_after == "full"
    if full:
        gc = nc.dram_tensor("gc", [NGU_PAD * 128, D], I8, kind="ExternalInput").ap()
        ga = nc.dram_tensor("ga", [NGU_PAD * 128, D // 64], F32, kind="ExternalInput").ap()
        uc = nc.dram_tensor("uc", [NGU_PAD * 128, D], I8, kind="ExternalInput").ap()
        ua = nc.dram_tensor("ua", [NGU_PAD * 128, D // 64], F32, kind="ExternalInput").ap()
        dcod = nc.dram_tensor("dcod", [4 * 128, DFF], I8, kind="ExternalInput").ap()
        dam = nc.dram_tensor("dam", [4 * 128, DFF // 64], F32, kind="ExternalInput").ap()
        gla = nc.dram_tensor("gla", [D, 16], F32, kind="ExternalInput").ap()
        glb = nc.dram_tensor("glb", [16, DFF], F32, kind="ExternalInput").ap()
        ula = nc.dram_tensor("ula", [D, 16], F32, kind="ExternalInput").ap()
        ulb = nc.dram_tensor("ulb", [16, DFF], F32, kind="ExternalInput").ap()
        dla = nc.dram_tensor("dla", [DFF, 16], F32, kind="ExternalInput").ap()
        dlb = nc.dram_tensor("dlb", [16, D], F32, kind="ExternalInput").ap()

    x_out = nc.dram_tensor("x_out", [TC, D], F32, kind="ExternalOutput").ap()

    if DEBUG_MAMBA:
        dbg_dtin = nc.dram_tensor("dbg_dtin", [128, T], F32, kind="ExternalOutput").ap()
        dbg_b = nc.dram_tensor("dbg_b", [16, T], F32, kind="ExternalOutput").ap()
        dbg_c = nc.dram_tensor("dbg_c", [16, T], F32, kind="ExternalOutput").ap()
        dbg_m = nc.dram_tensor("dbg_m", [128, 64], F32, kind="ExternalOutput").ap()
        dbg_r = nc.dram_tensor("dbg_r", [128, 64], F32, kind="ExternalOutput").ap()
        dbg_dt = nc.dram_tensor("dbg_dt", [128, 4 * SC], F32, kind="ExternalOutput").ap()
        dbg_carry = nc.dram_tensor("dbg_carry", [128, NDN], F32, kind="ExternalOutput").ap()
        dbg_da = nc.dram_tensor("dbg_da", [128, SC], F32, kind="ExternalOutput").ap()
        dbg_dbx = nc.dram_tensor("dbg_dbx", [128, SC], F32, kind="ExternalOutput").ap()
        dbg_u = nc.dram_tensor("dbg_u", [128, 4 * SC], F32, kind="ExternalOutput").ap()
        dbg_stats = nc.dram_tensor("dbg_stats", [128, 16], F32, kind="ExternalOutput").ap()
        dbg_stt = nc.dram_tensor("dbg_stt", [16, 128], F32, kind="ExternalOutput").ap()
        dbg_xp = nc.dram_tensor("dbg_xp", [128, 8 * 128], F32, kind="ExternalOutput").ap()
        dbg_yloc = nc.dram_tensor("dbg_yloc", [DSL, T], F32, kind="ExternalOutput").ap()
        dbg_h = nc.dram_tensor("dbg_h", [128, SC], F32, kind="ExternalOutput").ap()
        dbg_crep = nc.dram_tensor("dbg_crep", [128, SC], F32, kind="ExternalOutput").ap()
        dbg_cin = nc.dram_tensor("dbg_cin", [128, 1], F32, kind="ExternalOutput").ap()
        dbg_row = nc.dram_tensor("dbg_row", [128, NDN * SC], F32, kind="ExternalOutput").ap()
        dbg_row2 = nc.dram_tensor("dbg_row2", [128, NDN * SC], F32, kind="ExternalOutput").ap()
        dbg_u13 = nc.dram_tensor("dbg_u13", [128, 4 * SC], F32, kind="ExternalOutput").ap()
        dbg_brep = nc.dram_tensor("dbg_brep", [128, SC], F32, kind="ExternalOutput").ap()
        dbg_bsnap = nc.dram_tensor("dbg_bsnap", [8, 16, T], F32, kind="ExternalOutput").ap()
        dbg_crep2 = nc.dram_tensor("dbg_crep2", [128, SC], F32, kind="ExternalOutput").ap()
    if full:
        wsh_gu0 = nc.dram_tensor("wsh_gu0", [4 * 64 * 128, 128], BF16, kind="Internal").ap()
        wsh_gu1 = nc.dram_tensor("wsh_gu1", [7 * 64 * 128, 128], BF16, kind="Internal").ap()
        wsh_d = nc.dram_tensor("wsh_d", [4 * 86 * 128, 128], BF16, kind="Internal").ap()
        wfull_gu0 = nc.dram_tensor("wfull_gu0", [NCORES * 4 * 64 * 128, 128], BF16,
                                   kind="Internal").ap()
        wfull_gu1 = nc.dram_tensor("wfull_gu1", [NCORES * 7 * 64 * 128, 128], BF16,
                                   kind="Internal").ap()
        wfull_d = nc.dram_tensor("wfull_d", [NCORES * 4 * 86 * 128, 128], BF16,
                                 kind="Internal").ap()
    cc1_in = nc.dram_tensor("cc1_in", [162, TC], F32, kind="Internal").ap()
    cc1_out = nc.dram_tensor("cc1_out", [162 * NCORES, TC], F32, kind="Internal").ap()
    cc2_in = nc.dram_tensor("cc2_in", [DSL, T], BF16, kind="Internal").ap()
    cc2_out = nc.dram_tensor("cc2_out", [D, T], BF16, kind="Internal").ap()
    xmid_hbm = nc.dram_tensor("xmid_hbm", [TC, D], F32, kind="Internal").ap()

    with ExitStack() as _ctx:
        g_sb = _ctx.enter_context(nc.sbuf_tensor("g_sb", [16, 128], BF16))
        eyeb = _ctx.enter_context(nc.sbuf_tensor("eyeb", [128, 128], BF16))
        eyef = _ctx.enter_context(nc.sbuf_tensor("eyef", [128, 128], F32))
        stats = _ctx.enter_context(nc.sbuf_tensor("stats", [128, 16], F32))
        st3 = _ctx.enter_context(nc.sbuf_tensor("st3", [128, 16], F32))
        cst = _ctx.enter_context(nc.sbuf_tensor("cst", [128, 2], F32))
        if full:
            dq_cod0 = _ctx.enter_context(nc.sbuf_tensor("dq_cod0", [128, 512], I8))
            dq_cod1 = _ctx.enter_context(nc.sbuf_tensor("dq_cod1", [128, 512], I8))
            dq_amx0 = _ctx.enter_context(nc.sbuf_tensor("dq_amx0", [128, 8], F32))
            dq_amx1 = _ctx.enter_context(nc.sbuf_tensor("dq_amx1", [128, 8], F32))
            dq_acc = _ctx.enter_context(nc.sbuf_tensor("dq_acc", [128, 512], BF16))
            dq_tmp = _ctx.enter_context(nc.sbuf_tensor("dq_tmp", [128, 512], BF16))
            dq_wq = _ctx.enter_context(nc.sbuf_tensor("dq_wq", [128, 512], BF16))
            dq_ob = _ctx.enter_context(nc.sbuf_tensor("dq_ob", [128, 512], BF16))
        DMA_SEMS = ["LDC", "LX0", "LX1", "LP", "LS0", "LS1", "LT0", "LT1",
                    "LY0", "LY1", "LLA", "LW0", "LW1", "LXM", "LMA0", "LMA1",
                    "SY0", "SY1", "SXM0", "SXM1", "SXO0", "SXO1", "SPK",
                    "LR0", "LR1", "LQ0", "LQ1", "SQ0", "SQ1"]
        dma_sems = {n: _ctx.enter_context(nc.semaphore("s" + n)) for n in DMA_SEMS}
        sCC = _ctx.enter_context(nc.semaphore("sCC"))
        sPE1 = _ctx.enter_context(nc.semaphore("sPE1"))
        sPE3 = _ctx.enter_context(nc.semaphore("sPE3"))
        sPE4 = _ctx.enter_context(nc.semaphore("sPE4"))
        sAC = _ctx.enter_context(nc.semaphore("sAC"))
        sACz = _ctx.enter_context(nc.semaphore("sACz"))
        sDV1 = _ctx.enter_context(nc.semaphore("sDV1"))
        sDV2 = _ctx.enter_context(nc.semaphore("sDV2"))
        sDV3 = _ctx.enter_context(nc.semaphore("sDV3"))
        sDV4 = _ctx.enter_context(nc.semaphore("sDV4"))
        sDV5 = _ctx.enter_context(nc.semaphore("sDV5"))
        blk = _ctx.enter_context(nc.Block())
        sGP = _ctx.enter_context(nc.semaphore("sGP"))
        sem_map = dict(CC=sCC, GP=sGP, PE1=sPE1, PE3=sPE3, PE4=sPE4,
                       AC=sAC, ACz=sACz, DV1=sDV1, DV2=sDV2, DV3=sDV3,
                       DV4=sDV4, DV5=sDV5)
        sem_map.update(dma_sems)
        P = Prog(sem_map)
        emit, dma, snap, at, C, mark = P.emit, P.dma, P.snap, P.at, P.C, P.mark

        emit("vector", lambda e: e.memset(cst[:, 0:1], EPS))
        emit("vector", lambda e: e.memset(cst[:, 1:2], -0.5), inc="DV5")
        snap("cst", "DV5")

        dq_state = {"i": 0}
        dq_work = []
        if full:
            v_gu0 = wsh_gu0.rearrange("(a p) f -> a p f", p=128)
            v_gu1 = wsh_gu1.rearrange("(a p) f -> a p f", p=128)
            v_d = wsh_d.rearrange("(a p) f -> a p f", p=128)
            for jc in range(NGU_PAD):
                for mat, codes_t, am_t, uoff in ((0, gc, ga, 0), (1, uc, ua, 32)):
                    for ib in range(8):
                        dst = (v_gu0[(jc * 64 + uoff + ib * 4):
                                     (jc * 64 + uoff + ib * 4) + 4]
                               if jc < 4 else
                               v_gu1[((jc - 4) * 64 + uoff + ib * 4):
                                     ((jc - 4) * 64 + uoff + ib * 4) + 4])
                        dq_work.append((codes_t, am_t, jc, ib, 512, dst))
            for q in range(4):
                for ib in range(22):
                    iw = min(512, DFF - ib * 512)
                    dst = v_d[(q * 86 + ib * 4):(q * 86 + ib * 4) + iw // 128]
                    dq_work.append((dcod, dam, q, ib, iw, dst))

        def emit_dq(n, ptb, eng="vector"):
            for _ in range(n):
                it = dq_state["i"]
                if it >= len(dq_work):
                    return
                dq_state["i"] += 1
                codes_t, am_t, jc, ib, iw, dst = dq_work[it]
                nt = iw // 128
                nb = iw // 64
                lqs = f"LQ{it % 2}"
                sqs = f"SQ{it % 2}"
                lsem = "GP" if eng == "gpsimd" else "DV5"
                dq_cod = (dq_cod0, dq_cod1)[it % 2]
                dq_amx = (dq_amx0, dq_amx1)[it % 2]
                dma("sync", dq_cod[:, 0:iw],
                    codes_t[jc * 128:(jc + 1) * 128, ib * 512:ib * 512 + iw],
                    waits=at(f"dq_lut{it - 2}"), name=lqs)
                w_a = dma("sync", dq_amx[:, 0:nb],
                          am_t[jc * 128:(jc + 1) * 128, ib * 8:ib * 8 + nb],
                          name=lqs)
                csl = dq_cod[:, 0:iw]
                emit(eng, lambda e, csl=csl, iw=iw: e.tensor_scalar(
                    dq_acc[:, 0:iw], csl, 0.0, float(NF4[0]),
                    OP.is_equal, OP.mult),
                    waits=((lqs, w_a),))
                for k in range(1, 16):
                    emit(eng, lambda e, csl=csl, iw=iw, k=k: e.tensor_scalar(
                        dq_tmp[:, 0:iw], csl, float(k), float(NF4[k]),
                        OP.is_equal, OP.mult))
                    emit(eng, lambda e, iw=iw: e.tensor_add(
                        dq_acc[:, 0:iw], dq_acc[:, 0:iw], dq_tmp[:, 0:iw]))
                lut = emit(eng, lambda e, iw=iw, nb=nb, am=dq_amx: e.tensor_tensor(
                    dq_wq[:, 0:iw].rearrange("p (b z) -> p b z", z=64),
                    dq_acc[:, 0:iw].rearrange("p (b z) -> p b z", z=64),
                    am[:, 0:nb].unsqueeze(2)
                    .broadcast_to([128, nb, 64]), OP.mult),
                    waits=(("PE4", dq_state.get("tpm", 0)),), inc=lsem)
                snap(f"dq_lut{it}", lsem)
                tpm = None
                for s in range(nt):
                    tpm = emit("tensor", lambda e, s=s, ptb=ptb: e.transpose(
                        ptb[:, s * 128:(s + 1) * 128],
                        dq_wq[:, s * 128:(s + 1) * 128],
                        eyeb[:]),
                        waits=(((lsem, lut), ("DV4", C["DV4"])) if s == 0 else ()),
                        inc="PE4" if s == nt - 1 else None)
                dq_state["tpm"] = tpm
                ev = emit("vector", lambda e, iw=iw, ptb=ptb: e.tensor_copy(
                    dq_ob[:, 0:iw], ptb[:, 0:iw]),
                    waits=(("PE4", tpm),) + at(f"dq_st{it - 1}"), inc="DV4")
                dma("gpsimd", dst.transpose([1, 0, 2]),
                    dq_ob[:, 0:iw].rearrange("p (c f) -> p c f", f=128),
                    waits=(("DV4", ev),), name=sqs)
                snap(f"dq_st{it}", sqs)
                if dq_state["i"] == 64:
                    snap("dq_b0_0", "SQ0")
                    snap("dq_b0_1", "SQ1")
                if dq_state["i"] == 176:
                    snap("dq_b1_0", "SQ0")
                    snap("dq_b1_1", "SQ1")
                if dq_state["i"] == len(dq_work):
                    snap("dq_b2_0", "SQ0")
                    snap("dq_b2_1", "SQ1")

        with ExitStack() as _ctx:
            xpwT = _ctx.enter_context(nc.sbuf_tensor("xpwT", [128, 32 * 160], BF16))
            wstg = _ctx.enter_context(nc.sbuf_tensor("wstg", [128, 4096], F32))
            wstg2 = _ctx.enter_context(nc.sbuf_tensor("wstg2", [32, 4096], F32))
            wstgb = _ctx.enter_context(nc.sbuf_tensor("wstgb", [128, 4096], BF16))
            wstgb2 = _ctx.enter_context(nc.sbuf_tensor("wstgb2", [32, 4096], BF16))
            xt = _ctx.enter_context(nc.sbuf_tensor("xt", [128, 2 * D], F32))
            xlnb = _ctx.enter_context(nc.sbuf_tensor("xlnb", [128, 2 * D], BF16))
            xlnT1 = _ctx.enter_context(nc.sbuf_tensor("xlnT1", [128, D], BF16))
            bnbuf = _ctx.enter_context(nc.sbuf_tensor("bnbuf", [128, 48], F32))
            mv = _ctx.enter_context(nc.sbuf_tensor("mv", [128, 16], F32))
            xp_sb = _ctx.enter_context(nc.sbuf_tensor("xp_sb", [128, 8 * 128], F32))
            bxp_sb = _ctx.enter_context(nc.sbuf_tensor("bxp_sb", [16, 8 * 128], F32))
            cxp_sb = _ctx.enter_context(nc.sbuf_tensor("cxp_sb", [16, 8 * 128], F32))
            st_t = _ctx.enter_context(nc.sbuf_tensor("st_t", [16, 128], F32))
            p1t0 = _ctx.enter_context(nc.psum_tensor("p1t0", [128, 512], BF16))
            p1t1 = _ctx.enter_context(nc.psum_tensor("p1t1", [128, 512], BF16))
            pxd = _ctx.enter_context(nc.psum_tensor("pxd", [128, 128], F32))
            pxb = _ctx.enter_context(nc.psum_tensor("pxb", [16, 128], F32))
            pxc = _ctx.enter_context(nc.psum_tensor("pxc", [16, 128], F32))
            pst = _ctx.enter_context(nc.psum_tensor("pst", [16, 128], F32))
            p1t = (p1t0, p1t1)
            dma("sync", eyeb[:], eye_bf, name="LDC")
            dma("sync", eyef[:], eye_f32i, name="LDC")
            dma("sync", g_sb[:], g16, name="LDC")
            dma("sync", wstg[:], xpw[0:128, :], name="LDC")
            ldc_all = dma("sync", wstg2[:], xpw[128:160, :], name="LDC")
            emit("vector", lambda e: e.tensor_copy(wstgb[:], wstg[:]),
                 waits=(("LDC", ldc_all),), inc="DV5")
            c2 = emit("vector", lambda e: e.tensor_copy(wstgb2[:], wstg2[:]),
                      inc="DV5")
            for dc in range(32):
                emit("tensor", lambda e, dc=dc: e.transpose(
                    p1t0[:, 0:128], wstgb[:, dc * 128:(dc + 1) * 128], eyeb[:]),
                    waits=(("DV5", c2), ("DV4", C["DV4"])), inc="PE4")
                t2 = emit("tensor", lambda e, dc=dc: e.transpose(
                    p1t0[:, 128:160], wstgb2[:, dc * 128:(dc + 1) * 128],
                    eyeb[0:32, 0:32]),
                    inc="PE4")
                emit("vector", lambda e, dc=dc: e.tensor_copy(
                    xpwT[:, dc * 160:dc * 160 + 128], p1t0[:, 0:128]),
                    waits=(("PE4", t2),), inc="DV4")
                emit("vector", lambda e, dc=dc: e.tensor_copy(
                    xpwT[:, dc * 160 + 128:dc * 160 + 160], p1t0[:, 128:160]),
                    inc="DV4")

            for c in range(8):
                par = c % 2
                lxs = f"LX{c % 2}"
                w_x = dma("sync", xt[:, par * D:(par + 1) * D],
                          x_tok[c * 128:(c + 1) * 128, :],
                          waits=at(f"ph1_ln{c - 2}"), name=lxs)
                bnm = None
                for b in range(8):
                    bnm = emit("vector", lambda e, b=b, par=par: e.bn_stats(
                        bnbuf[:, 6 * b:6 * b + 6],
                        xt[:, par * D + 512 * b:par * D + 512 * (b + 1)]),
                        waits=((lxs, w_x),) if b == 0 else (),
                        inc="DV2" if b == 7 else None)
                ag = emit("vector", lambda e, c=c: e.bn_aggr(
                    mv[:, 2 * c:2 * c + 2], bnbuf[:]),
                    waits=(("DV2", bnm),), inc="DV2")
                m1 = emit("vector", lambda e, c=c: e.tensor_copy(
                    stats[:, c:c + 1], mv[:, 2 * c:2 * c + 1]),
                    waits=(("DV2", ag),), inc="DV5")
                emit("scalar", lambda e, c=c: e.activation(
                    mv[:, 2 * c + 1:2 * c + 2], mv[:, 2 * c + 1:2 * c + 2],
                    AF.Ln, bias=cst[:, 0:1]),
                    waits=(("DV5", m1),) + at("cst"))
                r1 = emit("scalar", lambda e, c=c: e.activation(
                    stats[:, 8 + c:9 + c], mv[:, 2 * c + 1:2 * c + 2],
                    AF.Exp, scale=cst[:, 1:2]),
                    inc="AC")
                lnm = emit("vector", lambda e, c=c, par=par: e.tensor_scalar(
                    xlnb[:, par * D:(par + 1) * D], xt[:, par * D:(par + 1) * D],
                    stats[:, c:c + 1], stats[:, 8 + c:9 + c], OP.subtract, OP.mult),
                    waits=(("AC", r1),) + at(f"ph1_tp{c - 2}"), inc="DV5")
                snap(f"ph1_ln{c}", "DV5")
                last_tp = None
                for b in range(8):
                    tb = b % 2
                    for s in range(4):
                        last_tp = emit("tensor", lambda e, b=b, s=s, par=par, tb=tb:
                                       e.transpose(
                            p1t[tb][:, s * 128:(s + 1) * 128],
                            xlnb[:, par * D + (4 * b + s) * 128:
                                 par * D + (4 * b + s + 1) * 128], eyeb[:]),
                            waits=((("DV5", lnm),) if (b == 0 and s == 0) else ()) +
                                  ((("DV4", C["DV4"]),) if s == 0 else ()),
                            inc="PE4" if s == 3 else None)
                    emit("vector", lambda e, b=b, tb=tb: e.tensor_copy(
                        xlnT1[:, b * 512:(b + 1) * 512], p1t[tb][:]),
                        waits=(("PE4", last_tp),), inc="DV4")
                snap(f"ph1_tp{c}", "PE4")
                ev = C["DV4"]
                mmx = None
                for dc in range(32):
                    emit("tensor", lambda e, dc=dc: e.matmul(
                        pxd[:], xpwT[:, dc * 160:dc * 160 + 128],
                        xlnT1[:, dc * 128:(dc + 1) * 128],
                        start=(dc == 0), stop=(dc == 31)),
                        waits=((("DV4", ev),) + at(f"ph1_xpe{c - 1}")) if dc == 0 else ())
                    emit("tensor", lambda e, dc=dc: e.matmul(
                        pxb[:], xpwT[:, dc * 160 + 128:dc * 160 + 144],
                        xlnT1[:, dc * 128:(dc + 1) * 128],
                        start=(dc == 0), stop=(dc == 31)))
                    mmx = emit("tensor", lambda e, dc=dc: e.matmul(
                        pxc[:], xpwT[:, dc * 160 + 144:dc * 160 + 160],
                        xlnT1[:, dc * 128:(dc + 1) * 128],
                        start=(dc == 0), stop=(dc == 31)),
                        inc="PE4" if dc == 31 else None)
                emit("vector", lambda e, c=c: e.tensor_copy(
                    xp_sb[:, c * 128:(c + 1) * 128], pxd[:]),
                    waits=(("PE4", mmx),), inc="DV4")
                emit("vector", lambda e, c=c: e.tensor_copy(
                    bxp_sb[:, c * 128:(c + 1) * 128], pxb[:]), inc="DV4")
                emit("vector", lambda e, c=c: e.tensor_copy(
                    cxp_sb[:, c * 128:(c + 1) * 128], pxc[:]), inc="DV4")
                snap(f"ph1_xpe{c}", "DV4")

            if DEBUG_MAMBA:
                dma("gpsimd", dbg_stats, stats[:],
                    waits=(("DV4", C["DV4"]), ("AC", C["AC"])), name="SPK")
                dma("gpsimd", dbg_xp, xp_sb[:], waits=(("DV4", C["DV4"]),), name="SPK")
            stt = emit("tensor", lambda e: e.transpose(pst[:], stats[:], eyef[:]),
                       waits=(("DV4", C["DV4"]), ("AC", C["AC"])), inc="PE4")
            stev = emit("vector", lambda e: e.tensor_copy(st_t[:], pst[:]),
                 waits=(("PE4", stt),), inc="DV4")
            if DEBUG_MAMBA:
                dma("gpsimd", dbg_stt, st_t[:], waits=(("DV4", stev),), name="SPK")
            dma("gpsimd", cc1_in[0:128, :], xp_sb[:], waits=(("DV4", C["DV4"]),), name="SPK")
            dma("gpsimd", cc1_in[128:144, :], bxp_sb[:], name="SPK")
            dma("gpsimd", cc1_in[144:160, :], cxp_sb[:], name="SPK")
            dma("gpsimd", cc1_in[160:161, :].rearrange("o (c f) -> (o c) f", f=128),
                st_t[0:8, :], name="SPK")
            dma("gpsimd", cc1_in[161:162, :].rearrange("o (c f) -> (o c) f", f=128),
                st_t[8:16, :], name="SPK")
            cc1_done = emit("gpsimd", lambda e: e.collective_compute(
                "AllGather", OP.bypass, replica_groups=[list(range(NCORES))],
                ins=[cc1_in.opt()], outs=[cc1_out.opt()]),
                waits=(("SPK", C["SPK"]),), inc="CC")

        with ExitStack() as _ctx:
            dtwT = _ctx.enter_context(nc.sbuf_tensor("dtwT", [128, 4 * 128], BF16))
            dtb_sb = _ctx.enter_context(nc.sbuf_tensor("dtb_sb", [128, 4], F32))
            acol = _ctx.enter_context(nc.sbuf_tensor("acol", [128, NDN], F32))
            e_sb = _ctx.enter_context(nc.sbuf_tensor("e_sb", [128, 16 * 128], BF16))
            f_sb = _ctx.enter_context(nc.sbuf_tensor("f_sb", [128, 16 * 128], BF16))
            mfull = _ctx.enter_context(nc.sbuf_tensor("mfull", [128, 64], F32))
            rfull = _ctx.enter_context(nc.sbuf_tensor("rfull", [128, 64], F32))
            carry = _ctx.enter_context(nc.sbuf_tensor("carry", [128, NDN], F32))
            dtin = _ctx.enter_context(nc.sbuf_tensor("dtin", [128, T], BF16))
            b_sb = _ctx.enter_context(nc.sbuf_tensor("b_sb", [16, T], BF16))
            c_sb = _ctx.enter_context(nc.sbuf_tensor("c_sb", [16, T], BF16))
            ldstg = _ctx.enter_context(nc.sbuf_tensor("ldstg", [128, 1024], F32))
            ldstg2 = _ctx.enter_context(nc.sbuf_tensor("ldstg2", [16, 2048], F32))
            xds = _ctx.enter_context(nc.sbuf_tensor("xds", [128, 8 * SC], F32))
            xlnds = _ctx.enter_context(nc.sbuf_tensor("xlnds", [128, 8 * SC], BF16))
            xlnT = _ctx.enter_context(nc.sbuf_tensor("xlnT", [128, 8 * SC], BF16))
            dt_sb = _ctx.enter_context(nc.sbuf_tensor("dt_sb", [128, 8 * SC], F32))
            dtbf = _ctx.enter_context(nc.sbuf_tensor("dtbf", [128, 8 * SC], BF16))
            u_sb = _ctx.enter_context(nc.sbuf_tensor("u_sb", [128, 8 * SC], BF16))
            brep = _ctx.enter_context(nc.sbuf_tensor("brep", [128, 2 * SC], BF16))
            crep = _ctx.enter_context(nc.sbuf_tensor("crep", [128, 2 * SC], BF16))
            zb_sb = _ctx.enter_context(nc.sbuf_tensor("zb_sb", [128, 2 * SC], F32))
            sp_r = _ctx.enter_context(nc.sbuf_tensor("sp_r", [128, 2 * SC], F32))
            sp_l = _ctx.enter_context(nc.sbuf_tensor("sp_l", [128, 2 * SC], F32))
            dA = _ctx.enter_context(nc.sbuf_tensor("dA", [128, 2 * SC], F32))
            dbx = _ctx.enter_context(nc.sbuf_tensor("dbx", [128, 2 * SC], BF16))
            h_sb = _ctx.enter_context(nc.sbuf_tensor("h_sb", [128, 2 * SC], BF16))
            hc_sb = _ctx.enter_context(nc.sbuf_tensor("hc_sb", [128, 2 * SC], BF16))
            ysb = _ctx.enter_context(nc.sbuf_tensor("ysb", [128, 2 * SC], BF16))
            ub_sb = _ctx.enter_context(nc.sbuf_tensor("ub_sb", [128, 2 * SC], BF16))
            pA0 = _ctx.enter_context(nc.psum_tensor("pA0", [128, SC], F32))
            pA1 = _ctx.enter_context(nc.psum_tensor("pA1", [128, SC], F32))
            pB0 = _ctx.enter_context(nc.psum_tensor("pB0", [128, SC], F32))
            pB1 = _ctx.enter_context(nc.psum_tensor("pB1", [128, SC], F32))
            pY0 = _ctx.enter_context(nc.psum_tensor("pY0", [128, SC], F32))
            pY1 = _ctx.enter_context(nc.psum_tensor("pY1", [128, SC], F32))
            pTb0 = _ctx.enter_context(nc.psum_tensor("pTb0", [128, SC], BF16))
            pTb1 = _ctx.enter_context(nc.psum_tensor("pTb1", [128, SC], BF16))
            pA, pB, pY, pTb = (pA0, pA1), (pB0, pB1), (pY0, pY1), (pTb0, pTb1)
            for v in range(16):
                dma("sync", e_sb[:, v * 128:(v + 1) * 128], e_cat[v], name="LP",
                    waits=(("CC", cc1_done),) if v == 0 else ())
                dma("sync", f_sb[:, v * 128:(v + 1) * 128], f_cat[v], name="LP")
            dma("sync", acol[:], a_pre, name="LP")
            for dc in range(4):
                dma("sync", dtb_sb[:, dc:dc + 1], dtb[dc * 128:(dc + 1) * 128, :],
                    name="LP")
            prep_efa = C["LP"]
            a1 = emit("scalar", lambda e: e.activation(acol[:], acol[:], AF.Exp),
                      waits=(("LP", prep_efa),), inc="AC")
            acneg = emit("vector", lambda e: e.tensor_scalar_mul(acol[:], acol[:], -1.0),
                         waits=(("AC", a1),), inc="DV5")
            wd1 = dma("sync", ldstg[:, 0:512].rearrange("p (c f) -> p c f", f=128),
                      dtw.rearrange("(c p) f -> p c f", p=128), name="LP")
            cd1 = emit("vector", lambda e: e.tensor_copy(dtbf[:, 0:512], ldstg[:, 0:512]),
                       waits=(("LP", wd1),), inc="DV5")
            for dc in range(4):
                td = emit("tensor", lambda e, dc=dc: e.transpose(
                    pTb0[:, 0:128], dtbf[:, dc * 128:(dc + 1) * 128], eyeb[:]),
                    waits=(("DV5", cd1), ("DV4", C["DV4"])), inc="PE4")
                emit("vector", lambda e, dc=dc: e.tensor_copy(
                    dtwT[:, dc * 128:(dc + 1) * 128], pTb0[:, 0:128]),
                    waits=(("PE4", td),), inc="DV4")
            for j in range(NCORES):
                dma("sync", ldstg[:, 0:1024], cc1_out[162 * j:162 * j + 128, :],
                    waits=(("CC", cc1_done), ("DV5", cd1)) if j == 0
                    else at("prep_cast2"), name="LP")
                dma("sync", ldstg2[:, 0:1024], cc1_out[162 * j + 128:162 * j + 144, :],
                    name="LP")
                dma("sync", ldstg2[:, 1024:2048],
                    cc1_out[162 * j + 144:162 * j + 160, :], name="LP")
                for q in range(8):
                    dma("sync", mfull[:, 8 * j + q:8 * j + q + 1],
                        cc1_out[162 * j + 160:162 * j + 161, 128 * q:128 * (q + 1)]
                        .rearrange("o f -> (o f)").unsqueeze(1), name="LP")
                    wj = dma("sync", rfull[:, 8 * j + q:8 * j + q + 1],
                             cc1_out[162 * j + 161:162 * j + 162, 128 * q:128 * (q + 1)]
                             .rearrange("o f -> (o f)").unsqueeze(1), name="LP")
                emit("vector", lambda e, j=j: e.tensor_copy(
                    dtin[:, j * 1024:(j + 1) * 1024], ldstg[:, 0:1024]),
                    waits=(("LP", wj),), inc="DV5")
                emit("vector", lambda e, j=j: e.tensor_copy(
                    b_sb[:, j * 1024:(j + 1) * 1024], ldstg2[:, 0:1024]),
                    inc="DV5")
                emit("vector", lambda e, j=j: e.tensor_copy(
                    c_sb[:, j * 1024:(j + 1) * 1024], ldstg2[:, 1024:2048]),
                    inc="DV5")
                snap("prep_cast2", "DV5")
            prep_ld = C["LP"]
            if DEBUG_MAMBA:
                dma("gpsimd", dbg_dtin, dtin[:], waits=(("LP", prep_ld), ("DV5", C["DV5"])), name="SPK")
                dma("gpsimd", dbg_m, mfull[:], name="SPK")
                dma("gpsimd", dbg_r, rfull[:], name="SPK")

            for tc in range(NSC):
                par = tc % 2
                lss = f"LS{tc % 2}"
                w_x = None
                for q in range(4):
                    w_x = dma("sync", xds[:, (par * 4 + q) * SC:(par * 4 + q + 1) * SC],
                              x_ds.rearrange("(a p) f -> a p f", p=128)[tc * 4 + q],
                              waits=at(f"lnq{tc - 2}") if q == 0 else (), name=lss)
                lnq = None
                for q in range(4):
                    lnq = emit("vector", lambda e, q=q, par=par, tc=tc: e.tensor_scalar(
                        xlnds[:, (par * 4 + q) * SC:(par * 4 + q + 1) * SC],
                        xds[:, (par * 4 + q) * SC:(par * 4 + q + 1) * SC],
                        mfull[:, tc * 4 + q:tc * 4 + q + 1],
                        rfull[:, tc * 4 + q:tc * 4 + q + 1], OP.subtract, OP.mult),
                        waits=((lss, w_x),) +
                              ((("LP", prep_ld),) if tc == 0 else ()) +
                              (at(f"scan_tp{tc - 2}") if q == 0 else ()),
                        inc="DV5")
                snap(f"lnq{tc}", "DV5")
                last_tp = None
                for dsub in range(4):
                    tb = 0
                    for q in range(4):
                        last_tp = emit("tensor",
                                       lambda e, q=q, dsub=dsub, par=par, tb=tb:
                                       e.transpose(
                            pTb[tb][:, q * 128:(q + 1) * 128],
                            xlnds[:, (par * 4 + q) * SC + dsub * 128:
                                  (par * 4 + q) * SC + (dsub + 1) * 128], eyeb[:]),
                            waits=((("DV5", lnq),) if (dsub == 0 and q == 0) else ()) +
                                  ((("DV4", C["DV4"]),) if q == 0 else ()),
                            inc="PE4" if q == 3 else None)
                    emit("vector", lambda e, dsub=dsub, par=par, tb=tb: e.tensor_copy(
                        xlnT[:, (par * 4 + dsub) * SC:(par * 4 + dsub + 1) * SC],
                        pTb[tb][:]),
                        waits=(("PE4", last_tp),), inc="DV4")
                snap(f"scan_tp{tc}", "PE4")
                if DEBUG_MAMBA and tc % 2 == 0:
                    dma("gpsimd", dbg_bsnap[tc // 2], b_sb[:], name="SPK")
                lrs = f"LR{tc % 2}"
                rep = None
                for kk in range(8):
                    dma("gpsimd", brep[16 * kk:16 * (kk + 1), par * SC:(par + 1) * SC],
                        b_sb[:, tc * SC:(tc + 1) * SC],
                        waits=(at(f"hcl{tc - 2}") + ((("LP", prep_ld),) if tc == 0 else ()))
                        if kk == 0 else (), name=lrs)
                    rep = dma("gpsimd",
                              crep[16 * kk:16 * (kk + 1), par * SC:(par + 1) * SC],
                              c_sb[:, tc * SC:(tc + 1) * SC], name=lrs)
                dtu = None
                for dc in range(4):
                    bpar = dc % 2
                    mmdt = emit("tensor", lambda e, dc=dc, tc=tc, bpar=bpar: e.matmul(
                        pA[bpar][:], dtwT[:, dc * 128:(dc + 1) * 128],
                        dtin[:, tc * SC:(tc + 1) * SC], start=True, stop=True),
                        waits=(at(f"dA_last{tc - 1}") if dc < 2 else ()) +
                              ((("ACz", C["ACz"] - 1),) if dc >= 2 else ()) +
                              ((("DV4", C["DV4"]),) if dc == 0 else ()),
                        inc="PE4")
                    zz = emit("scalar", lambda e, dc=dc, par=par, bpar=bpar: e.activation(
                        zb_sb[:, par * SC:(par + 1) * SC], pA[bpar][:],
                        AF.Identity, bias=dtb_sb[:, dc:dc + 1]),
                        waits=(("PE4", mmdt),) +
                              (at(f"dtu{tc}:{dc - 1}") if dc > 0 else
                               at(f"dtu{tc - 1}:3")),
                        inc="ACz")
                    emit("scalar", lambda e, par=par: e.activation(
                        sp_r[:, par * SC:(par + 1) * SC],
                        zb_sb[:, par * SC:(par + 1) * SC], AF.Relu))
                    emit("scalar", lambda e, par=par: e.activation(
                        sp_l[:, par * SC:(par + 1) * SC],
                        zb_sb[:, par * SC:(par + 1) * SC], AF.Abs))
                    emit("scalar", lambda e, par=par: e.activation(
                        sp_l[:, par * SC:(par + 1) * SC],
                        sp_l[:, par * SC:(par + 1) * SC], AF.Exp, scale=-1.0))
                    spl = emit("scalar", lambda e, par=par: e.activation(
                        sp_l[:, par * SC:(par + 1) * SC],
                        sp_l[:, par * SC:(par + 1) * SC], AF.Ln, bias=1.0), inc="AC")
                    emit("vector", lambda e, dc=dc, par=par: e.tensor_add(
                        dt_sb[:, (par * 4 + dc) * SC:(par * 4 + dc + 1) * SC],
                        sp_r[:, par * SC:(par + 1) * SC],
                        sp_l[:, par * SC:(par + 1) * SC]),
                        waits=(("AC", spl),) +
                              ((("PE1", mark.get(f"mmlast{tc - 2}", ("PE1", 0))[1]),)
                               if dc == 0 else ()))
                    emit("vector", lambda e, dc=dc, par=par: e.tensor_copy(
                        dtbf[:, (par * 4 + dc) * SC:(par * 4 + dc + 1) * SC],
                        dt_sb[:, (par * 4 + dc) * SC:(par * 4 + dc + 1) * SC]))
                    dtu = emit("vector", lambda e, dc=dc, par=par: e.tensor_tensor(
                        u_sb[:, (par * 4 + dc) * SC:(par * 4 + dc + 1) * SC],
                        dt_sb[:, (par * 4 + dc) * SC:(par * 4 + dc + 1) * SC],
                        xlnT[:, (par * 4 + dc) * SC:(par * 4 + dc + 1) * SC],
                        OP.mult), inc="DV5")
                    snap(f"dtu{tc}:{dc}", "DV5")
                if DEBUG_MAMBA and tc == 12:
                    dma("gpsimd", dbg_dt, dt_sb[:, 0:4 * SC], waits=(("DV5", dtu),), name="SPK")
                    dma("gpsimd", dbg_u, u_sb[:, 0:4 * SC], waits=(("DV5", dtu),), name="SPK")

                if DEBUG_MAMBA and tc == 14:
                    dma("gpsimd", dbg_brep, brep[:, par * SC:(par + 1) * SC],
                        waits=((lrs, rep),), name="SPK")
                    dma("gpsimd", dbg_crep2, crep[:, par * SC:(par + 1) * SC],
                        waits=((lrs, rep),), name="SPK")

                if full:
                    emit_dq(2, pTb1)
                for dc in range(NDN):
                    i = tc * NDN + dc
                    ip = i % 2
                    v = dc % 16
                    dch = dc // 16
                    g_idx = tc * 4 + dch
                    yb = g_idx % 2
                    dsl = slice((par * 4 + dch) * SC, (par * 4 + dch + 1) * SC)
                    isl = slice(ip * SC, (ip + 1) * SC)

                    mma = emit("tensor", lambda e, v=v, dsl=dsl, ip=ip: e.matmul(
                        pA[ip][:], e_sb[:, v * 128:(v + 1) * 128], dtbf[:, dsl],
                        start=True, stop=True),
                        waits=at(f"dA{i - 2}") +
                              ((("ACz", C["ACz"]), ("DV5", dtu)) if dc == 0 else ()) +
                              ((("LP", prep_efa),) if i == 0 else ()),
                        inc="PE1")
                    mmb = emit("tensor", lambda e, v=v, dsl=dsl, ip=ip: e.matmul(
                        pB[ip][:], e_sb[:, v * 128:(v + 1) * 128], u_sb[:, dsl],
                        start=True, stop=True),
                        waits=at(f"ub{i - 2}"),
                        inc="PE1")
                    if dc == NDN - 1:
                        snap(f"mmlast{tc}", "PE1")
                    da_m = emit("scalar", lambda e, ip=ip, isl=isl, dc=dc: e.activation(
                        dA[:, isl], pA[ip][:], AF.Exp, scale=acol[:, dc:dc + 1]),
                        waits=(("PE1", mma),) + at(f"scan{i - 2}") +
                              ((("DV5", acneg),) if i == 0 else ()),
                        inc="AC")
                    snap(f"dA{i}", "AC")
                    if dc == NDN - 1:
                        snap(f"dA_last{tc}", "AC")
                    ubm = emit("scalar", lambda e, ip=ip, isl=isl: e.activation(
                        ub_sb[:, isl], pB[ip][:], AF.Copy),
                        waits=(("PE1", mmb),) + at(f"dbx{i - 2}"), inc="AC")
                    snap(f"ub{i}", "AC")
                    dbxm = emit("vector", lambda e, ip=ip, isl=isl, par=par:
                                e.tensor_tensor(
                        dbx[:, isl], ub_sb[:, isl], brep[:, par * SC:(par + 1) * SC],
                        OP.mult),
                        waits=(("AC", ubm),) +
                              (((lrs, rep),) if dc == 0 else ()),
                        inc="DV1")
                    snap(f"dbx{i}", "DV1")
                    if dc == NDN - 1:
                        snap(f"dbx_last{tc}", "DV1")
                    if tc == 0:
                        emit("vector", lambda e, isl=isl: e.tensor_tensor_scan(
                            h_sb[:, isl], dA[:, isl], dbx[:, isl], 0.0,
                            OP.mult, OP.add),
                            waits=(("AC", da_m),), inc="DV2")
                    else:
                        emit("vector", lambda e, isl=isl, dc=dc: e.tensor_tensor_scan(
                            h_sb[:, isl], dA[:, isl], dbx[:, isl],
                            carry[:, dc:dc + 1], OP.mult, OP.add),
                            waits=(("AC", da_m),), inc="DV2")
                    scm = mark[f"scan{i}"] if False else C["DV2"]
                    snap(f"scan{i}", "DV2")
                    if DEBUG_MAMBA and i == 13 * NDN:
                        pass
                    emit("vector", lambda e, isl=isl, dc=dc: e.tensor_copy(
                        carry[:, dc:dc + 1],
                        h_sb[:, isl.start + SC - 1:isl.start + SC]),
                        waits=(("DV2", scm),))
                    hcm = emit("vector", lambda e, isl=isl, par=par: e.tensor_tensor(
                        hc_sb[:, isl], h_sb[:, isl],
                        crep[:, par * SC:(par + 1) * SC], OP.mult),
                        waits=at(f"mmF{i - 2}"), inc="DV3")
                    if dc == NDN - 1:
                        snap(f"hcl{tc}", "DV3")
                    mmf = emit("tensor", lambda e, v=v, isl=isl, yb=yb: e.matmul(
                        pY[yb][:], f_sb[:, v * 128:(v + 1) * 128], hc_sb[:, isl],
                        start=(v == 0), stop=(v == 15)),
                        waits=(("DV3", hcm),) +
                              (at(f"yev{g_idx - 2}") if v == 0 else ()),
                        inc="PE3")
                    snap(f"mmF{i}", "PE3")
                    if v == 15:
                        yev = emit("vector", lambda e, yb=yb: e.tensor_copy(
                            ysb[:, yb * SC:(yb + 1) * SC], pY[yb][:]),
                            waits=(("PE3", mmf),) + at(f"yst{g_idx - 2}"),
                            inc="DV4")
                        snap(f"yev{g_idx}", "DV4")
                        dma("gpsimd",
                            cc2_in[dch * 128:(dch + 1) * 128, tc * SC:(tc + 1) * SC],
                            ysb[:, yb * SC:(yb + 1) * SC],
                            waits=(("DV4", yev),), name=f"SY{g_idx % 2}")
                        snap(f"yst{g_idx}", f"SY{g_idx % 2}")
            if DEBUG_MAMBA:
                dma("gpsimd", dbg_b[:, 0:T], b_sb[:],
                    waits=(("SY0", C["SY0"]), ("SY1", C["SY1"])), name="SPK")
                dma("gpsimd", dbg_c[:, 0:T], c_sb[:], name="SPK")
                dma("gpsimd", dbg_u13, u_sb[:, 0:4 * SC], name="SPK")
                dma("gpsimd", dbg_yloc, cc2_in,
                    waits=(("SY0", C["SY0"]), ("SY1", C["SY1"])), name="SPK")
                dma("gpsimd", dbg_carry, carry[:],
                    waits=(("DV2", mark["scan63"][1] if "scan63" in mark else 0),),
                    name="SPK")
            cc2_done = emit("gpsimd", lambda e: e.collective_compute(
                "AllGather", OP.bypass, replica_groups=[list(range(NCORES))],
                ins=[cc2_in.opt()], outs=[cc2_out.opt()]),
                waits=(("SY0", C["SY0"]), ("SY1", C["SY1"])), inc="CC")
            scan_dv4 = C["DV4"]

        with ExitStack() as _ctx:
            db_sb = _ctx.enter_context(nc.sbuf_tensor("db_sb", [128, D], F32))
            dstg = _ctx.enter_context(nc.sbuf_tensor("dstg", [1, D], F32))
            xt3 = _ctx.enter_context(nc.sbuf_tensor("xt3", [128, 2 * D], F32))
            yfm = _ctx.enter_context(nc.sbuf_tensor("yfm", [128, 2 * D], BF16))
            xmid = _ctx.enter_context(nc.sbuf_tensor("xmid", [128, 2 * D], F32))
            t3buf = _ctx.enter_context(nc.sbuf_tensor("t3buf", [128, D], F32))
            bnbuf3 = _ctx.enter_context(nc.sbuf_tensor("bnbuf3", [128, 48], F32))
            mv3 = _ctx.enter_context(nc.sbuf_tensor("mv3", [128, 16], F32))
            qm0 = _ctx.enter_context(nc.psum_tensor("qm0", [128, 512], BF16))
            qm1 = _ctx.enter_context(nc.psum_tensor("qm1", [128, 512], BF16))
            mdq = _ctx.enter_context(nc.psum_tensor("mdq", [128, 512], BF16))
            qm = (qm0, qm1)
            if full:
                emit_dq(48, mdq)
                cc_gu0 = emit("gpsimd", lambda e: e.collective_compute(
                    "AllGather", OP.bypass, replica_groups=[list(range(NCORES))],
                    ins=[wsh_gu0.opt()], outs=[wfull_gu0.opt()]),
                    waits=at("dq_b0_0") + at("dq_b0_1"), inc="CC")
                assert "dq_b0_0" in mark and "dq_b0_1" in mark
            wdv = dma("sync", db_sb[:], dvec.broadcast_to([128, D]), name="LDC",
                      waits=(("SY0", C["SY0"]), ("SY1", C["SY1"])))
            snap("db", "LDC")

            for c in range(8):
                par = c % 2
                lts = f"LT{c % 2}"
                w_x = dma("sync", xt3[:, par * D:(par + 1) * D],
                          x_tok[c * 128:(c + 1) * 128, :], waits=at(f"t3{c - 2}"),
                          name=lts)

                def load_yfm(e, c=c, par=par):
                    pid = e.partition_id()
                    srcv = cc2_out.rearrange("(a p) t -> p a t", p=128)
                    sl = srcv[0:128, 0:32, 0:128]
                    ap2 = bass.AP(sl.tensor, sl.offset + pid * TC + c * 128, sl.ap)
                    return e.dma_start(
                        out=yfm[:, par * D:(par + 1) * D]
                        .rearrange("p (a f) -> p a f", f=128), in_=ap2)
                w_y = emit("sync", load_yfm,
                           waits=((("CC", cc2_done),) if c == 0 else ()) +
                                 at(f"xm{c - 2}"),
                           inc=f"LY{c % 2}", inc_n=16)
                emit("vector", lambda e, c=c, par=par: e.tensor_scalar(
                    t3buf[:], xt3[:, par * D:(par + 1) * D],
                    stats[:, c:c + 1], stats[:, 8 + c:9 + c], OP.subtract, OP.mult),
                    waits=((lts, w_x),) + at("db"), inc="DV5")
                emit("vector", lambda e: e.tensor_tensor(
                    t3buf[:], t3buf[:], db_sb[:], OP.mult))
                emit("vector", lambda e, c=c, par=par: e.tensor_tensor(
                    t3buf[:], t3buf[:], xt3[:, par * D:(par + 1) * D], OP.add),
                    inc="DV5")
                snap(f"t3{c}", "DV5")
                for b in range(8):
                    mb = b % 2
                    tpm = None
                    for s in range(4):
                        tpm = emit("tensor", lambda e, b=b, s=s, par=par, mb=mb:
                                   e.transpose(
                            qm[mb][:, s * 128:(s + 1) * 128],
                            yfm[:, par * D + (4 * b + s) * 128:
                                par * D + (4 * b + s + 1) * 128], eyeb[:]),
                            waits=(((f"LY{c % 2}", w_y),) if (b == 0 and s == 0) else ()) +
                                  ((("DV4", C["DV4"]),) if s == 0 else ()),
                            inc="PE4" if s == 3 else None)
                    emit("vector", lambda e, b=b, mb=mb, par=par: e.tensor_tensor(
                        xmid[:, par * D + b * 512:par * D + (b + 1) * 512],
                        t3buf[:, b * 512:(b + 1) * 512], qm[mb][:], OP.add),
                        waits=(("PE4", tpm),) +
                              (at(f"xms{c - 2}") if b == 0 else ()), inc="DV4")
                snap(f"xm{c}", "DV4")
                dma("gpsimd", xmid_hbm[c * 128:(c + 1) * 128, :],
                    xmid[:, par * D:(par + 1) * D], waits=(("DV4", C["DV4"]),),
                    name=f"SXM{c % 2}")
                snap(f"xms{c}", f"SXM{c % 2}")
                if stop_after == "mamba":
                    dma("gpsimd", x_out[c * 128:(c + 1) * 128, :],
                        xmid[:, par * D:(par + 1) * D], name=f"SXM{c % 2}")
                bn3 = None
                for b in range(8):
                    bn3 = emit("vector", lambda e, b=b, par=par: e.bn_stats(
                        bnbuf3[:, 6 * b:6 * b + 6],
                        xmid[:, par * D + 512 * b:par * D + 512 * (b + 1)]),
                        inc="DV2" if b == 7 else None)
                ag3 = emit("vector", lambda e, c=c: e.bn_aggr(
                    mv3[:, 2 * c:2 * c + 2], bnbuf3[:]),
                    waits=(("DV2", bn3),), inc="DV2")
                m3 = emit("vector", lambda e, c=c: e.tensor_copy(
                    st3[:, c:c + 1], mv3[:, 2 * c:2 * c + 1]),
                    waits=(("DV2", ag3),), inc="DV5")
                emit("scalar", lambda e, c=c: e.activation(
                    mv3[:, 2 * c + 1:2 * c + 2], mv3[:, 2 * c + 1:2 * c + 2],
                    AF.Ln, bias=cst[:, 0:1]),
                    waits=(("DV5", m3),))
                emit("scalar", lambda e, c=c: e.activation(
                    st3[:, 8 + c:9 + c], mv3[:, 2 * c + 1:2 * c + 2],
                    AF.Exp, scale=cst[:, 1:2]),
                    inc="AC")
                if full:
                    emit_dq(5, mdq)
            snap("merge_done0", "SXM0")
            snap("merge_done1", "SXM1")
            snap("merge_ac", "AC")
            snap("merge_dv4", "DV4")

        if full:
            with ExitStack() as _ctx:
                xln2T = _ctx.enter_context(nc.sbuf_tensor("xln2T", [128, ND * SC], BF16))
                x3 = _ctx.enter_context(nc.sbuf_tensor("x3", [128, NJ * SC], BF16))
                wstr = _ctx.enter_context(nc.sbuf_tensor("wstr", [128, 2 * NJ * 128], BF16))
                lora_a = _ctx.enter_context(nc.sbuf_tensor("lora_a", [128, (2 * ND + NJ) * 16], BF16))
                lora_b = _ctx.enter_context(nc.sbuf_tensor("lora_b", [96, DFF], BF16))
                uvec = _ctx.enter_context(nc.sbuf_tensor("uvec", [96, 512], BF16))
                xshare = _ctx.enter_context(nc.sbuf_tensor("xshare", [128, 3072], F32))
                xshb = xshare[:].bitcast(BF16)
                xmr = xshare[:, 2048:3072]
                xln2b = lambda: xshb[:, 0:4096]
                r1_sb = xshb[:, 0:512]
                xo_sb = xshare[:, 0:1024]
                xma = xshare[:, 1024:2048]
                xob = xshare[:, 2048:3072]
                q1a = _ctx.enter_context(nc.psum_tensor("q1a", [128, 512], F32))
                q1b = _ctx.enter_context(nc.psum_tensor("q1b", [128, 512], F32))
                q2a = _ctx.enter_context(nc.psum_tensor("q2a", [128, 512], F32))
                q2b = _ctx.enter_context(nc.psum_tensor("q2b", [128, 512], F32))
                qu = _ctx.enter_context(nc.psum_tensor("qu", [96, 512], F32))
                qt0 = _ctx.enter_context(nc.psum_tensor("qt0", [128, 512], F32))
                qt1 = _ctx.enter_context(nc.psum_tensor("qt1", [128, 512], F32))
                qtb = _ctx.enter_context(nc.psum_tensor("qtb", [128, 512], BF16))
                q1 = (q1a, q1b)
                q2 = (q2a, q2b)
                QT = (qt0, qt1, q2a, q2b)

                wb = wstr[:].bitcast(F32)
                wl = dma("sync", wb[:, 0:ND * 16].rearrange("p (c f) -> p c f", f=16),
                         gla.rearrange("(c p) f -> p c f", p=128),
                         waits=at("merge_done0") + at("merge_done1"), name="LLA")
                la1 = emit("vector", lambda e, wb=wb: e.tensor_copy(
                    lora_a[:, 0:ND * 16], wb[:, 0:ND * 16]),
                    waits=(("LLA", wl),), inc="DV5")
                wl = dma("sync", wb[:, 0:ND * 16].rearrange("p (c f) -> p c f", f=16),
                         ula.rearrange("(c p) f -> p c f", p=128),
                         waits=(("DV5", la1),), name="LLA")
                la2 = emit("vector", lambda e, wb=wb: e.tensor_copy(
                    lora_a[:, ND * 16:2 * ND * 16], wb[:, 0:ND * 16]),
                    waits=(("LLA", wl),), inc="DV5")
                wl = dma("sync", wb[:, 0:NJ * 16].rearrange("p (c f) -> p c f", f=16),
                         dla.rearrange("(c p) f -> p c f", p=128),
                         waits=(("DV5", la2),), name="LLA")
                la3 = emit("vector", lambda e, wb=wb: e.tensor_copy(
                    lora_a[:, 2 * ND * 16:(2 * ND + NJ) * 16], wb[:, 0:NJ * 16]),
                    waits=(("LLA", wl),), inc="DV5")
                wl = dma("sync", wb[0:16, 0:DFF], glb, waits=(("DV5", la3),), name="LLA")
                lb1 = emit("vector", lambda e, wb=wb: e.tensor_copy(
                    lora_b[0:16, :], wb[0:16, 0:DFF]), waits=(("LLA", wl),), inc="DV5")
                wl = dma("sync", wb[0:16, 0:DFF], ulb, waits=(("DV5", lb1),), name="LLA")
                lb2 = emit("vector", lambda e, wb=wb: e.tensor_copy(
                    lora_b[32:48, :], wb[0:16, 0:DFF]), waits=(("LLA", wl),), inc="DV5")
                wl = dma("sync", wb[0:16, 0:D], dlb, waits=(("DV5", lb2),), name="LLA")
                emit("vector", lambda e, wb=wb: e.tensor_copy(
                    lora_b[64:80, 0:D], wb[0:16, 0:D]),
                    waits=(("LLA", wl),), inc="DV5")
                snap("lora", "DV5")

                def GLA(dc):
                    return lora_a[:, dc * 16:(dc + 1) * 16]

                def ULA(dc):
                    return lora_a[:, (ND + dc) * 16:(ND + dc + 1) * 16]

                def DLA(ic):
                    return lora_a[:, (2 * ND + ic) * 16:(2 * ND + ic + 1) * 16]

                for hf in range(2):
                    lx = None
                    for cq in range(4):
                        c = hf * 4 + cq
                        for hh in range(4):
                            wx = dma("sync", xmr,
                                     xmid_hbm[c * 128:(c + 1) * 128,
                                              hh * 1024:(hh + 1) * 1024],
                                     waits=(at("merge_done0") + at("merge_done1") +
                                            at("xobst0") + at("xobst1") +
                                            at(f"ffn_lx{hf}:{cq - 1}"))
                                     if hh == 0 else
                                     ((("DV5", lx),) if lx else ()),
                                     name="LXM")
                            lx = emit("vector", lambda e, c=c, hh=hh: e.tensor_scalar(
                                xln2b()[:, hh * 1024:(hh + 1) * 1024], xmr,
                                st3[:, c:c + 1], st3[:, 8 + c:9 + c],
                                OP.subtract, OP.mult),
                                waits=(("LXM", wx),) + at("merge_ac") +
                                      (at(f"ffn_tp{hf}:{cq - 1}") if hh == 0 else ()) +
                                      (at(f"relu{NJ - 1}") if (hh == 0 and cq == 0)
                                       else ()),
                                inc="DV5")
                        snap(f"ffn_lx{hf}:{cq}", "DV5")
                        for dc in range(ND):
                            tpm = emit("tensor", lambda e, dc=dc: e.transpose(
                                qtb[:, 0:128], xln2b()[:, dc * 128:(dc + 1) * 128],
                                eyeb[:]),
                                waits=(("DV5", lx), ("DV4", C["DV4"])) +
                                      (at("merge_dv4") if (hf == 0 and cq == 0 and dc == 0)
                                       else ()),
                                inc="PE4")
                            emit("vector", lambda e, dc=dc, cq=cq: e.tensor_copy(
                                xln2T[:, dc * SC + cq * 128:dc * SC + cq * 128 + 128],
                                qtb[:, 0:128]),
                                waits=(("PE4", tpm),), inc="DV4")
                        snap(f"ffn_tp{hf}:{cq}", "DV4")
                    snap(f"xln2T{hf}", "DV4")
                    mmu = None
                    for dc in range(ND):
                        emit("tensor", lambda e, dc=dc: e.matmul(
                            qu[0:16, :], GLA(dc), xln2T[:, dc * SC:(dc + 1) * SC],
                            start=(dc == 0), stop=(dc == ND - 1)),
                            waits=(at(f"xln2T{hf}") + at("lora") + at(f"uev{hf - 1}"))
                            if dc == 0 else ())
                        mmu = emit("tensor", lambda e, dc=dc: e.matmul(
                            qu[32:48, :], ULA(dc), xln2T[:, dc * SC:(dc + 1) * SC],
                            start=(dc == 0), stop=(dc == ND - 1)),
                            inc="PE4" if dc == ND - 1 else None)
                    ue1 = emit("vector", lambda e: e.tensor_copy(
                        uvec[0:48, :], qu[0:48, :]),
                        waits=(("PE4", mmu),), inc="DV4")
                    snap(f"uev{hf}", "DV4")
                    vfull0 = wfull_gu0.rearrange("(a p) f -> a p f", p=128)
                    vfull1 = wfull_gu1.rearrange("(a p) f -> a p f", p=128)
                    j_order = ([(r, jc) for jc in range(4) for r in range(NCORES)
                                if r * NGU_PAD + jc < NJ] +
                               [(r, jc) for jc in range(4, NGU_PAD) for r in range(NCORES)
                                if r * NGU_PAD + jc < NJ])
                    for jj, (r_, jc_) in enumerate(j_order):
                        j = r_ * NGU_PAD + jc_
                        jp = jj % 2
                        wof = jp * NJ * 128
                        lws = f"LW{jj % 2}"
                        if jc_ < 4:
                            gt = vfull0[(r_ * 4 + jc_) * 64:(r_ * 4 + jc_) * 64 + 32]
                            ut = vfull0[(r_ * 4 + jc_) * 64 + 32:(r_ * 4 + jc_) * 64 + 64]
                        else:
                            gt = vfull1[(r_ * 7 + jc_ - 4) * 64:
                                        (r_ * 7 + jc_ - 4) * 64 + 32]
                            ut = vfull1[(r_ * 7 + jc_ - 4) * 64 + 32:
                                        (r_ * 7 + jc_ - 4) * 64 + 64]
                        dma("sync",
                            wstr[:, wof:wof + ND * 128]
                            .rearrange("p (c f) -> p c f", f=128),
                            gt.transpose([1, 0, 2]),
                            waits=at(f"relu{jj - 2}") +
                                  (at("dmmlast") if jj < 2 else ()) +
                                  (at("lora") if jj == 0 else ()) +
                                  ((("CC", cc_gu0),) if (hf == 0 and jj == 0) else ()) +
                                  ((("CC", mark["ccgu1"][1]),)
                                   if (hf == 0 and jj == 32) else ()),
                            name=lws)
                        wu_l = dma("sync",
                                   wstr[:, wof + ND * 128:wof + 2 * ND * 128]
                                   .rearrange("p (c f) -> p c f", f=128),
                                   ut.transpose([1, 0, 2]), name=lws)
                        emit("tensor", lambda e, j=j, jp=jp: e.matmul(
                            q1[jp][:], lora_b[0:16, j * 128:(j + 1) * 128],
                            uvec[0:16, :], start=True, stop=False),
                            waits=(((lws, wu_l), ("DV4", ue1)) +
                                   at(f"relu{jj - 2}") + at(f"x3m{jj - 2}") +
                                   (at("y3evlast") if jj == 0 else ())))
                        for ic in range(ND):
                            emit("tensor", lambda e, ic=ic, wof=wof, jp=jp: e.matmul(
                                q1[jp][:],
                                wstr[:, wof + ic * 128:wof + (ic + 1) * 128],
                                xln2T[:, ic * SC:(ic + 1) * SC],
                                start=False, stop=(ic == ND - 1)))
                        emit("tensor", lambda e, j=j, jp=jp: e.matmul(
                            q2[jp][:], lora_b[32:48, j * 128:(j + 1) * 128],
                            uvec[32:48, :], start=True, stop=False),
                            waits=(at("xoaddlast") if jj == 0 else ()))
                        gmm = None
                        for ic in range(ND):
                            gmm = emit("tensor", lambda e, ic=ic, wof=wof, jp=jp: e.matmul(
                                q2[jp][:],
                                wstr[:, wof + (ND + ic) * 128:wof + (ND + ic + 1) * 128],
                                xln2T[:, ic * SC:(ic + 1) * SC],
                                start=False, stop=(ic == ND - 1)),
                                inc="PE1" if ic == ND - 1 else None)
                        if jj == NJ - 1:
                            snap("gmmlast", "PE1")
                        rel = emit("scalar", lambda e, jp=jp: e.activation(
                            r1_sb, q1[jp][:], AF.Relu),
                            waits=(("PE1", gmm),) + at(f"x3m{jj - 1}") +
                                  (at(f"ffn_lx{hf}:3") if jj == 0 else ()), inc="AC")
                        snap(f"relu{jj}", "AC")
                        emit("vector", lambda e, j=j, jp=jp: e.tensor_tensor(
                            x3[:, j * SC:(j + 1) * SC], r1_sb, q2[jp][:], OP.mult),
                            waits=(("AC", rel),), inc="DV4")
                        snap(f"x3m{jj}", "DV4")
                        if hf == 0:
                            emit_dq(3, qtb)
                            if jj == 18:
                                snap("ccgu1", "CC")
                                emit("gpsimd", lambda e: e.collective_compute(
                                    "AllGather", OP.bypass,
                                    replica_groups=[list(range(NCORES))],
                                    ins=[wsh_gu1.opt()], outs=[wfull_gu1.opt()]),
                                    waits=at("dq_b1_0") + at("dq_b1_1"), inc="CC")
                                assert "dq_b1_0" in mark and "dq_b1_1" in mark
                                snap("ccgu1", "CC")
                            if jj == 48:
                                snap("ccgd", "CC")
                                emit("gpsimd", lambda e: e.collective_compute(
                                    "AllGather", OP.bypass,
                                    replica_groups=[list(range(NCORES))],
                                    ins=[wsh_d.opt()], outs=[wfull_d.opt()]),
                                    waits=at("dq_b2_0") + at("dq_b2_1"), inc="CC")
                                assert "dq_b2_0" in mark and "dq_b2_1" in mark
                                snap("ccgd", "CC")
                    snap(f"x3done{hf}", "DV4")
                    mmd = None
                    for ic in range(NJ):
                        mmd = emit("tensor", lambda e, ic=ic: e.matmul(
                            qu[64:80, :], DLA(ic), x3[:, ic * SC:(ic + 1) * SC],
                            start=(ic == 0), stop=(ic == NJ - 1)),
                            waits=(at(f"x3done{hf}") + at(f"udev{hf - 1}"))
                            if ic == 0 else (),
                            inc="PE4" if ic == NJ - 1 else None)
                    ud1 = emit("vector", lambda e: e.tensor_copy(
                        uvec[64:80, :], qu[64:80, :]),
                        waits=(("PE4", mmd),), inc="DV4")
                    snap(f"udev{hf}", "DV4")
                    vfd = wfull_d.rearrange("(a p) f -> a p f", p=128)
                    for dm in range(ND):
                        dp = dm % 2
                        dg = dm // 4
                        wof = dp * NJ * 128
                        lws = f"LW{dm % 2}"
                        dt_base = (dm // 4) * 4 * 86 + (dm % 4) * 86
                        dma("sync",
                            wstr[:, wof:wof + 43 * 128]
                            .rearrange("p (c f) -> p c f", f=128),
                            vfd[dt_base:dt_base + 43].transpose([1, 0, 2]),
                            waits=at(f"y3ev{dm - 2}") +
                                  (at("gmmlast") if dm < 2 else ()) +
                                  ((("CC", mark["ccgd"][1]),)
                                   if (hf == 0 and dm == 0) else ()), name=lws)
                        wd2_l = dma("sync",
                                    wstr[:, wof + 43 * 128:wof + 86 * 128]
                                    .rearrange("p (c f) -> p c f", f=128),
                                    vfd[dt_base + 43:dt_base + 86].transpose([1, 0, 2]),
                                    name=lws)
                        emit("tensor", lambda e, dm=dm, dp=dp: e.matmul(
                            q1[dp][:], lora_b[64:80, dm * 128:(dm + 1) * 128],
                            uvec[64:80, :], start=True, stop=False),
                            waits=((lws, wd2_l), ("DV4", ud1)) + at(f"y3ev{dm - 2}"))
                        dmm = None
                        for ic in range(NJ):
                            dmm = emit("tensor", lambda e, ic=ic, wof=wof, dp=dp: e.matmul(
                                q1[dp][:],
                                wstr[:, wof + ic * 128:wof + (ic + 1) * 128],
                                x3[:, ic * SC:(ic + 1) * SC],
                                start=False, stop=(ic == NJ - 1)),
                                inc="PE1" if ic == NJ - 1 else None)
                        if dm == ND - 1:
                            snap("dmmlast", "PE1")
                        y3e = emit("vector", lambda e, dp=dp: e.tensor_copy(
                            xo_sb[:, dp * 512:(dp + 1) * 512], q1[dp][:]),
                            waits=(("PE1", dmm),) + at(f"y3tp{dm - 2}") +
                                  ((("AC", C["AC"]),) if dm == 0 else ()), inc="DV4")
                        snap(f"y3ev{dm}", "DV4")
                        if dm == ND - 1:
                            snap("y3evlast", "DV4")
                        for s in range(4):
                            emit("tensor", lambda e, s=s, dp=dp, dm=dm: e.transpose(
                                QT[s][:, (dm % 4) * 128:(dm % 4) * 128 + 128],
                                xo_sb[:, dp * 512 + s * 128:dp * 512 + (s + 1) * 128],
                                eyef[:]),
                                waits=(("DV4", y3e),) +
                                      (at("xoaddlast") if (dm % 4 == 0 and s == 0)
                                       else ()) +
                                      (at(f"x3m{NJ - 1}") if (dm == 0 and s >= 2)
                                       else ()),
                                inc="PE4" if s == 3 else None)
                        snap(f"y3tp{dm}", "PE4")
                        if dm % 4 == 3:
                            tpg = mark[f"y3tp{dm}"][1]
                            for s in range(4):
                                c = hf * 4 + s
                                wxa = dma("sync", xma[:, (s % 2) * 512:(s % 2 + 1) * 512],
                                          xmid_hbm.rearrange("(a p) f -> a p f", p=128)
                                          [c, :, dg * 512:(dg + 1) * 512],
                                          waits=at(f"xmafree{s % 2}"),
                                          name=f"LMA{s % 2}")
                                xoa = emit("vector",
                                           lambda e, s=s, dg=dg: e.tensor_tensor(
                                    xob[:, (s % 2) * 512:(s % 2 + 1) * 512],
                                    xma[:, (s % 2) * 512:(s % 2 + 1) * 512],
                                    QT[s][:], OP.add),
                                    waits=(("PE4", tpg), (f"LMA{s % 2}", wxa)) +
                                          at(f"xobst{s % 2}"),
                                    inc="DV4")
                                snap(f"xmafree{s % 2}", "DV4")
                                xst = dma("gpsimd",
                                          x_out[c * 128:(c + 1) * 128,
                                                dg * 512:(dg + 1) * 512],
                                          xob[:, (s % 2) * 512:(s % 2 + 1) * 512],
                                          waits=(("DV4", xoa),), name=f"SXO{s % 2}")
                                snap(f"xobst{s % 2}", f"SXO{s % 2}")
                            snap("xoaddlast", "DV4")

        store_names = ["SPK", "SY0", "SY1", "SXM0", "SXM1", "SXO0", "SXO1"]
        fin = [(dma_sems[n], C[n]) for n in store_names if C[n] > 0]

        def final_wait(e, fin=fin):
            for s, v in fin:
                e.wait_ge(s, v)
        emit("gpsimd", final_wait)

        @blk.sync
        def _(e):
            for fn in P.queues["sync"]:
                fn(e)

        @blk.gpsimd
        def _(e):
            for fn in P.queues["gpsimd"]:
                fn(e)

        @blk.vector
        def _(e):
            for fn in P.queues["vector"]:
                fn(e)

        @blk.scalar
        def _(e):
            for fn in P.queues["scalar"]:
                fn(e)

        @blk.tensor
        def _(e):
            for fn in P.queues["tensor"]:
                fn(e)

    return nc


def _fwd_inputs(k, x2, A_log, Dv, x_proj_w, dt_proj_w, dt_proj_b, consts,
                wtiles=None):
    a_sl = np.ascontiguousarray(A_log[k * DSL:(k + 1) * DSL]).reshape(-1)
    a_pre = np.ascontiguousarray(a_sl.reshape(NDN, 128).T)
    d = dict(
        x_tok=np.ascontiguousarray(x2[k * TC:(k + 1) * TC]),
        x_ds=np.ascontiguousarray(x2[:, k * DSL:(k + 1) * DSL]),
        xpw=x_proj_w,
        dtw=np.ascontiguousarray(dt_proj_w[k * DSL:(k + 1) * DSL]),
        dtb=np.ascontiguousarray(dt_proj_b[k * DSL:(k + 1) * DSL]).reshape(DSL, 1),
        a_pre=a_pre,
        dvec=Dv.reshape(1, D),
    )
    d.update(consts)
    if wtiles is not None:
        d.update(wtiles)
    return d


def _consts():
    ev = np.zeros((16, 128, 128), dtype=BF16NP)
    fv = np.zeros((16, 128, 128), dtype=BF16NP)
    for v in range(16):
        for p in range(128):
            r = 8 * v + p // 16
            ev[v, r, p] = 1
            fv[v, p, r] = 1
    g = np.zeros((16, 128), dtype=BF16NP)
    for p in range(128):
        g[p % 16, p] = 1
    return dict(
        e_cat=ev, f_cat=fv, g16=g,
        eye_bf=np.eye(128, dtype=BF16NP),
        eye_f32=np.eye(128, dtype=np.float32),
    )


def kernel(**inp):
    x2 = np.ascontiguousarray(np.asarray(inp["L0_attn_x"])[0])
    consts = _consts()
    R = NGU_PAD * 128
    gcod = _pad_rows(np.asarray(inp["L0_ffn_g_codes"]), R * NCORES)
    gam = _pad_rows(np.asarray(inp["L0_ffn_g_absmax"]), R * NCORES)
    ucod = _pad_rows(np.asarray(inp["L0_ffn_u_codes"]), R * NCORES)
    uam = _pad_rows(np.asarray(inp["L0_ffn_u_absmax"]), R * NCORES)
    dcodf = np.asarray(inp["L0_ffn_d_codes"])
    damf = np.asarray(inp["L0_ffn_d_absmax"])

    nc = build_forward("full")
    in_maps = []
    for k in range(NCORES):
        m = _fwd_inputs(k, x2,
                        np.asarray(inp["L0_attn_A_log"]),
                        np.asarray(inp["L0_attn_D"]),
                        np.asarray(inp["L0_attn_x_proj_w"]),
                        np.asarray(inp["L0_attn_dt_proj_w"]),
                        np.asarray(inp["L0_attn_dt_proj_b"]),
                        consts)
        m.update(dict(
            gc=np.ascontiguousarray(gcod[k * R:(k + 1) * R]).astype(np.int8),
            ga=np.ascontiguousarray(gam[k * R:(k + 1) * R]),
            uc=np.ascontiguousarray(ucod[k * R:(k + 1) * R]).astype(np.int8),
            ua=np.ascontiguousarray(uam[k * R:(k + 1) * R]),
            dcod=np.ascontiguousarray(dcodf[k * 512:(k + 1) * 512]).astype(np.int8),
            dam=np.ascontiguousarray(damf[k * 512:(k + 1) * 512]),
            gla=np.asarray(inp["L0_ffn_g_lora_a"]),
            glb=np.asarray(inp["L0_ffn_g_lora_b"]),
            ula=np.asarray(inp["L0_ffn_u_lora_a"]),
            ulb=np.asarray(inp["L0_ffn_u_lora_b"]),
            dla=np.asarray(inp["L0_ffn_d_lora_a"]),
            dlb=np.asarray(inp["L0_ffn_d_lora_b"]),
        ))
        in_maps.append(m)
    res = run_bass_kernel_spmd(nc, in_maps, core_ids=list(range(NCORES)))
    out = np.concatenate([res.results[k]["x_out"] for k in range(NCORES)], axis=0)
    return out[None]


NGU_PAD = 11


def build_dequant():
    nc = bass.Bass("TRN2", target_bir_lowering=False, debug=False)
    gc = nc.dram_tensor("gc", [NGU_PAD * 128, D], I32, kind="ExternalInput").ap()
    ga = nc.dram_tensor("ga", [NGU_PAD * 128, D // 64], F32, kind="ExternalInput").ap()
    uc = nc.dram_tensor("uc", [NGU_PAD * 128, D], I32, kind="ExternalInput").ap()
    ua = nc.dram_tensor("ua", [NGU_PAD * 128, D // 64], F32, kind="ExternalInput").ap()
    dcod = nc.dram_tensor("dcod", [4 * 128, DFF], I32, kind="ExternalInput").ap()
    dam = nc.dram_tensor("dam", [4 * 128, DFF // 64], F32, kind="ExternalInput").ap()
    eye_bf = nc.dram_tensor("eye_bf", [128, 128], BF16, kind="ExternalInput").ap()

    og = nc.dram_tensor("og", [NGU_PAD, ND, 128, 128], BF16, kind="ExternalOutput").ap()
    ou = nc.dram_tensor("ou", [NGU_PAD, ND, 128, 128], BF16, kind="ExternalOutput").ap()
    od = nc.dram_tensor("od", [4, NJ, 128, 128], BF16, kind="ExternalOutput").ap()

    with ExitStack() as _ctx:
        eyeb = _ctx.enter_context(nc.sbuf_tensor("eyeb", [128, 128], BF16))
        cod = _ctx.enter_context(nc.sbuf_tensor("cod", [128, 2 * 512], I32))
        amx = _ctx.enter_context(nc.sbuf_tensor("amx", [128, 2 * 8], F32))
        accs = _ctx.enter_context(nc.sbuf_tensor("accs", [128, 512], BF16))
        tmp = _ctx.enter_context(nc.sbuf_tensor("tmp", [128, 512], BF16))
        wq = _ctx.enter_context(nc.sbuf_tensor("wq", [128, 2 * 512], BF16))
        obuf = _ctx.enter_context(nc.sbuf_tensor("obuf", [128, 2 * 512], BF16))
        ptr0 = _ctx.enter_context(nc.psum_tensor("ptr0", [128, 512], BF16))
        ptr1 = _ctx.enter_context(nc.psum_tensor("ptr1", [128, 512], BF16))
        sLD = _ctx.enter_context(nc.semaphore("sLD"))
        sLQ1 = _ctx.enter_context(nc.semaphore("sLQ1"))
        sST = _ctx.enter_context(nc.semaphore("sST"))
        sSQ1 = _ctx.enter_context(nc.semaphore("sSQ1"))
        sDV = _ctx.enter_context(nc.semaphore("sDV"))
        sPE = _ctx.enter_context(nc.semaphore("sPE"))
        blk = _ctx.enter_context(nc.Block())
        P = Prog(dict(LQ0=sLD, LQ1=sLQ1, SQ0=sST, SQ1=sSQ1, DV=sDV, PE=sPE))
        emit, dma, snap, at, C = P.emit, P.dma, P.snap, P.at, P.C
        ptr = (ptr0, ptr1)
        w_eye = dma("sync", eyeb[:], eye_bf, name="LQ0")
        git = [0]

        def one_matrix(codes, am, out_t, nrow, icols):
            nib = (icols + 511) // 512
            for jc in range(nrow):
                for ib in range(nib):
                    it = git[0]
                    git[0] += 1
                    par = it % 2
                    iw = min(512, icols - ib * 512)
                    nt = iw // 128
                    nb = iw // 64
                    lqs = f"LQ{it % 2}"
                    dma("sync", cod[:, par * 512:par * 512 + iw],
                        codes[jc * 128:(jc + 1) * 128, ib * 512:ib * 512 + iw],
                        waits=at(f"dqlut{it - 2}"), name=lqs)
                    w_a = dma("sync", amx[:, par * 8:par * 8 + nb],
                              am[jc * 128:(jc + 1) * 128, ib * 8:ib * 8 + nb],
                              name=lqs)
                    csl = cod[:, par * 512:par * 512 + iw]
                    emit("vector", lambda e, csl=csl, iw=iw: e.tensor_scalar(
                        accs[:, 0:iw], csl, 0.0, float(NF4[0]),
                        OP.is_equal, OP.mult),
                        waits=((lqs, w_a),) + at(f"dqw{it - 2}"))
                    for k in range(1, 16):
                        emit("vector", lambda e, csl=csl, iw=iw, k=k: e.tensor_scalar(
                            tmp[:, 0:iw], csl, float(k), float(NF4[k]),
                            OP.is_equal, OP.mult))
                        emit("vector", lambda e, iw=iw: e.tensor_add(
                            accs[:, 0:iw], accs[:, 0:iw], tmp[:, 0:iw]))
                    lut = emit("vector", lambda e, par=par, iw=iw, nb=nb: e.tensor_tensor(
                        wq[:, par * 512:par * 512 + iw]
                        .rearrange("p (b z) -> p b z", z=64),
                        accs[:, 0:iw].rearrange("p (b z) -> p b z", z=64),
                        amx[:, par * 8:par * 8 + nb].unsqueeze(2)
                        .broadcast_to([128, nb, 64]),
                        OP.mult), inc="DV")
                    snap(f"dqlut{it}", "DV")
                    tpm = None
                    for s in range(nt):
                        tpm = emit("tensor", lambda e, s=s, par=par, tb=par: e.transpose(
                            ptr[tb][:, s * 128:(s + 1) * 128],
                            wq[:, par * 512 + s * 128:par * 512 + (s + 1) * 128],
                            eyeb[:]),
                            waits=((("DV", lut),) + at(f"dqev{it - 2}") +
                                   ((("LQ0", w_eye),) if it == 0 else ()))
                            if s == 0 else (),
                            inc="PE" if s == nt - 1 else None)
                    ev = emit("vector", lambda e, par=par, iw=iw, tb=par: e.tensor_copy(
                        obuf[:, par * 512:par * 512 + iw], ptr[tb][:, 0:iw]),
                        waits=(("PE", tpm),) + at(f"dqst{it - 2}"), inc="DV")
                    snap(f"dqev{it}", "DV")
                    snap(f"dqw{it}", "DV")
                    dma("gpsimd", out_t[jc, ib * 4:ib * 4 + nt].transpose([1, 0, 2]),
                        obuf[:, par * 512:par * 512 + iw]
                        .rearrange("p (c f) -> p c f", f=128),
                        waits=(("DV", ev),), name=f"SQ{it % 2}")
                    snap(f"dqst{it}", f"SQ{it % 2}")

        one_matrix(gc, ga, og, NGU_PAD, D)
        one_matrix(uc, ua, ou, NGU_PAD, D)
        one_matrix(dcod, dam, od, 4, DFF)
        fin = [(sST, C["SQ0"]), (sSQ1, C["SQ1"])]

        def final_wait(e, fin=fin):
            for s, v in fin:
                e.wait_ge(s, v)
        emit("gpsimd", final_wait)

        @blk.sync
        def _(e):
            for fn in P.queues["sync"]:
                fn(e)

        @blk.gpsimd
        def _(e):
            for fn in P.queues["gpsimd"]:
                fn(e)

        @blk.vector
        def _(e):
            for fn in P.queues["vector"]:
                fn(e)

        @blk.scalar
        def _(e):
            for fn in P.queues["scalar"]:
                fn(e)

        @blk.tensor
        def _(e):
            for fn in P.queues["tensor"]:
                fn(e)
    return nc


def _pad_rows(a, rows):
    if a.shape[0] == rows:
        return a
    pad = np.zeros((rows - a.shape[0],) + a.shape[1:], dtype=a.dtype)
    return np.concatenate([a, pad], axis=0)


def _run_dequant(inp):
    gcod = _pad_rows(np.asarray(inp["L0_ffn_g_codes"]), NGU_PAD * 128 * NCORES)
    gam = _pad_rows(np.asarray(inp["L0_ffn_g_absmax"]), NGU_PAD * 128 * NCORES)
    ucod = _pad_rows(np.asarray(inp["L0_ffn_u_codes"]), NGU_PAD * 128 * NCORES)
    uam = _pad_rows(np.asarray(inp["L0_ffn_u_absmax"]), NGU_PAD * 128 * NCORES)
    dcod = np.asarray(inp["L0_ffn_d_codes"])
    dam = np.asarray(inp["L0_ffn_d_absmax"])
    eye = np.eye(128, dtype=BF16NP)
    R = NGU_PAD * 128
    nc = build_dequant()
    in_maps = []
    for k in range(NCORES):
        in_maps.append(dict(
            gc=np.ascontiguousarray(gcod[k * R:(k + 1) * R]).astype(np.int8),
            ga=np.ascontiguousarray(gam[k * R:(k + 1) * R]),
            uc=np.ascontiguousarray(ucod[k * R:(k + 1) * R]).astype(np.int8),
            ua=np.ascontiguousarray(uam[k * R:(k + 1) * R]),
            dcod=np.ascontiguousarray(dcod[k * 512:(k + 1) * 512]),
            dam=np.ascontiguousarray(dam[k * 512:(k + 1) * 512]),
            eye_bf=eye,
        ))
    res = run_bass_kernel_spmd(nc, in_maps, core_ids=list(range(NCORES)))
    wg = np.concatenate([res.results[k]["og"] for k in range(NCORES)], axis=0)[:NJ]
    wu = np.concatenate([res.results[k]["ou"] for k in range(NCORES)], axis=0)[:NJ]
    wd = np.concatenate([res.results[k]["od"] for k in range(NCORES)], axis=0)
    return wg, wu, wd


# version 07
# speedup vs baseline: 1.0076x; 1.0076x over previous
import numpy as np
import ml_dtypes
from contextlib import ExitStack
import concourse.bass as bass
import concourse.mybir as mybir
from concourse.bass_utils import run_bass_kernel_spmd

F32 = mybir.dt.float32
BF16 = mybir.dt.bfloat16
I32 = mybir.dt.int32
I8 = mybir.dt.int8
AF = mybir.ActivationFunctionType
OP = mybir.AluOpType
BF16NP = ml_dtypes.bfloat16

NCORES = 8
T = 8192
TC = T // NCORES
D = 4096
DSL = D // NCORES
NST = 16
DTR = 128
DFF = 11008
NJ = DFF // 128
ND = D // 128
SC = 512
NSC = T // SC
NDN = DSL * NST // 128
EPS = 1e-5

NF4 = [-1.0, -0.6961928009986877, -0.5250730514526367, -0.39491748809814453,
       -0.28444138169288635, -0.18477343022823334, -0.09105003625154495, 0.0,
       0.07958029955625534, 0.16093020141124725, 0.2461123019456863,
       0.33791524171829224, 0.44070982933044434, 0.5626170039176941,
       0.7229568362236023, 1.0]

ENGS = ("sync", "gpsimd", "vector", "scalar", "tensor")


class Prog:

    def __init__(self, sems):
        self.queues = {e: [] for e in ENGS}
        self.sems = sems
        self.C = {k: 0 for k in sems}
        self.mark = {}

    def emit(self, eng, fn, waits=(), inc=None, inc_n=1):
        if inc is not None:
            self.C[inc] += inc_n
        wl = tuple((self.sems[n], v) for (n, v) in waits if v > 0)
        io = (self.sems[inc], inc_n) if inc is not None else None

        def thunk(e, fn=fn, wl=wl, io=io):
            for (s, v) in wl:
                e.wait_ge(s, v)
            r = fn(e)
            if io is not None:
                s, n = io
                if r is None:
                    e.sem_inc(s, n)
                else:
                    r.then_inc(s, n)
        self.queues[eng].append(thunk)
        return self.C[inc] if inc is not None else None

    def dma(self, eng, dst, src, waits=(), name=None):
        assert name is not None
        return self.emit(eng, lambda e, d=dst, s=src: e.dma_start(out=d, in_=s),
                         waits=waits, inc=name, inc_n=16)

    def snap(self, label, name):
        self.mark[label] = (name, self.C[name])

    def at(self, label):
        if label not in self.mark:
            return ()
        n, v = self.mark[label]
        return ((n, v),)


DEBUG_MAMBA = False


def build_forward(stop_after="full"):
    nc = bass.Bass("TRN2", target_bir_lowering=False, debug=False)

    x_tok = nc.dram_tensor("x_tok", [TC, D], F32, kind="ExternalInput").ap()
    x_ds = nc.dram_tensor("x_ds", [T, DSL], F32, kind="ExternalInput").ap()
    xpw = nc.dram_tensor("xpw", [DTR + 2 * NST, D], F32, kind="ExternalInput").ap()
    dtw = nc.dram_tensor("dtw", [DSL, DTR], F32, kind="ExternalInput").ap()
    dtb = nc.dram_tensor("dtb", [DSL, 1], F32, kind="ExternalInput").ap()
    a_pre = nc.dram_tensor("a_pre", [128, NDN], F32, kind="ExternalInput").ap()
    dvec = nc.dram_tensor("dvec", [1, D], F32, kind="ExternalInput").ap()
    e_cat = nc.dram_tensor("e_cat", [16, 128, 128], BF16, kind="ExternalInput").ap()
    f_cat = nc.dram_tensor("f_cat", [16, 128, 128], BF16, kind="ExternalInput").ap()
    g16 = nc.dram_tensor("g16", [16, 128], BF16, kind="ExternalInput").ap()
    eye_bf = nc.dram_tensor("eye_bf", [128, 128], BF16, kind="ExternalInput").ap()
    eye_f32i = nc.dram_tensor("eye_f32", [128, 128], F32, kind="ExternalInput").ap()
    full = stop_after == "full"
    if full:
        gc = nc.dram_tensor("gc", [NGU_PAD * 128, D], I8, kind="ExternalInput").ap()
        ga = nc.dram_tensor("ga", [NGU_PAD * 128, D // 64], F32, kind="ExternalInput").ap()
        uc = nc.dram_tensor("uc", [NGU_PAD * 128, D], I8, kind="ExternalInput").ap()
        ua = nc.dram_tensor("ua", [NGU_PAD * 128, D // 64], F32, kind="ExternalInput").ap()
        dcod = nc.dram_tensor("dcod", [4 * 128, DFF], I8, kind="ExternalInput").ap()
        dam = nc.dram_tensor("dam", [4 * 128, DFF // 64], F32, kind="ExternalInput").ap()
        gla = nc.dram_tensor("gla", [D, 16], F32, kind="ExternalInput").ap()
        glb = nc.dram_tensor("glb", [16, DFF], F32, kind="ExternalInput").ap()
        ula = nc.dram_tensor("ula", [D, 16], F32, kind="ExternalInput").ap()
        ulb = nc.dram_tensor("ulb", [16, DFF], F32, kind="ExternalInput").ap()
        dla = nc.dram_tensor("dla", [DFF, 16], F32, kind="ExternalInput").ap()
        dlb = nc.dram_tensor("dlb", [16, D], F32, kind="ExternalInput").ap()

    x_out = nc.dram_tensor("x_out", [TC, D], F32, kind="ExternalOutput").ap()

    if DEBUG_MAMBA:
        dbg_dtin = nc.dram_tensor("dbg_dtin", [128, T], F32, kind="ExternalOutput").ap()
        dbg_b = nc.dram_tensor("dbg_b", [16, T], F32, kind="ExternalOutput").ap()
        dbg_c = nc.dram_tensor("dbg_c", [16, T], F32, kind="ExternalOutput").ap()
        dbg_m = nc.dram_tensor("dbg_m", [128, 64], F32, kind="ExternalOutput").ap()
        dbg_r = nc.dram_tensor("dbg_r", [128, 64], F32, kind="ExternalOutput").ap()
        dbg_dt = nc.dram_tensor("dbg_dt", [128, 4 * SC], F32, kind="ExternalOutput").ap()
        dbg_carry = nc.dram_tensor("dbg_carry", [128, NDN], F32, kind="ExternalOutput").ap()
        dbg_da = nc.dram_tensor("dbg_da", [128, SC], F32, kind="ExternalOutput").ap()
        dbg_dbx = nc.dram_tensor("dbg_dbx", [128, SC], F32, kind="ExternalOutput").ap()
        dbg_u = nc.dram_tensor("dbg_u", [128, 4 * SC], F32, kind="ExternalOutput").ap()
        dbg_stats = nc.dram_tensor("dbg_stats", [128, 16], F32, kind="ExternalOutput").ap()
        dbg_stt = nc.dram_tensor("dbg_stt", [16, 128], F32, kind="ExternalOutput").ap()
        dbg_xp = nc.dram_tensor("dbg_xp", [128, 8 * 128], F32, kind="ExternalOutput").ap()
        dbg_yloc = nc.dram_tensor("dbg_yloc", [DSL, T], F32, kind="ExternalOutput").ap()
        dbg_h = nc.dram_tensor("dbg_h", [128, SC], F32, kind="ExternalOutput").ap()
        dbg_crep = nc.dram_tensor("dbg_crep", [128, SC], F32, kind="ExternalOutput").ap()
        dbg_cin = nc.dram_tensor("dbg_cin", [128, 1], F32, kind="ExternalOutput").ap()
        dbg_row = nc.dram_tensor("dbg_row", [128, NDN * SC], F32, kind="ExternalOutput").ap()
        dbg_row2 = nc.dram_tensor("dbg_row2", [128, NDN * SC], F32, kind="ExternalOutput").ap()
        dbg_u13 = nc.dram_tensor("dbg_u13", [128, 4 * SC], F32, kind="ExternalOutput").ap()
        dbg_brep = nc.dram_tensor("dbg_brep", [128, SC], F32, kind="ExternalOutput").ap()
        dbg_bsnap = nc.dram_tensor("dbg_bsnap", [8, 16, T], F32, kind="ExternalOutput").ap()
        dbg_crep2 = nc.dram_tensor("dbg_crep2", [128, SC], F32, kind="ExternalOutput").ap()
    if full:
        wsh_gu0 = nc.dram_tensor("wsh_gu0", [4 * 64 * 128, 128], BF16, kind="Internal").ap()
        wsh_gu1 = nc.dram_tensor("wsh_gu1", [7 * 64 * 128, 128], BF16, kind="Internal").ap()
        wsh_d = nc.dram_tensor("wsh_d", [4 * 86 * 128, 128], BF16, kind="Internal").ap()
        wfull_gu0 = nc.dram_tensor("wfull_gu0", [NCORES * 4 * 64 * 128, 128], BF16,
                                   kind="Internal").ap()
        wfull_gu1 = nc.dram_tensor("wfull_gu1", [NCORES * 7 * 64 * 128, 128], BF16,
                                   kind="Internal").ap()
        wfull_d = nc.dram_tensor("wfull_d", [NCORES * 4 * 86 * 128, 128], BF16,
                                 kind="Internal").ap()
    cc1_in = nc.dram_tensor("cc1_in", [162, TC], F32, kind="Internal").ap()
    cc1_out = nc.dram_tensor("cc1_out", [162 * NCORES, TC], F32, kind="Internal").ap()
    cc2_in = nc.dram_tensor("cc2_in", [DSL, T], BF16, kind="Internal").ap()
    cc2_out = nc.dram_tensor("cc2_out", [D, T], BF16, kind="Internal").ap()
    xmid_hbm = nc.dram_tensor("xmid_hbm", [TC, D], F32, kind="Internal").ap()

    with ExitStack() as _ctx:
        g_sb = _ctx.enter_context(nc.sbuf_tensor("g_sb", [16, 128], BF16))
        eyeb = _ctx.enter_context(nc.sbuf_tensor("eyeb", [128, 128], BF16))
        eyef = _ctx.enter_context(nc.sbuf_tensor("eyef", [128, 128], F32))
        stats = _ctx.enter_context(nc.sbuf_tensor("stats", [128, 16], F32))
        st3 = _ctx.enter_context(nc.sbuf_tensor("st3", [128, 16], F32))
        cst = _ctx.enter_context(nc.sbuf_tensor("cst", [128, 2], F32))
        if full:
            dq_cod0 = _ctx.enter_context(nc.sbuf_tensor("dq_cod0", [128, 512], I8))
            dq_cod1 = _ctx.enter_context(nc.sbuf_tensor("dq_cod1", [128, 512], I8))
            dq_amx0 = _ctx.enter_context(nc.sbuf_tensor("dq_amx0", [128, 8], F32))
            dq_amx1 = _ctx.enter_context(nc.sbuf_tensor("dq_amx1", [128, 8], F32))
            dq_acc = _ctx.enter_context(nc.sbuf_tensor("dq_acc", [128, 512], BF16))
            dq_tmp = _ctx.enter_context(nc.sbuf_tensor("dq_tmp", [128, 512], BF16))
            dq_wq = _ctx.enter_context(nc.sbuf_tensor("dq_wq", [128, 512], BF16))
            dq_ob = _ctx.enter_context(nc.sbuf_tensor("dq_ob", [128, 512], BF16))
        DMA_SEMS = ["LDC", "LX0", "LX1", "LP", "LS0", "LS1", "LT0", "LT1",
                    "LY0", "LY1", "LLA", "LW0", "LW1", "LXM", "LMA0", "LMA1",
                    "SY0", "SY1", "SXM0", "SXM1", "SXO0", "SXO1", "SPK",
                    "LR0", "LR1", "LQ0", "LQ1", "SQ0", "SQ1"]
        dma_sems = {n: _ctx.enter_context(nc.semaphore("s" + n)) for n in DMA_SEMS}
        sCC = _ctx.enter_context(nc.semaphore("sCC"))
        sPE1 = _ctx.enter_context(nc.semaphore("sPE1"))
        sPE3 = _ctx.enter_context(nc.semaphore("sPE3"))
        sPE4 = _ctx.enter_context(nc.semaphore("sPE4"))
        sAC = _ctx.enter_context(nc.semaphore("sAC"))
        sACz = _ctx.enter_context(nc.semaphore("sACz"))
        sDV1 = _ctx.enter_context(nc.semaphore("sDV1"))
        sDV2 = _ctx.enter_context(nc.semaphore("sDV2"))
        sDV3 = _ctx.enter_context(nc.semaphore("sDV3"))
        sDV4 = _ctx.enter_context(nc.semaphore("sDV4"))
        sDV5 = _ctx.enter_context(nc.semaphore("sDV5"))
        blk = _ctx.enter_context(nc.Block())
        sGP = _ctx.enter_context(nc.semaphore("sGP"))
        sem_map = dict(CC=sCC, GP=sGP, PE1=sPE1, PE3=sPE3, PE4=sPE4,
                       AC=sAC, ACz=sACz, DV1=sDV1, DV2=sDV2, DV3=sDV3,
                       DV4=sDV4, DV5=sDV5)
        sem_map.update(dma_sems)
        P = Prog(sem_map)
        emit, dma, snap, at, C, mark = P.emit, P.dma, P.snap, P.at, P.C, P.mark

        emit("vector", lambda e: e.memset(cst[:, 0:1], EPS))
        emit("vector", lambda e: e.memset(cst[:, 1:2], -0.5), inc="DV5")
        snap("cst", "DV5")

        dq_state = {"i": 0}
        dq_work = []
        if full:
            v_gu0 = wsh_gu0.rearrange("(a p) f -> a p f", p=128)
            v_gu1 = wsh_gu1.rearrange("(a p) f -> a p f", p=128)
            v_d = wsh_d.rearrange("(a p) f -> a p f", p=128)
            for jc in range(NGU_PAD):
                for mat, codes_t, am_t, uoff in ((0, gc, ga, 0), (1, uc, ua, 32)):
                    for ib in range(8):
                        dst = (v_gu0[(jc * 64 + uoff + ib * 4):
                                     (jc * 64 + uoff + ib * 4) + 4]
                               if jc < 4 else
                               v_gu1[((jc - 4) * 64 + uoff + ib * 4):
                                     ((jc - 4) * 64 + uoff + ib * 4) + 4])
                        dq_work.append((codes_t, am_t, jc, ib, 512, dst))
            for q in range(4):
                for ib in range(22):
                    iw = min(512, DFF - ib * 512)
                    dst = v_d[(q * 86 + ib * 4):(q * 86 + ib * 4) + iw // 128]
                    dq_work.append((dcod, dam, q, ib, iw, dst))

        def emit_dq(n, ptb, eng="vector"):
            for _ in range(n):
                it = dq_state["i"]
                if it >= len(dq_work):
                    return
                dq_state["i"] += 1
                codes_t, am_t, jc, ib, iw, dst = dq_work[it]
                nt = iw // 128
                nb = iw // 64
                lqs = f"LQ{it % 2}"
                sqs = f"SQ{it % 2}"
                lsem = "GP" if eng == "gpsimd" else "DV5"
                dq_cod = (dq_cod0, dq_cod1)[it % 2]
                dq_amx = (dq_amx0, dq_amx1)[it % 2]
                dma("sync", dq_cod[:, 0:iw],
                    codes_t[jc * 128:(jc + 1) * 128, ib * 512:ib * 512 + iw],
                    waits=at(f"dq_lut{it - 2}"), name=lqs)
                w_a = dma("sync", dq_amx[:, 0:nb],
                          am_t[jc * 128:(jc + 1) * 128, ib * 8:ib * 8 + nb],
                          name=lqs)
                csl = dq_cod[:, 0:iw]
                emit(eng, lambda e, csl=csl, iw=iw: e.tensor_scalar(
                    dq_acc[:, 0:iw], csl, 0.0, float(NF4[0]),
                    OP.is_equal, OP.mult),
                    waits=((lqs, w_a),))
                for k in range(1, 16):
                    emit(eng, lambda e, csl=csl, iw=iw, k=k: e.tensor_scalar(
                        dq_tmp[:, 0:iw], csl, float(k), float(NF4[k]),
                        OP.is_equal, OP.mult))
                    emit(eng, lambda e, iw=iw: e.tensor_add(
                        dq_acc[:, 0:iw], dq_acc[:, 0:iw], dq_tmp[:, 0:iw]))
                lut = emit(eng, lambda e, iw=iw, nb=nb, am=dq_amx: e.tensor_tensor(
                    dq_wq[:, 0:iw].rearrange("p (b z) -> p b z", z=64),
                    dq_acc[:, 0:iw].rearrange("p (b z) -> p b z", z=64),
                    am[:, 0:nb].unsqueeze(2)
                    .broadcast_to([128, nb, 64]), OP.mult),
                    waits=(("PE4", dq_state.get("tpm", 0)),), inc=lsem)
                snap(f"dq_lut{it}", lsem)
                tpm = None
                for s in range(nt):
                    tpm = emit("tensor", lambda e, s=s, ptb=ptb: e.transpose(
                        ptb[:, s * 128:(s + 1) * 128],
                        dq_wq[:, s * 128:(s + 1) * 128],
                        eyeb[:]),
                        waits=(((lsem, lut), ("DV4", C["DV4"])) if s == 0 else ()),
                        inc="PE4" if s == nt - 1 else None)
                dq_state["tpm"] = tpm
                ev = emit("vector", lambda e, iw=iw, ptb=ptb: e.tensor_copy(
                    dq_ob[:, 0:iw], ptb[:, 0:iw]),
                    waits=(("PE4", tpm),) + at(f"dq_st{it - 1}"), inc="DV4")
                dma("gpsimd", dst.transpose([1, 0, 2]),
                    dq_ob[:, 0:iw].rearrange("p (c f) -> p c f", f=128),
                    waits=(("DV4", ev),), name=sqs)
                snap(f"dq_st{it}", sqs)
                if dq_state["i"] == 64:
                    snap("dq_b0_0", "SQ0")
                    snap("dq_b0_1", "SQ1")
                if dq_state["i"] == 176:
                    snap("dq_b1_0", "SQ0")
                    snap("dq_b1_1", "SQ1")
                if dq_state["i"] == len(dq_work):
                    snap("dq_b2_0", "SQ0")
                    snap("dq_b2_1", "SQ1")

        with ExitStack() as _ctx:
            xpwT = _ctx.enter_context(nc.sbuf_tensor("xpwT", [128, 32 * 160], BF16))
            wstg = _ctx.enter_context(nc.sbuf_tensor("wstg", [128, 4096], F32))
            wstg2 = _ctx.enter_context(nc.sbuf_tensor("wstg2", [32, 4096], F32))
            wstgb = _ctx.enter_context(nc.sbuf_tensor("wstgb", [128, 4096], BF16))
            wstgb2 = _ctx.enter_context(nc.sbuf_tensor("wstgb2", [32, 4096], BF16))
            xt = _ctx.enter_context(nc.sbuf_tensor("xt", [128, 2 * D], F32))
            xlnb = _ctx.enter_context(nc.sbuf_tensor("xlnb", [128, 2 * D], BF16))
            xlnT1 = _ctx.enter_context(nc.sbuf_tensor("xlnT1", [128, D], BF16))
            bnbuf = _ctx.enter_context(nc.sbuf_tensor("bnbuf", [128, 48], F32))
            mv = _ctx.enter_context(nc.sbuf_tensor("mv", [128, 16], F32))
            xp_sb = _ctx.enter_context(nc.sbuf_tensor("xp_sb", [128, 8 * 128], F32))
            bxp_sb = _ctx.enter_context(nc.sbuf_tensor("bxp_sb", [16, 8 * 128], F32))
            cxp_sb = _ctx.enter_context(nc.sbuf_tensor("cxp_sb", [16, 8 * 128], F32))
            st_t = _ctx.enter_context(nc.sbuf_tensor("st_t", [16, 128], F32))
            p1t0 = _ctx.enter_context(nc.psum_tensor("p1t0", [128, 512], BF16))
            p1t1 = _ctx.enter_context(nc.psum_tensor("p1t1", [128, 512], BF16))
            pxd = _ctx.enter_context(nc.psum_tensor("pxd", [128, 128], F32))
            pxb = _ctx.enter_context(nc.psum_tensor("pxb", [16, 128], F32))
            pxc = _ctx.enter_context(nc.psum_tensor("pxc", [16, 128], F32))
            pst = _ctx.enter_context(nc.psum_tensor("pst", [16, 128], F32))
            p1t = (p1t0, p1t1)
            dma("sync", eyeb[:], eye_bf, name="LDC")
            dma("sync", eyef[:], eye_f32i, name="LDC")
            dma("sync", g_sb[:], g16, name="LDC")
            dma("sync", wstg[:], xpw[0:128, :], name="LDC")
            ldc_all = dma("sync", wstg2[:], xpw[128:160, :], name="LDC")
            emit("vector", lambda e: e.tensor_copy(wstgb[:], wstg[:]),
                 waits=(("LDC", ldc_all),), inc="DV5")
            c2 = emit("vector", lambda e: e.tensor_copy(wstgb2[:], wstg2[:]),
                      inc="DV5")
            for dc in range(32):
                emit("tensor", lambda e, dc=dc: e.transpose(
                    p1t0[:, 0:128], wstgb[:, dc * 128:(dc + 1) * 128], eyeb[:]),
                    waits=(("DV5", c2), ("DV4", C["DV4"])), inc="PE4")
                t2 = emit("tensor", lambda e, dc=dc: e.transpose(
                    p1t0[:, 128:160], wstgb2[:, dc * 128:(dc + 1) * 128],
                    eyeb[0:32, 0:32]),
                    inc="PE4")
                emit("vector", lambda e, dc=dc: e.tensor_copy(
                    xpwT[:, dc * 160:dc * 160 + 128], p1t0[:, 0:128]),
                    waits=(("PE4", t2),), inc="DV4")
                emit("vector", lambda e, dc=dc: e.tensor_copy(
                    xpwT[:, dc * 160 + 128:dc * 160 + 160], p1t0[:, 128:160]),
                    inc="DV4")

            for c in range(8):
                par = c % 2
                lxs = f"LX{c % 2}"
                w_x = dma("sync", xt[:, par * D:(par + 1) * D],
                          x_tok[c * 128:(c + 1) * 128, :],
                          waits=at(f"ph1_ln{c - 2}"), name=lxs)
                bnm = None
                for b in range(8):
                    bnm = emit("vector", lambda e, b=b, par=par: e.bn_stats(
                        bnbuf[:, 6 * b:6 * b + 6],
                        xt[:, par * D + 512 * b:par * D + 512 * (b + 1)]),
                        waits=((lxs, w_x),) if b == 0 else (),
                        inc="DV2" if b == 7 else None)
                ag = emit("vector", lambda e, c=c: e.bn_aggr(
                    mv[:, 2 * c:2 * c + 2], bnbuf[:]),
                    waits=(("DV2", bnm),), inc="DV2")
                m1 = emit("vector", lambda e, c=c: e.tensor_copy(
                    stats[:, c:c + 1], mv[:, 2 * c:2 * c + 1]),
                    waits=(("DV2", ag),), inc="DV5")
                emit("scalar", lambda e, c=c: e.activation(
                    mv[:, 2 * c + 1:2 * c + 2], mv[:, 2 * c + 1:2 * c + 2],
                    AF.Ln, bias=cst[:, 0:1]),
                    waits=(("DV5", m1),) + at("cst"))
                r1 = emit("scalar", lambda e, c=c: e.activation(
                    stats[:, 8 + c:9 + c], mv[:, 2 * c + 1:2 * c + 2],
                    AF.Exp, scale=cst[:, 1:2]),
                    inc="AC")
                lnm = emit("vector", lambda e, c=c, par=par: e.tensor_scalar(
                    xlnb[:, par * D:(par + 1) * D], xt[:, par * D:(par + 1) * D],
                    stats[:, c:c + 1], stats[:, 8 + c:9 + c], OP.subtract, OP.mult),
                    waits=(("AC", r1),) + at(f"ph1_tp{c - 2}"), inc="DV5")
                snap(f"ph1_ln{c}", "DV5")
                last_tp = None
                for b in range(8):
                    tb = b % 2
                    for s in range(4):
                        last_tp = emit("tensor", lambda e, b=b, s=s, par=par, tb=tb:
                                       e.transpose(
                            p1t[tb][:, s * 128:(s + 1) * 128],
                            xlnb[:, par * D + (4 * b + s) * 128:
                                 par * D + (4 * b + s + 1) * 128], eyeb[:]),
                            waits=((("DV5", lnm),) if (b == 0 and s == 0) else ()) +
                                  ((("DV4", C["DV4"]),) if s == 0 else ()),
                            inc="PE4" if s == 3 else None)
                    emit("vector", lambda e, b=b, tb=tb: e.tensor_copy(
                        xlnT1[:, b * 512:(b + 1) * 512], p1t[tb][:]),
                        waits=(("PE4", last_tp),), inc="DV4")
                snap(f"ph1_tp{c}", "PE4")
                ev = C["DV4"]
                mmx = None
                for dc in range(32):
                    emit("tensor", lambda e, dc=dc: e.matmul(
                        pxd[:], xpwT[:, dc * 160:dc * 160 + 128],
                        xlnT1[:, dc * 128:(dc + 1) * 128],
                        start=(dc == 0), stop=(dc == 31)),
                        waits=((("DV4", ev),) + at(f"ph1_xpe{c - 1}")) if dc == 0 else ())
                    emit("tensor", lambda e, dc=dc: e.matmul(
                        pxb[:], xpwT[:, dc * 160 + 128:dc * 160 + 144],
                        xlnT1[:, dc * 128:(dc + 1) * 128],
                        start=(dc == 0), stop=(dc == 31)))
                    mmx = emit("tensor", lambda e, dc=dc: e.matmul(
                        pxc[:], xpwT[:, dc * 160 + 144:dc * 160 + 160],
                        xlnT1[:, dc * 128:(dc + 1) * 128],
                        start=(dc == 0), stop=(dc == 31)),
                        inc="PE4" if dc == 31 else None)
                emit("vector", lambda e, c=c: e.tensor_copy(
                    xp_sb[:, c * 128:(c + 1) * 128], pxd[:]),
                    waits=(("PE4", mmx),), inc="DV4")
                emit("vector", lambda e, c=c: e.tensor_copy(
                    bxp_sb[:, c * 128:(c + 1) * 128], pxb[:]), inc="DV4")
                emit("vector", lambda e, c=c: e.tensor_copy(
                    cxp_sb[:, c * 128:(c + 1) * 128], pxc[:]), inc="DV4")
                snap(f"ph1_xpe{c}", "DV4")

            if DEBUG_MAMBA:
                dma("gpsimd", dbg_stats, stats[:],
                    waits=(("DV4", C["DV4"]), ("AC", C["AC"])), name="SPK")
                dma("gpsimd", dbg_xp, xp_sb[:], waits=(("DV4", C["DV4"]),), name="SPK")
            stt = emit("tensor", lambda e: e.transpose(pst[:], stats[:], eyef[:]),
                       waits=(("DV4", C["DV4"]), ("AC", C["AC"])), inc="PE4")
            stev = emit("vector", lambda e: e.tensor_copy(st_t[:], pst[:]),
                 waits=(("PE4", stt),), inc="DV4")
            if DEBUG_MAMBA:
                dma("gpsimd", dbg_stt, st_t[:], waits=(("DV4", stev),), name="SPK")
            dma("gpsimd", cc1_in[0:128, :], xp_sb[:], waits=(("DV4", C["DV4"]),), name="SPK")
            dma("gpsimd", cc1_in[128:144, :], bxp_sb[:], name="SPK")
            dma("gpsimd", cc1_in[144:160, :], cxp_sb[:], name="SPK")
            dma("gpsimd", cc1_in[160:161, :].rearrange("o (c f) -> (o c) f", f=128),
                st_t[0:8, :], name="SPK")
            dma("gpsimd", cc1_in[161:162, :].rearrange("o (c f) -> (o c) f", f=128),
                st_t[8:16, :], name="SPK")
            cc1_done = emit("gpsimd", lambda e: e.collective_compute(
                "AllGather", OP.bypass, replica_groups=[list(range(NCORES))],
                ins=[cc1_in.opt()], outs=[cc1_out.opt()]),
                waits=(("SPK", C["SPK"]),), inc="CC")

        with ExitStack() as _ctx:
            dtwT = _ctx.enter_context(nc.sbuf_tensor("dtwT", [128, 4 * 128], BF16))
            dtb_sb = _ctx.enter_context(nc.sbuf_tensor("dtb_sb", [128, 4], F32))
            acol = _ctx.enter_context(nc.sbuf_tensor("acol", [128, NDN], F32))
            e_sb = _ctx.enter_context(nc.sbuf_tensor("e_sb", [128, 16 * 128], BF16))
            f_sb = _ctx.enter_context(nc.sbuf_tensor("f_sb", [128, 16 * 128], BF16))
            mfull = _ctx.enter_context(nc.sbuf_tensor("mfull", [128, 64], F32))
            rfull = _ctx.enter_context(nc.sbuf_tensor("rfull", [128, 64], F32))
            carry = _ctx.enter_context(nc.sbuf_tensor("carry", [128, NDN], F32))
            dtin = _ctx.enter_context(nc.sbuf_tensor("dtin", [128, T], BF16))
            b_sb = _ctx.enter_context(nc.sbuf_tensor("b_sb", [16, T], BF16))
            c_sb = _ctx.enter_context(nc.sbuf_tensor("c_sb", [16, T], BF16))
            ldstg = _ctx.enter_context(nc.sbuf_tensor("ldstg", [128, 1024], F32))
            ldstg2 = _ctx.enter_context(nc.sbuf_tensor("ldstg2", [16, 2048], F32))
            xds = _ctx.enter_context(nc.sbuf_tensor("xds", [128, 8 * SC], F32))
            xlnds = _ctx.enter_context(nc.sbuf_tensor("xlnds", [128, 8 * SC], BF16))
            xlnT = _ctx.enter_context(nc.sbuf_tensor("xlnT", [128, 8 * SC], BF16))
            dt_sb = _ctx.enter_context(nc.sbuf_tensor("dt_sb", [128, 8 * SC], F32))
            dtbf = _ctx.enter_context(nc.sbuf_tensor("dtbf", [128, 8 * SC], BF16))
            u_sb = _ctx.enter_context(nc.sbuf_tensor("u_sb", [128, 8 * SC], BF16))
            brep = _ctx.enter_context(nc.sbuf_tensor("brep", [128, 2 * SC], BF16))
            crep = _ctx.enter_context(nc.sbuf_tensor("crep", [128, 2 * SC], BF16))
            zb_sb = _ctx.enter_context(nc.sbuf_tensor("zb_sb", [128, 2 * SC], F32))
            sp_r = _ctx.enter_context(nc.sbuf_tensor("sp_r", [128, 2 * SC], F32))
            sp_l = _ctx.enter_context(nc.sbuf_tensor("sp_l", [128, 2 * SC], F32))
            dA = _ctx.enter_context(nc.sbuf_tensor("dA", [128, 4 * SC], F32))
            dbx = _ctx.enter_context(nc.sbuf_tensor("dbx", [128, 4 * SC], BF16))
            h_sb = _ctx.enter_context(nc.sbuf_tensor("h_sb", [128, 4 * SC], BF16))
            hc_sb = _ctx.enter_context(nc.sbuf_tensor("hc_sb", [128, 4 * SC], BF16))
            ysb = _ctx.enter_context(nc.sbuf_tensor("ysb", [128, 2 * SC], BF16))
            ub_sb = _ctx.enter_context(nc.sbuf_tensor("ub_sb", [128, 4 * SC], BF16))
            pAw = _ctx.enter_context(nc.psum_tensor("pAw", [128, 2 * SC], F32))
            pBw = _ctx.enter_context(nc.psum_tensor("pBw", [128, 2 * SC], F32))
            pY0 = _ctx.enter_context(nc.psum_tensor("pY0", [128, SC], F32))
            pY1 = _ctx.enter_context(nc.psum_tensor("pY1", [128, SC], F32))
            pTb0 = _ctx.enter_context(nc.psum_tensor("pTb0", [128, SC], BF16))
            pTb1 = _ctx.enter_context(nc.psum_tensor("pTb1", [128, SC], BF16))
            pY, pTb = (pY0, pY1), (pTb0, pTb1)
            for v in range(16):
                dma("sync", e_sb[:, v * 128:(v + 1) * 128], e_cat[v], name="LP",
                    waits=(("CC", cc1_done),) if v == 0 else ())
                dma("sync", f_sb[:, v * 128:(v + 1) * 128], f_cat[v], name="LP")
            dma("sync", acol[:], a_pre, name="LP")
            for dc in range(4):
                dma("sync", dtb_sb[:, dc:dc + 1], dtb[dc * 128:(dc + 1) * 128, :],
                    name="LP")
            prep_efa = C["LP"]
            a1 = emit("scalar", lambda e: e.activation(acol[:], acol[:], AF.Exp),
                      waits=(("LP", prep_efa),), inc="AC")
            acneg = emit("vector", lambda e: e.tensor_scalar_mul(acol[:], acol[:], -1.0),
                         waits=(("AC", a1),), inc="DV5")
            wd1 = dma("sync", ldstg[:, 0:512].rearrange("p (c f) -> p c f", f=128),
                      dtw.rearrange("(c p) f -> p c f", p=128), name="LP")
            cd1 = emit("vector", lambda e: e.tensor_copy(dtbf[:, 0:512], ldstg[:, 0:512]),
                       waits=(("LP", wd1),), inc="DV5")
            for dc in range(4):
                td = emit("tensor", lambda e, dc=dc: e.transpose(
                    pTb0[:, 0:128], dtbf[:, dc * 128:(dc + 1) * 128], eyeb[:]),
                    waits=(("DV5", cd1), ("DV4", C["DV4"])), inc="PE4")
                emit("vector", lambda e, dc=dc: e.tensor_copy(
                    dtwT[:, dc * 128:(dc + 1) * 128], pTb0[:, 0:128]),
                    waits=(("PE4", td),), inc="DV4")
            for j in range(NCORES):
                dma("sync", ldstg[:, 0:1024], cc1_out[162 * j:162 * j + 128, :],
                    waits=(("CC", cc1_done), ("DV5", cd1)) if j == 0
                    else at("prep_cast2"), name="LP")
                dma("sync", ldstg2[:, 0:1024], cc1_out[162 * j + 128:162 * j + 144, :],
                    name="LP")
                dma("sync", ldstg2[:, 1024:2048],
                    cc1_out[162 * j + 144:162 * j + 160, :], name="LP")
                for q in range(8):
                    dma("sync", mfull[:, 8 * j + q:8 * j + q + 1],
                        cc1_out[162 * j + 160:162 * j + 161, 128 * q:128 * (q + 1)]
                        .rearrange("o f -> (o f)").unsqueeze(1), name="LP")
                    wj = dma("sync", rfull[:, 8 * j + q:8 * j + q + 1],
                             cc1_out[162 * j + 161:162 * j + 162, 128 * q:128 * (q + 1)]
                             .rearrange("o f -> (o f)").unsqueeze(1), name="LP")
                emit("vector", lambda e, j=j: e.tensor_copy(
                    dtin[:, j * 1024:(j + 1) * 1024], ldstg[:, 0:1024]),
                    waits=(("LP", wj),), inc="DV5")
                emit("vector", lambda e, j=j: e.tensor_copy(
                    b_sb[:, j * 1024:(j + 1) * 1024], ldstg2[:, 0:1024]),
                    inc="DV5")
                emit("vector", lambda e, j=j: e.tensor_copy(
                    c_sb[:, j * 1024:(j + 1) * 1024], ldstg2[:, 1024:2048]),
                    inc="DV5")
                snap("prep_cast2", "DV5")
            prep_ld = C["LP"]
            if DEBUG_MAMBA:
                dma("gpsimd", dbg_dtin, dtin[:], waits=(("LP", prep_ld), ("DV5", C["DV5"])), name="SPK")
                dma("gpsimd", dbg_m, mfull[:], name="SPK")
                dma("gpsimd", dbg_r, rfull[:], name="SPK")

            for tc in range(NSC):
                par = tc % 2
                lss = f"LS{tc % 2}"
                w_x = None
                for q in range(4):
                    w_x = dma("sync", xds[:, (par * 4 + q) * SC:(par * 4 + q + 1) * SC],
                              x_ds.rearrange("(a p) f -> a p f", p=128)[tc * 4 + q],
                              waits=at(f"lnq{tc - 2}") if q == 0 else (), name=lss)
                lnq = None
                for q in range(4):
                    lnq = emit("vector", lambda e, q=q, par=par, tc=tc: e.tensor_scalar(
                        xlnds[:, (par * 4 + q) * SC:(par * 4 + q + 1) * SC],
                        xds[:, (par * 4 + q) * SC:(par * 4 + q + 1) * SC],
                        mfull[:, tc * 4 + q:tc * 4 + q + 1],
                        rfull[:, tc * 4 + q:tc * 4 + q + 1], OP.subtract, OP.mult),
                        waits=((lss, w_x),) +
                              ((("LP", prep_ld),) if tc == 0 else ()) +
                              (at(f"scan_tp{tc - 2}") if q == 0 else ()),
                        inc="DV5")
                snap(f"lnq{tc}", "DV5")
                last_tp = None
                for dsub in range(4):
                    tb = 0
                    for q in range(4):
                        last_tp = emit("tensor",
                                       lambda e, q=q, dsub=dsub, par=par, tb=tb:
                                       e.transpose(
                            pTb[tb][:, q * 128:(q + 1) * 128],
                            xlnds[:, (par * 4 + q) * SC + dsub * 128:
                                  (par * 4 + q) * SC + (dsub + 1) * 128], eyeb[:]),
                            waits=((("DV5", lnq),) if (dsub == 0 and q == 0) else ()) +
                                  ((("DV4", C["DV4"]),) if q == 0 else ()),
                            inc="PE4" if q == 3 else None)
                    emit("vector", lambda e, dsub=dsub, par=par, tb=tb: e.tensor_copy(
                        xlnT[:, (par * 4 + dsub) * SC:(par * 4 + dsub + 1) * SC],
                        pTb[tb][:]),
                        waits=(("PE4", last_tp),), inc="DV4")
                snap(f"scan_tp{tc}", "PE4")
                if DEBUG_MAMBA and tc % 2 == 0:
                    dma("gpsimd", dbg_bsnap[tc // 2], b_sb[:], name="SPK")
                lrs = f"LR{tc % 2}"
                rep = None
                for kk in range(8):
                    dma("gpsimd", brep[16 * kk:16 * (kk + 1), par * SC:(par + 1) * SC],
                        b_sb[:, tc * SC:(tc + 1) * SC],
                        waits=(at(f"hcl{tc - 2}") + ((("LP", prep_ld),) if tc == 0 else ()))
                        if kk == 0 else (), name=lrs)
                    rep = dma("gpsimd",
                              crep[16 * kk:16 * (kk + 1), par * SC:(par + 1) * SC],
                              c_sb[:, tc * SC:(tc + 1) * SC], name=lrs)
                dtu = None
                for dc in range(4):
                    bpar = dc % 2
                    mmdt = emit("tensor", lambda e, dc=dc, tc=tc, bpar=bpar: e.matmul(
                        pAw[:, bpar * SC:(bpar + 1) * SC],
                        dtwT[:, dc * 128:(dc + 1) * 128],
                        dtin[:, tc * SC:(tc + 1) * SC], start=True, stop=True),
                        waits=(at(f"dA_last{tc - 1}") if dc < 2 else ()) +
                              ((("ACz", C["ACz"] - 1),) if dc >= 2 else ()) +
                              ((("DV4", C["DV4"]),) if dc == 0 else ()),
                        inc="PE4")
                    zz = emit("scalar", lambda e, dc=dc, par=par, bpar=bpar: e.activation(
                        zb_sb[:, par * SC:(par + 1) * SC],
                        pAw[:, bpar * SC:(bpar + 1) * SC],
                        AF.Identity, bias=dtb_sb[:, dc:dc + 1]),
                        waits=(("PE4", mmdt),) +
                              (at(f"dtu{tc}:{dc - 1}") if dc > 0 else
                               at(f"dtu{tc - 1}:3")),
                        inc="ACz")
                    emit("scalar", lambda e, par=par: e.activation(
                        sp_r[:, par * SC:(par + 1) * SC],
                        zb_sb[:, par * SC:(par + 1) * SC], AF.Relu))
                    emit("scalar", lambda e, par=par: e.activation(
                        sp_l[:, par * SC:(par + 1) * SC],
                        zb_sb[:, par * SC:(par + 1) * SC], AF.Abs))
                    emit("scalar", lambda e, par=par: e.activation(
                        sp_l[:, par * SC:(par + 1) * SC],
                        sp_l[:, par * SC:(par + 1) * SC], AF.Exp, scale=-1.0))
                    spl = emit("scalar", lambda e, par=par: e.activation(
                        sp_l[:, par * SC:(par + 1) * SC],
                        sp_l[:, par * SC:(par + 1) * SC], AF.Ln, bias=1.0), inc="AC")
                    emit("vector", lambda e, dc=dc, par=par: e.tensor_add(
                        dt_sb[:, (par * 4 + dc) * SC:(par * 4 + dc + 1) * SC],
                        sp_r[:, par * SC:(par + 1) * SC],
                        sp_l[:, par * SC:(par + 1) * SC]),
                        waits=(("AC", spl),) +
                              ((("PE1", mark.get(f"mmlast{tc - 2}", ("PE1", 0))[1]),)
                               if dc == 0 else ()))
                    emit("vector", lambda e, dc=dc, par=par: e.tensor_copy(
                        dtbf[:, (par * 4 + dc) * SC:(par * 4 + dc + 1) * SC],
                        dt_sb[:, (par * 4 + dc) * SC:(par * 4 + dc + 1) * SC]))
                    dtu = emit("vector", lambda e, dc=dc, par=par: e.tensor_tensor(
                        u_sb[:, (par * 4 + dc) * SC:(par * 4 + dc + 1) * SC],
                        dt_sb[:, (par * 4 + dc) * SC:(par * 4 + dc + 1) * SC],
                        xlnT[:, (par * 4 + dc) * SC:(par * 4 + dc + 1) * SC],
                        OP.mult), inc="DV5")
                    snap(f"dtu{tc}:{dc}", "DV5")
                if DEBUG_MAMBA and tc == 12:
                    dma("gpsimd", dbg_dt, dt_sb[:, 0:4 * SC], waits=(("DV5", dtu),), name="SPK")
                    dma("gpsimd", dbg_u, u_sb[:, 0:4 * SC], waits=(("DV5", dtu),), name="SPK")

                if DEBUG_MAMBA and tc == 14:
                    dma("gpsimd", dbg_brep, brep[:, par * SC:(par + 1) * SC],
                        waits=((lrs, rep),), name="SPK")
                    dma("gpsimd", dbg_crep2, crep[:, par * SC:(par + 1) * SC],
                        waits=((lrs, rep),), name="SPK")

                if full:
                    emit_dq(2, pTb1)
                for m in range(NDN // 2):
                    dc0, dc1 = 2 * m, 2 * m + 1
                    mp = m % 2
                    v0, v1 = dc0 % 16, dc1 % 16
                    dch = dc0 // 16
                    g_idx = tc * 4 + dch
                    yb = g_idx % 2
                    dsl = slice((par * 4 + dch) * SC, (par * 4 + dch + 1) * SC)
                    ip0 = slice(mp * 2 * SC, mp * 2 * SC + SC)
                    ip1 = slice(mp * 2 * SC + SC, (mp + 1) * 2 * SC)
                    ipp = slice(mp * 2 * SC, (mp + 1) * 2 * SC)
                    pa, pb = pAw, pBw

                    emit("tensor", lambda e, v0=v0, dsl=dsl, pa=pa: e.matmul(
                        pa[:, 0:SC], e_sb[:, v0 * 128:(v0 + 1) * 128], dtbf[:, dsl],
                        start=True, stop=True),
                        waits=at(f"dAp{m - 1}") +
                              ((("ACz", C["ACz"]), ("DV5", dtu)) if m == 0 else ()) +
                              ((("LP", prep_efa),) if (tc == 0 and m == 0) else ()))
                    mma = emit("tensor", lambda e, v1=v1, dsl=dsl, pa=pa: e.matmul(
                        pa[:, SC:2 * SC], e_sb[:, v1 * 128:(v1 + 1) * 128],
                        dtbf[:, dsl], start=True, stop=True), inc="PE1")
                    emit("tensor", lambda e, v0=v0, dsl=dsl, pb=pb: e.matmul(
                        pb[:, 0:SC], e_sb[:, v0 * 128:(v0 + 1) * 128], u_sb[:, dsl],
                        start=True, stop=True),
                        waits=at(f"ubp{m - 1}"))
                    mmb = emit("tensor", lambda e, v1=v1, dsl=dsl, pb=pb: e.matmul(
                        pb[:, SC:2 * SC], e_sb[:, v1 * 128:(v1 + 1) * 128],
                        u_sb[:, dsl], start=True, stop=True), inc="PE1")
                    if m == NDN // 2 - 1:
                        snap(f"mmlast{tc}", "PE1")
                    emit("scalar", lambda e, pa=pa, ipx=ip0, dc0=dc0: e.activation(
                        dA[:, ipx], pa[:, 0:SC], AF.Exp, scale=acol[:, dc0:dc0 + 1]),
                        waits=(("PE1", mma),) + at(f"scanp{m - 2}") +
                              ((("DV5", acneg),) if (tc == 0 and m == 0) else ()))
                    da_m = emit("scalar", lambda e, pa=pa, ipx=ip1, dc1=dc1:
                                e.activation(
                        dA[:, ipx], pa[:, SC:2 * SC], AF.Exp,
                        scale=acol[:, dc1:dc1 + 1]), inc="AC")
                    snap(f"dAp{m}", "AC")
                    if m == NDN // 2 - 1:
                        snap(f"dA_last{tc}", "AC")
                    ubm = emit("scalar", lambda e, pb=pb, ipx=ipp: e.activation(
                        ub_sb[:, ipx], pb[:], AF.Copy),
                        waits=(("PE1", mmb),) + at(f"dbxp{m - 2}"), inc="AC")
                    snap(f"ubp{m}", "AC")
                    dbxm = emit("vector", lambda e, ipx=ipp, par=par: e.tensor_tensor(
                        dbx[:, ipx].rearrange("p (c f) -> p c f", f=SC),
                        ub_sb[:, ipx].rearrange("p (c f) -> p c f", f=SC),
                        brep[:, par * SC:(par + 1) * SC].unsqueeze(1)
                        .broadcast_to([128, 2, SC]), OP.mult),
                        waits=(("AC", ubm),) +
                              (((lrs, rep),) if m == 0 else ()),
                        inc="DV1")
                    snap(f"dbxp{m}", "DV1")
                    for s, (dcx, ipx) in enumerate(((dc0, ip0), (dc1, ip1))):
                        if tc == 0:
                            emit("vector", lambda e, ipx=ipx: e.tensor_tensor_scan(
                                h_sb[:, ipx], dA[:, ipx], dbx[:, ipx], 0.0,
                                OP.mult, OP.add),
                                waits=(("AC", da_m),) if s == 0 else (),
                                inc="DV2" if s == 1 else None)
                        else:
                            emit("vector", lambda e, ipx=ipx, dcx=dcx:
                                 e.tensor_tensor_scan(
                                h_sb[:, ipx], dA[:, ipx], dbx[:, ipx],
                                carry[:, dcx:dcx + 1], OP.mult, OP.add),
                                waits=(("AC", da_m),) if s == 0 else (),
                                inc="DV2" if s == 1 else None)
                    scm = C["DV2"]
                    snap(f"scanp{m}", "DV2")
                    emit("vector", lambda e, ipx=ipp, dc0=dc0: e.tensor_copy(
                        carry[:, dc0:dc0 + 2].unsqueeze(2),
                        h_sb[:, ipx].rearrange("p (c f) -> p c f", f=SC)
                        [:, :, SC - 1:SC]),
                        waits=(("DV2", scm),))
                    hcm = emit("vector", lambda e, ipx=ipp, par=par: e.tensor_tensor(
                        hc_sb[:, ipx].rearrange("p (c f) -> p c f", f=SC),
                        h_sb[:, ipx].rearrange("p (c f) -> p c f", f=SC),
                        crep[:, par * SC:(par + 1) * SC].unsqueeze(1)
                        .broadcast_to([128, 2, SC]), OP.mult),
                        waits=at(f"mmFp{m - 2}"), inc="DV3")
                    if m == NDN // 2 - 1:
                        snap(f"hcl{tc}", "DV3")
                    emit("tensor", lambda e, v0=v0, ipx=ip0, yb=yb: e.matmul(
                        pY[yb][:], f_sb[:, v0 * 128:(v0 + 1) * 128], hc_sb[:, ipx],
                        start=(v0 == 0), stop=False),
                        waits=(("DV3", hcm),) +
                              (at(f"yev{g_idx - 2}") if v0 == 0 else ()))
                    mmf = emit("tensor", lambda e, v1=v1, ipx=ip1, yb=yb: e.matmul(
                        pY[yb][:], f_sb[:, v1 * 128:(v1 + 1) * 128], hc_sb[:, ipx],
                        start=False, stop=(v1 == 15)),
                        inc="PE3")
                    snap(f"mmFp{m}", "PE3")
                    if v1 == 15:
                        yev = emit("vector", lambda e, yb=yb: e.tensor_copy(
                            ysb[:, yb * SC:(yb + 1) * SC], pY[yb][:]),
                            waits=(("PE3", mmf),) + at(f"yst{g_idx - 2}"),
                            inc="DV4")
                        snap(f"yev{g_idx}", "DV4")
                        dma("gpsimd",
                            cc2_in[dch * 128:(dch + 1) * 128, tc * SC:(tc + 1) * SC],
                            ysb[:, yb * SC:(yb + 1) * SC],
                            waits=(("DV4", yev),), name=f"SY{g_idx % 2}")
                        snap(f"yst{g_idx}", f"SY{g_idx % 2}")
            if DEBUG_MAMBA:
                dma("gpsimd", dbg_b[:, 0:T], b_sb[:],
                    waits=(("SY0", C["SY0"]), ("SY1", C["SY1"])), name="SPK")
                dma("gpsimd", dbg_c[:, 0:T], c_sb[:], name="SPK")
                dma("gpsimd", dbg_u13, u_sb[:, 0:4 * SC], name="SPK")
                dma("gpsimd", dbg_yloc, cc2_in,
                    waits=(("SY0", C["SY0"]), ("SY1", C["SY1"])), name="SPK")
                dma("gpsimd", dbg_carry, carry[:],
                    waits=(("DV2", mark["scan63"][1] if "scan63" in mark else 0),),
                    name="SPK")
            cc2_done = emit("gpsimd", lambda e: e.collective_compute(
                "AllGather", OP.bypass, replica_groups=[list(range(NCORES))],
                ins=[cc2_in.opt()], outs=[cc2_out.opt()]),
                waits=(("SY0", C["SY0"]), ("SY1", C["SY1"])), inc="CC")
            scan_dv4 = C["DV4"]

        with ExitStack() as _ctx:
            db_sb = _ctx.enter_context(nc.sbuf_tensor("db_sb", [128, D], F32))
            dstg = _ctx.enter_context(nc.sbuf_tensor("dstg", [1, D], F32))
            xt3 = _ctx.enter_context(nc.sbuf_tensor("xt3", [128, 2 * D], F32))
            yfm = _ctx.enter_context(nc.sbuf_tensor("yfm", [128, 2 * D], BF16))
            xmid = _ctx.enter_context(nc.sbuf_tensor("xmid", [128, 2 * D], F32))
            t3buf = _ctx.enter_context(nc.sbuf_tensor("t3buf", [128, D], F32))
            bnbuf3 = _ctx.enter_context(nc.sbuf_tensor("bnbuf3", [128, 48], F32))
            mv3 = _ctx.enter_context(nc.sbuf_tensor("mv3", [128, 16], F32))
            qm0 = _ctx.enter_context(nc.psum_tensor("qm0", [128, 512], BF16))
            qm1 = _ctx.enter_context(nc.psum_tensor("qm1", [128, 512], BF16))
            mdq = _ctx.enter_context(nc.psum_tensor("mdq", [128, 512], BF16))
            qm = (qm0, qm1)
            if full:
                emit_dq(48, mdq)
                cc_gu0 = emit("gpsimd", lambda e: e.collective_compute(
                    "AllGather", OP.bypass, replica_groups=[list(range(NCORES))],
                    ins=[wsh_gu0.opt()], outs=[wfull_gu0.opt()]),
                    waits=at("dq_b0_0") + at("dq_b0_1"), inc="CC")
                assert "dq_b0_0" in mark and "dq_b0_1" in mark
            wdv = dma("sync", db_sb[:], dvec.broadcast_to([128, D]), name="LDC",
                      waits=(("SY0", C["SY0"]), ("SY1", C["SY1"])))
            snap("db", "LDC")

            for c in range(8):
                par = c % 2
                lts = f"LT{c % 2}"
                w_x = dma("sync", xt3[:, par * D:(par + 1) * D],
                          x_tok[c * 128:(c + 1) * 128, :], waits=at(f"t3{c - 2}"),
                          name=lts)

                def load_yfm(e, c=c, par=par):
                    pid = e.partition_id()
                    srcv = cc2_out.rearrange("(a p) t -> p a t", p=128)
                    sl = srcv[0:128, 0:32, 0:128]
                    ap2 = bass.AP(sl.tensor, sl.offset + pid * TC + c * 128, sl.ap)
                    return e.dma_start(
                        out=yfm[:, par * D:(par + 1) * D]
                        .rearrange("p (a f) -> p a f", f=128), in_=ap2)
                w_y = emit("sync", load_yfm,
                           waits=((("CC", cc2_done),) if c == 0 else ()) +
                                 at(f"xm{c - 2}"),
                           inc=f"LY{c % 2}", inc_n=16)
                emit("vector", lambda e, c=c, par=par: e.tensor_scalar(
                    t3buf[:], xt3[:, par * D:(par + 1) * D],
                    stats[:, c:c + 1], stats[:, 8 + c:9 + c], OP.subtract, OP.mult),
                    waits=((lts, w_x),) + at("db"), inc="DV5")
                emit("vector", lambda e: e.tensor_tensor(
                    t3buf[:], t3buf[:], db_sb[:], OP.mult))
                emit("vector", lambda e, c=c, par=par: e.tensor_tensor(
                    t3buf[:], t3buf[:], xt3[:, par * D:(par + 1) * D], OP.add),
                    inc="DV5")
                snap(f"t3{c}", "DV5")
                for b in range(8):
                    mb = b % 2
                    tpm = None
                    for s in range(4):
                        tpm = emit("tensor", lambda e, b=b, s=s, par=par, mb=mb:
                                   e.transpose(
                            qm[mb][:, s * 128:(s + 1) * 128],
                            yfm[:, par * D + (4 * b + s) * 128:
                                par * D + (4 * b + s + 1) * 128], eyeb[:]),
                            waits=(((f"LY{c % 2}", w_y),) if (b == 0 and s == 0) else ()) +
                                  ((("DV4", C["DV4"]),) if s == 0 else ()),
                            inc="PE4" if s == 3 else None)
                    emit("vector", lambda e, b=b, mb=mb, par=par: e.tensor_tensor(
                        xmid[:, par * D + b * 512:par * D + (b + 1) * 512],
                        t3buf[:, b * 512:(b + 1) * 512], qm[mb][:], OP.add),
                        waits=(("PE4", tpm),) +
                              (at(f"xms{c - 2}") if b == 0 else ()), inc="DV4")
                snap(f"xm{c}", "DV4")
                dma("gpsimd", xmid_hbm[c * 128:(c + 1) * 128, :],
                    xmid[:, par * D:(par + 1) * D], waits=(("DV4", C["DV4"]),),
                    name=f"SXM{c % 2}")
                snap(f"xms{c}", f"SXM{c % 2}")
                if stop_after == "mamba":
                    dma("gpsimd", x_out[c * 128:(c + 1) * 128, :],
                        xmid[:, par * D:(par + 1) * D], name=f"SXM{c % 2}")
                bn3 = None
                for b in range(8):
                    bn3 = emit("vector", lambda e, b=b, par=par: e.bn_stats(
                        bnbuf3[:, 6 * b:6 * b + 6],
                        xmid[:, par * D + 512 * b:par * D + 512 * (b + 1)]),
                        inc="DV2" if b == 7 else None)
                ag3 = emit("vector", lambda e, c=c: e.bn_aggr(
                    mv3[:, 2 * c:2 * c + 2], bnbuf3[:]),
                    waits=(("DV2", bn3),), inc="DV2")
                m3 = emit("vector", lambda e, c=c: e.tensor_copy(
                    st3[:, c:c + 1], mv3[:, 2 * c:2 * c + 1]),
                    waits=(("DV2", ag3),), inc="DV5")
                emit("scalar", lambda e, c=c: e.activation(
                    mv3[:, 2 * c + 1:2 * c + 2], mv3[:, 2 * c + 1:2 * c + 2],
                    AF.Ln, bias=cst[:, 0:1]),
                    waits=(("DV5", m3),))
                emit("scalar", lambda e, c=c: e.activation(
                    st3[:, 8 + c:9 + c], mv3[:, 2 * c + 1:2 * c + 2],
                    AF.Exp, scale=cst[:, 1:2]),
                    inc="AC")
                if full:
                    emit_dq(5, mdq)
            snap("merge_done0", "SXM0")
            snap("merge_done1", "SXM1")
            snap("merge_ac", "AC")
            snap("merge_dv4", "DV4")

        if full:
            with ExitStack() as _ctx:
                xln2T = _ctx.enter_context(nc.sbuf_tensor("xln2T", [128, ND * SC], BF16))
                x3 = _ctx.enter_context(nc.sbuf_tensor("x3", [128, NJ * SC], BF16))
                wstr = _ctx.enter_context(nc.sbuf_tensor("wstr", [128, 2 * NJ * 128], BF16))
                lora_a = _ctx.enter_context(nc.sbuf_tensor("lora_a", [128, (2 * ND + NJ) * 16], BF16))
                lora_b = _ctx.enter_context(nc.sbuf_tensor("lora_b", [96, DFF], BF16))
                uvec = _ctx.enter_context(nc.sbuf_tensor("uvec", [96, 512], BF16))
                xshare = _ctx.enter_context(nc.sbuf_tensor("xshare", [128, 3072], F32))
                xshb = xshare[:].bitcast(BF16)
                xmr = xshare[:, 2048:3072]
                xln2b = lambda: xshb[:, 0:4096]
                r1_sb = xshb[:, 0:512]
                xo_sb = xshare[:, 0:1024]
                xma = xshare[:, 1024:2048]
                xob = xshare[:, 2048:3072]
                q1a = _ctx.enter_context(nc.psum_tensor("q1a", [128, 512], F32))
                q1b = _ctx.enter_context(nc.psum_tensor("q1b", [128, 512], F32))
                q2a = _ctx.enter_context(nc.psum_tensor("q2a", [128, 512], F32))
                q2b = _ctx.enter_context(nc.psum_tensor("q2b", [128, 512], F32))
                qu = _ctx.enter_context(nc.psum_tensor("qu", [96, 512], F32))
                qt0 = _ctx.enter_context(nc.psum_tensor("qt0", [128, 512], F32))
                qt1 = _ctx.enter_context(nc.psum_tensor("qt1", [128, 512], F32))
                qtb = _ctx.enter_context(nc.psum_tensor("qtb", [128, 512], BF16))
                q1 = (q1a, q1b)
                q2 = (q2a, q2b)
                QT = (qt0, qt1, q2a, q2b)

                wb = wstr[:].bitcast(F32)
                wl = dma("sync", wb[:, 0:ND * 16].rearrange("p (c f) -> p c f", f=16),
                         gla.rearrange("(c p) f -> p c f", p=128),
                         waits=at("merge_done0") + at("merge_done1"), name="LLA")
                la1 = emit("vector", lambda e, wb=wb: e.tensor_copy(
                    lora_a[:, 0:ND * 16], wb[:, 0:ND * 16]),
                    waits=(("LLA", wl),), inc="DV5")
                wl = dma("sync", wb[:, 0:ND * 16].rearrange("p (c f) -> p c f", f=16),
                         ula.rearrange("(c p) f -> p c f", p=128),
                         waits=(("DV5", la1),), name="LLA")
                la2 = emit("vector", lambda e, wb=wb: e.tensor_copy(
                    lora_a[:, ND * 16:2 * ND * 16], wb[:, 0:ND * 16]),
                    waits=(("LLA", wl),), inc="DV5")
                wl = dma("sync", wb[:, 0:NJ * 16].rearrange("p (c f) -> p c f", f=16),
                         dla.rearrange("(c p) f -> p c f", p=128),
                         waits=(("DV5", la2),), name="LLA")
                la3 = emit("vector", lambda e, wb=wb: e.tensor_copy(
                    lora_a[:, 2 * ND * 16:(2 * ND + NJ) * 16], wb[:, 0:NJ * 16]),
                    waits=(("LLA", wl),), inc="DV5")
                wl = dma("sync", wb[0:16, 0:DFF], glb, waits=(("DV5", la3),), name="LLA")
                lb1 = emit("vector", lambda e, wb=wb: e.tensor_copy(
                    lora_b[0:16, :], wb[0:16, 0:DFF]), waits=(("LLA", wl),), inc="DV5")
                wl = dma("sync", wb[0:16, 0:DFF], ulb, waits=(("DV5", lb1),), name="LLA")
                lb2 = emit("vector", lambda e, wb=wb: e.tensor_copy(
                    lora_b[32:48, :], wb[0:16, 0:DFF]), waits=(("LLA", wl),), inc="DV5")
                wl = dma("sync", wb[0:16, 0:D], dlb, waits=(("DV5", lb2),), name="LLA")
                emit("vector", lambda e, wb=wb: e.tensor_copy(
                    lora_b[64:80, 0:D], wb[0:16, 0:D]),
                    waits=(("LLA", wl),), inc="DV5")
                snap("lora", "DV5")

                def GLA(dc):
                    return lora_a[:, dc * 16:(dc + 1) * 16]

                def ULA(dc):
                    return lora_a[:, (ND + dc) * 16:(ND + dc + 1) * 16]

                def DLA(ic):
                    return lora_a[:, (2 * ND + ic) * 16:(2 * ND + ic + 1) * 16]

                for hf in range(2):
                    lx = None
                    for cq in range(4):
                        c = hf * 4 + cq
                        for hh in range(4):
                            wx = dma("sync", xmr,
                                     xmid_hbm[c * 128:(c + 1) * 128,
                                              hh * 1024:(hh + 1) * 1024],
                                     waits=(at("merge_done0") + at("merge_done1") +
                                            at("xobst0") + at("xobst1") +
                                            at(f"ffn_lx{hf}:{cq - 1}"))
                                     if hh == 0 else
                                     ((("DV5", lx),) if lx else ()),
                                     name="LXM")
                            lx = emit("vector", lambda e, c=c, hh=hh: e.tensor_scalar(
                                xln2b()[:, hh * 1024:(hh + 1) * 1024], xmr,
                                st3[:, c:c + 1], st3[:, 8 + c:9 + c],
                                OP.subtract, OP.mult),
                                waits=(("LXM", wx),) + at("merge_ac") +
                                      (at(f"ffn_tp{hf}:{cq - 1}") if hh == 0 else ()) +
                                      (at(f"relu{NJ - 1}") if (hh == 0 and cq == 0)
                                       else ()),
                                inc="DV5")
                        snap(f"ffn_lx{hf}:{cq}", "DV5")
                        for dc in range(ND):
                            tpm = emit("tensor", lambda e, dc=dc: e.transpose(
                                qtb[:, 0:128], xln2b()[:, dc * 128:(dc + 1) * 128],
                                eyeb[:]),
                                waits=(("DV5", lx), ("DV4", C["DV4"])) +
                                      (at("merge_dv4") if (hf == 0 and cq == 0 and dc == 0)
                                       else ()),
                                inc="PE4")
                            emit("vector", lambda e, dc=dc, cq=cq: e.tensor_copy(
                                xln2T[:, dc * SC + cq * 128:dc * SC + cq * 128 + 128],
                                qtb[:, 0:128]),
                                waits=(("PE4", tpm),), inc="DV4")
                        snap(f"ffn_tp{hf}:{cq}", "DV4")
                    snap(f"xln2T{hf}", "DV4")
                    mmu = None
                    for dc in range(ND):
                        emit("tensor", lambda e, dc=dc: e.matmul(
                            qu[0:16, :], GLA(dc), xln2T[:, dc * SC:(dc + 1) * SC],
                            start=(dc == 0), stop=(dc == ND - 1)),
                            waits=(at(f"xln2T{hf}") + at("lora") + at(f"uev{hf - 1}"))
                            if dc == 0 else ())
                        mmu = emit("tensor", lambda e, dc=dc: e.matmul(
                            qu[32:48, :], ULA(dc), xln2T[:, dc * SC:(dc + 1) * SC],
                            start=(dc == 0), stop=(dc == ND - 1)),
                            inc="PE4" if dc == ND - 1 else None)
                    ue1 = emit("vector", lambda e: e.tensor_copy(
                        uvec[0:48, :], qu[0:48, :]),
                        waits=(("PE4", mmu),), inc="DV4")
                    snap(f"uev{hf}", "DV4")
                    vfull0 = wfull_gu0.rearrange("(a p) f -> a p f", p=128)
                    vfull1 = wfull_gu1.rearrange("(a p) f -> a p f", p=128)
                    j_order = ([(r, jc) for jc in range(4) for r in range(NCORES)
                                if r * NGU_PAD + jc < NJ] +
                               [(r, jc) for jc in range(4, NGU_PAD) for r in range(NCORES)
                                if r * NGU_PAD + jc < NJ])
                    for jj, (r_, jc_) in enumerate(j_order):
                        j = r_ * NGU_PAD + jc_
                        jp = jj % 2
                        wof = jp * NJ * 128
                        lws = f"LW{jj % 2}"
                        if jc_ < 4:
                            gt = vfull0[(r_ * 4 + jc_) * 64:(r_ * 4 + jc_) * 64 + 32]
                            ut = vfull0[(r_ * 4 + jc_) * 64 + 32:(r_ * 4 + jc_) * 64 + 64]
                        else:
                            gt = vfull1[(r_ * 7 + jc_ - 4) * 64:
                                        (r_ * 7 + jc_ - 4) * 64 + 32]
                            ut = vfull1[(r_ * 7 + jc_ - 4) * 64 + 32:
                                        (r_ * 7 + jc_ - 4) * 64 + 64]
                        dma("sync",
                            wstr[:, wof:wof + ND * 128]
                            .rearrange("p (c f) -> p c f", f=128),
                            gt.transpose([1, 0, 2]),
                            waits=at(f"relu{jj - 2}") +
                                  (at("dmmlast") if jj < 2 else ()) +
                                  (at("lora") if jj == 0 else ()) +
                                  ((("CC", cc_gu0),) if (hf == 0 and jj == 0) else ()) +
                                  ((("CC", mark["ccgu1"][1]),)
                                   if (hf == 0 and jj == 32) else ()),
                            name=lws)
                        wu_l = dma("sync",
                                   wstr[:, wof + ND * 128:wof + 2 * ND * 128]
                                   .rearrange("p (c f) -> p c f", f=128),
                                   ut.transpose([1, 0, 2]), name=lws)
                        emit("tensor", lambda e, j=j, jp=jp: e.matmul(
                            q1[jp][:], lora_b[0:16, j * 128:(j + 1) * 128],
                            uvec[0:16, :], start=True, stop=False),
                            waits=(((lws, wu_l), ("DV4", ue1)) +
                                   at(f"relu{jj - 2}") + at(f"x3m{jj - 2}") +
                                   (at("y3evlast") if jj == 0 else ())))
                        for ic in range(ND):
                            emit("tensor", lambda e, ic=ic, wof=wof, jp=jp: e.matmul(
                                q1[jp][:],
                                wstr[:, wof + ic * 128:wof + (ic + 1) * 128],
                                xln2T[:, ic * SC:(ic + 1) * SC],
                                start=False, stop=(ic == ND - 1)))
                        emit("tensor", lambda e, j=j, jp=jp: e.matmul(
                            q2[jp][:], lora_b[32:48, j * 128:(j + 1) * 128],
                            uvec[32:48, :], start=True, stop=False),
                            waits=(at("xoaddlast") if jj == 0 else ()))
                        gmm = None
                        for ic in range(ND):
                            gmm = emit("tensor", lambda e, ic=ic, wof=wof, jp=jp: e.matmul(
                                q2[jp][:],
                                wstr[:, wof + (ND + ic) * 128:wof + (ND + ic + 1) * 128],
                                xln2T[:, ic * SC:(ic + 1) * SC],
                                start=False, stop=(ic == ND - 1)),
                                inc="PE1" if ic == ND - 1 else None)
                        if jj == NJ - 1:
                            snap("gmmlast", "PE1")
                        rel = emit("scalar", lambda e, jp=jp: e.activation(
                            r1_sb, q1[jp][:], AF.Relu),
                            waits=(("PE1", gmm),) + at(f"x3m{jj - 1}") +
                                  (at(f"ffn_lx{hf}:3") if jj == 0 else ()), inc="AC")
                        snap(f"relu{jj}", "AC")
                        emit("vector", lambda e, j=j, jp=jp: e.tensor_tensor(
                            x3[:, j * SC:(j + 1) * SC], r1_sb, q2[jp][:], OP.mult),
                            waits=(("AC", rel),), inc="DV4")
                        snap(f"x3m{jj}", "DV4")
                        if hf == 0:
                            emit_dq(3, qtb)
                            if jj == 18:
                                snap("ccgu1", "CC")
                                emit("gpsimd", lambda e: e.collective_compute(
                                    "AllGather", OP.bypass,
                                    replica_groups=[list(range(NCORES))],
                                    ins=[wsh_gu1.opt()], outs=[wfull_gu1.opt()]),
                                    waits=at("dq_b1_0") + at("dq_b1_1"), inc="CC")
                                assert "dq_b1_0" in mark and "dq_b1_1" in mark
                                snap("ccgu1", "CC")
                            if jj == 48:
                                snap("ccgd", "CC")
                                emit("gpsimd", lambda e: e.collective_compute(
                                    "AllGather", OP.bypass,
                                    replica_groups=[list(range(NCORES))],
                                    ins=[wsh_d.opt()], outs=[wfull_d.opt()]),
                                    waits=at("dq_b2_0") + at("dq_b2_1"), inc="CC")
                                assert "dq_b2_0" in mark and "dq_b2_1" in mark
                                snap("ccgd", "CC")
                    snap(f"x3done{hf}", "DV4")
                    mmd = None
                    for ic in range(NJ):
                        mmd = emit("tensor", lambda e, ic=ic: e.matmul(
                            qu[64:80, :], DLA(ic), x3[:, ic * SC:(ic + 1) * SC],
                            start=(ic == 0), stop=(ic == NJ - 1)),
                            waits=(at(f"x3done{hf}") + at(f"udev{hf - 1}"))
                            if ic == 0 else (),
                            inc="PE4" if ic == NJ - 1 else None)
                    ud1 = emit("vector", lambda e: e.tensor_copy(
                        uvec[64:80, :], qu[64:80, :]),
                        waits=(("PE4", mmd),), inc="DV4")
                    snap(f"udev{hf}", "DV4")
                    vfd = wfull_d.rearrange("(a p) f -> a p f", p=128)
                    for dm in range(ND):
                        dp = dm % 2
                        dg = dm // 4
                        wof = dp * NJ * 128
                        lws = f"LW{dm % 2}"
                        dt_base = (dm // 4) * 4 * 86 + (dm % 4) * 86
                        dma("sync",
                            wstr[:, wof:wof + 43 * 128]
                            .rearrange("p (c f) -> p c f", f=128),
                            vfd[dt_base:dt_base + 43].transpose([1, 0, 2]),
                            waits=at(f"y3ev{dm - 2}") +
                                  (at("gmmlast") if dm < 2 else ()) +
                                  ((("CC", mark["ccgd"][1]),)
                                   if (hf == 0 and dm == 0) else ()), name=lws)
                        wd2_l = dma("sync",
                                    wstr[:, wof + 43 * 128:wof + 86 * 128]
                                    .rearrange("p (c f) -> p c f", f=128),
                                    vfd[dt_base + 43:dt_base + 86].transpose([1, 0, 2]),
                                    name=lws)
                        emit("tensor", lambda e, dm=dm, dp=dp: e.matmul(
                            q1[dp][:], lora_b[64:80, dm * 128:(dm + 1) * 128],
                            uvec[64:80, :], start=True, stop=False),
                            waits=((lws, wd2_l), ("DV4", ud1)) + at(f"y3ev{dm - 2}"))
                        dmm = None
                        for ic in range(NJ):
                            dmm = emit("tensor", lambda e, ic=ic, wof=wof, dp=dp: e.matmul(
                                q1[dp][:],
                                wstr[:, wof + ic * 128:wof + (ic + 1) * 128],
                                x3[:, ic * SC:(ic + 1) * SC],
                                start=False, stop=(ic == NJ - 1)),
                                inc="PE1" if ic == NJ - 1 else None)
                        if dm == ND - 1:
                            snap("dmmlast", "PE1")
                        y3e = emit("vector", lambda e, dp=dp: e.tensor_copy(
                            xo_sb[:, dp * 512:(dp + 1) * 512], q1[dp][:]),
                            waits=(("PE1", dmm),) + at(f"y3tp{dm - 2}") +
                                  ((("AC", C["AC"]),) if dm == 0 else ()), inc="DV4")
                        snap(f"y3ev{dm}", "DV4")
                        if dm == ND - 1:
                            snap("y3evlast", "DV4")
                        for s in range(4):
                            emit("tensor", lambda e, s=s, dp=dp, dm=dm: e.transpose(
                                QT[s][:, (dm % 4) * 128:(dm % 4) * 128 + 128],
                                xo_sb[:, dp * 512 + s * 128:dp * 512 + (s + 1) * 128],
                                eyef[:]),
                                waits=(("DV4", y3e),) +
                                      (at("xoaddlast") if (dm % 4 == 0 and s == 0)
                                       else ()) +
                                      (at(f"x3m{NJ - 1}") if (dm == 0 and s >= 2)
                                       else ()),
                                inc="PE4" if s == 3 else None)
                        snap(f"y3tp{dm}", "PE4")
                        if dm % 4 == 3:
                            tpg = mark[f"y3tp{dm}"][1]
                            for s in range(4):
                                c = hf * 4 + s
                                wxa = dma("sync", xma[:, (s % 2) * 512:(s % 2 + 1) * 512],
                                          xmid_hbm.rearrange("(a p) f -> a p f", p=128)
                                          [c, :, dg * 512:(dg + 1) * 512],
                                          waits=at(f"xmafree{s % 2}"),
                                          name=f"LMA{s % 2}")
                                xoa = emit("vector",
                                           lambda e, s=s, dg=dg: e.tensor_tensor(
                                    xob[:, (s % 2) * 512:(s % 2 + 1) * 512],
                                    xma[:, (s % 2) * 512:(s % 2 + 1) * 512],
                                    QT[s][:], OP.add),
                                    waits=(("PE4", tpg), (f"LMA{s % 2}", wxa)) +
                                          at(f"xobst{s % 2}"),
                                    inc="DV4")
                                snap(f"xmafree{s % 2}", "DV4")
                                xst = dma("gpsimd",
                                          x_out[c * 128:(c + 1) * 128,
                                                dg * 512:(dg + 1) * 512],
                                          xob[:, (s % 2) * 512:(s % 2 + 1) * 512],
                                          waits=(("DV4", xoa),), name=f"SXO{s % 2}")
                                snap(f"xobst{s % 2}", f"SXO{s % 2}")
                            snap("xoaddlast", "DV4")

        store_names = ["SPK", "SY0", "SY1", "SXM0", "SXM1", "SXO0", "SXO1"]
        fin = [(dma_sems[n], C[n]) for n in store_names if C[n] > 0]

        def final_wait(e, fin=fin):
            for s, v in fin:
                e.wait_ge(s, v)
        emit("gpsimd", final_wait)

        @blk.sync
        def _(e):
            for fn in P.queues["sync"]:
                fn(e)

        @blk.gpsimd
        def _(e):
            for fn in P.queues["gpsimd"]:
                fn(e)

        @blk.vector
        def _(e):
            for fn in P.queues["vector"]:
                fn(e)

        @blk.scalar
        def _(e):
            for fn in P.queues["scalar"]:
                fn(e)

        @blk.tensor
        def _(e):
            for fn in P.queues["tensor"]:
                fn(e)

    return nc


def _fwd_inputs(k, x2, A_log, Dv, x_proj_w, dt_proj_w, dt_proj_b, consts,
                wtiles=None):
    a_sl = np.ascontiguousarray(A_log[k * DSL:(k + 1) * DSL]).reshape(-1)
    a_pre = np.ascontiguousarray(a_sl.reshape(NDN, 128).T)
    d = dict(
        x_tok=np.ascontiguousarray(x2[k * TC:(k + 1) * TC]),
        x_ds=np.ascontiguousarray(x2[:, k * DSL:(k + 1) * DSL]),
        xpw=x_proj_w,
        dtw=np.ascontiguousarray(dt_proj_w[k * DSL:(k + 1) * DSL]),
        dtb=np.ascontiguousarray(dt_proj_b[k * DSL:(k + 1) * DSL]).reshape(DSL, 1),
        a_pre=a_pre,
        dvec=Dv.reshape(1, D),
    )
    d.update(consts)
    if wtiles is not None:
        d.update(wtiles)
    return d


def _consts():
    ev = np.zeros((16, 128, 128), dtype=BF16NP)
    fv = np.zeros((16, 128, 128), dtype=BF16NP)
    for v in range(16):
        for p in range(128):
            r = 8 * v + p // 16
            ev[v, r, p] = 1
            fv[v, p, r] = 1
    g = np.zeros((16, 128), dtype=BF16NP)
    for p in range(128):
        g[p % 16, p] = 1
    return dict(
        e_cat=ev, f_cat=fv, g16=g,
        eye_bf=np.eye(128, dtype=BF16NP),
        eye_f32=np.eye(128, dtype=np.float32),
    )


def kernel(**inp):
    x2 = np.ascontiguousarray(np.asarray(inp["L0_attn_x"])[0])
    consts = _consts()
    R = NGU_PAD * 128
    gcod = _pad_rows(np.asarray(inp["L0_ffn_g_codes"]), R * NCORES)
    gam = _pad_rows(np.asarray(inp["L0_ffn_g_absmax"]), R * NCORES)
    ucod = _pad_rows(np.asarray(inp["L0_ffn_u_codes"]), R * NCORES)
    uam = _pad_rows(np.asarray(inp["L0_ffn_u_absmax"]), R * NCORES)
    dcodf = np.asarray(inp["L0_ffn_d_codes"])
    damf = np.asarray(inp["L0_ffn_d_absmax"])

    nc = build_forward("full")
    in_maps = []
    for k in range(NCORES):
        m = _fwd_inputs(k, x2,
                        np.asarray(inp["L0_attn_A_log"]),
                        np.asarray(inp["L0_attn_D"]),
                        np.asarray(inp["L0_attn_x_proj_w"]),
                        np.asarray(inp["L0_attn_dt_proj_w"]),
                        np.asarray(inp["L0_attn_dt_proj_b"]),
                        consts)
        m.update(dict(
            gc=np.ascontiguousarray(gcod[k * R:(k + 1) * R]).astype(np.int8),
            ga=np.ascontiguousarray(gam[k * R:(k + 1) * R]),
            uc=np.ascontiguousarray(ucod[k * R:(k + 1) * R]).astype(np.int8),
            ua=np.ascontiguousarray(uam[k * R:(k + 1) * R]),
            dcod=np.ascontiguousarray(dcodf[k * 512:(k + 1) * 512]).astype(np.int8),
            dam=np.ascontiguousarray(damf[k * 512:(k + 1) * 512]),
            gla=np.asarray(inp["L0_ffn_g_lora_a"]),
            glb=np.asarray(inp["L0_ffn_g_lora_b"]),
            ula=np.asarray(inp["L0_ffn_u_lora_a"]),
            ulb=np.asarray(inp["L0_ffn_u_lora_b"]),
            dla=np.asarray(inp["L0_ffn_d_lora_a"]),
            dlb=np.asarray(inp["L0_ffn_d_lora_b"]),
        ))
        in_maps.append(m)
    res = run_bass_kernel_spmd(nc, in_maps, core_ids=list(range(NCORES)))
    out = np.concatenate([res.results[k]["x_out"] for k in range(NCORES)], axis=0)
    return out[None]


NGU_PAD = 11


def build_dequant():
    nc = bass.Bass("TRN2", target_bir_lowering=False, debug=False)
    gc = nc.dram_tensor("gc", [NGU_PAD * 128, D], I32, kind="ExternalInput").ap()
    ga = nc.dram_tensor("ga", [NGU_PAD * 128, D // 64], F32, kind="ExternalInput").ap()
    uc = nc.dram_tensor("uc", [NGU_PAD * 128, D], I32, kind="ExternalInput").ap()
    ua = nc.dram_tensor("ua", [NGU_PAD * 128, D // 64], F32, kind="ExternalInput").ap()
    dcod = nc.dram_tensor("dcod", [4 * 128, DFF], I32, kind="ExternalInput").ap()
    dam = nc.dram_tensor("dam", [4 * 128, DFF // 64], F32, kind="ExternalInput").ap()
    eye_bf = nc.dram_tensor("eye_bf", [128, 128], BF16, kind="ExternalInput").ap()

    og = nc.dram_tensor("og", [NGU_PAD, ND, 128, 128], BF16, kind="ExternalOutput").ap()
    ou = nc.dram_tensor("ou", [NGU_PAD, ND, 128, 128], BF16, kind="ExternalOutput").ap()
    od = nc.dram_tensor("od", [4, NJ, 128, 128], BF16, kind="ExternalOutput").ap()

    with ExitStack() as _ctx:
        eyeb = _ctx.enter_context(nc.sbuf_tensor("eyeb", [128, 128], BF16))
        cod = _ctx.enter_context(nc.sbuf_tensor("cod", [128, 2 * 512], I32))
        amx = _ctx.enter_context(nc.sbuf_tensor("amx", [128, 2 * 8], F32))
        accs = _ctx.enter_context(nc.sbuf_tensor("accs", [128, 512], BF16))
        tmp = _ctx.enter_context(nc.sbuf_tensor("tmp", [128, 512], BF16))
        wq = _ctx.enter_context(nc.sbuf_tensor("wq", [128, 2 * 512], BF16))
        obuf = _ctx.enter_context(nc.sbuf_tensor("obuf", [128, 2 * 512], BF16))
        ptr0 = _ctx.enter_context(nc.psum_tensor("ptr0", [128, 512], BF16))
        ptr1 = _ctx.enter_context(nc.psum_tensor("ptr1", [128, 512], BF16))
        sLD = _ctx.enter_context(nc.semaphore("sLD"))
        sLQ1 = _ctx.enter_context(nc.semaphore("sLQ1"))
        sST = _ctx.enter_context(nc.semaphore("sST"))
        sSQ1 = _ctx.enter_context(nc.semaphore("sSQ1"))
        sDV = _ctx.enter_context(nc.semaphore("sDV"))
        sPE = _ctx.enter_context(nc.semaphore("sPE"))
        blk = _ctx.enter_context(nc.Block())
        P = Prog(dict(LQ0=sLD, LQ1=sLQ1, SQ0=sST, SQ1=sSQ1, DV=sDV, PE=sPE))
        emit, dma, snap, at, C = P.emit, P.dma, P.snap, P.at, P.C
        ptr = (ptr0, ptr1)
        w_eye = dma("sync", eyeb[:], eye_bf, name="LQ0")
        git = [0]

        def one_matrix(codes, am, out_t, nrow, icols):
            nib = (icols + 511) // 512
            for jc in range(nrow):
                for ib in range(nib):
                    it = git[0]
                    git[0] += 1
                    par = it % 2
                    iw = min(512, icols - ib * 512)
                    nt = iw // 128
                    nb = iw // 64
                    lqs = f"LQ{it % 2}"
                    dma("sync", cod[:, par * 512:par * 512 + iw],
                        codes[jc * 128:(jc + 1) * 128, ib * 512:ib * 512 + iw],
                        waits=at(f"dqlut{it - 2}"), name=lqs)
                    w_a = dma("sync", amx[:, par * 8:par * 8 + nb],
                              am[jc * 128:(jc + 1) * 128, ib * 8:ib * 8 + nb],
                              name=lqs)
                    csl = cod[:, par * 512:par * 512 + iw]
                    emit("vector", lambda e, csl=csl, iw=iw: e.tensor_scalar(
                        accs[:, 0:iw], csl, 0.0, float(NF4[0]),
                        OP.is_equal, OP.mult),
                        waits=((lqs, w_a),) + at(f"dqw{it - 2}"))
                    for k in range(1, 16):
                        emit("vector", lambda e, csl=csl, iw=iw, k=k: e.tensor_scalar(
                            tmp[:, 0:iw], csl, float(k), float(NF4[k]),
                            OP.is_equal, OP.mult))
                        emit("vector", lambda e, iw=iw: e.tensor_add(
                            accs[:, 0:iw], accs[:, 0:iw], tmp[:, 0:iw]))
                    lut = emit("vector", lambda e, par=par, iw=iw, nb=nb: e.tensor_tensor(
                        wq[:, par * 512:par * 512 + iw]
                        .rearrange("p (b z) -> p b z", z=64),
                        accs[:, 0:iw].rearrange("p (b z) -> p b z", z=64),
                        amx[:, par * 8:par * 8 + nb].unsqueeze(2)
                        .broadcast_to([128, nb, 64]),
                        OP.mult), inc="DV")
                    snap(f"dqlut{it}", "DV")
                    tpm = None
                    for s in range(nt):
                        tpm = emit("tensor", lambda e, s=s, par=par, tb=par: e.transpose(
                            ptr[tb][:, s * 128:(s + 1) * 128],
                            wq[:, par * 512 + s * 128:par * 512 + (s + 1) * 128],
                            eyeb[:]),
                            waits=((("DV", lut),) + at(f"dqev{it - 2}") +
                                   ((("LQ0", w_eye),) if it == 0 else ()))
                            if s == 0 else (),
                            inc="PE" if s == nt - 1 else None)
                    ev = emit("vector", lambda e, par=par, iw=iw, tb=par: e.tensor_copy(
                        obuf[:, par * 512:par * 512 + iw], ptr[tb][:, 0:iw]),
                        waits=(("PE", tpm),) + at(f"dqst{it - 2}"), inc="DV")
                    snap(f"dqev{it}", "DV")
                    snap(f"dqw{it}", "DV")
                    dma("gpsimd", out_t[jc, ib * 4:ib * 4 + nt].transpose([1, 0, 2]),
                        obuf[:, par * 512:par * 512 + iw]
                        .rearrange("p (c f) -> p c f", f=128),
                        waits=(("DV", ev),), name=f"SQ{it % 2}")
                    snap(f"dqst{it}", f"SQ{it % 2}")

        one_matrix(gc, ga, og, NGU_PAD, D)
        one_matrix(uc, ua, ou, NGU_PAD, D)
        one_matrix(dcod, dam, od, 4, DFF)
        fin = [(sST, C["SQ0"]), (sSQ1, C["SQ1"])]

        def final_wait(e, fin=fin):
            for s, v in fin:
                e.wait_ge(s, v)
        emit("gpsimd", final_wait)

        @blk.sync
        def _(e):
            for fn in P.queues["sync"]:
                fn(e)

        @blk.gpsimd
        def _(e):
            for fn in P.queues["gpsimd"]:
                fn(e)

        @blk.vector
        def _(e):
            for fn in P.queues["vector"]:
                fn(e)

        @blk.scalar
        def _(e):
            for fn in P.queues["scalar"]:
                fn(e)

        @blk.tensor
        def _(e):
            for fn in P.queues["tensor"]:
                fn(e)
    return nc


def _pad_rows(a, rows):
    if a.shape[0] == rows:
        return a
    pad = np.zeros((rows - a.shape[0],) + a.shape[1:], dtype=a.dtype)
    return np.concatenate([a, pad], axis=0)


def _run_dequant(inp):
    gcod = _pad_rows(np.asarray(inp["L0_ffn_g_codes"]), NGU_PAD * 128 * NCORES)
    gam = _pad_rows(np.asarray(inp["L0_ffn_g_absmax"]), NGU_PAD * 128 * NCORES)
    ucod = _pad_rows(np.asarray(inp["L0_ffn_u_codes"]), NGU_PAD * 128 * NCORES)
    uam = _pad_rows(np.asarray(inp["L0_ffn_u_absmax"]), NGU_PAD * 128 * NCORES)
    dcod = np.asarray(inp["L0_ffn_d_codes"])
    dam = np.asarray(inp["L0_ffn_d_absmax"])
    eye = np.eye(128, dtype=BF16NP)
    R = NGU_PAD * 128
    nc = build_dequant()
    in_maps = []
    for k in range(NCORES):
        in_maps.append(dict(
            gc=np.ascontiguousarray(gcod[k * R:(k + 1) * R]).astype(np.int8),
            ga=np.ascontiguousarray(gam[k * R:(k + 1) * R]),
            uc=np.ascontiguousarray(ucod[k * R:(k + 1) * R]).astype(np.int8),
            ua=np.ascontiguousarray(uam[k * R:(k + 1) * R]),
            dcod=np.ascontiguousarray(dcod[k * 512:(k + 1) * 512]),
            dam=np.ascontiguousarray(dam[k * 512:(k + 1) * 512]),
            eye_bf=eye,
        ))
    res = run_bass_kernel_spmd(nc, in_maps, core_ids=list(range(NCORES)))
    wg = np.concatenate([res.results[k]["og"] for k in range(NCORES)], axis=0)[:NJ]
    wu = np.concatenate([res.results[k]["ou"] for k in range(NCORES)], axis=0)[:NJ]
    wd = np.concatenate([res.results[k]["od"] for k in range(NCORES)], axis=0)
    return wg, wu, wd


# version 08
# speedup vs baseline: 1.0536x; 1.0456x over previous
import numpy as np
import ml_dtypes
from contextlib import ExitStack
import concourse.bass as bass
import concourse.mybir as mybir
from concourse.bass_utils import run_bass_kernel_spmd

F32 = mybir.dt.float32
BF16 = mybir.dt.bfloat16
I32 = mybir.dt.int32
I8 = mybir.dt.int8
AF = mybir.ActivationFunctionType
OP = mybir.AluOpType
BF16NP = ml_dtypes.bfloat16

NCORES = 8
T = 8192
TC = T // NCORES
D = 4096
DSL = D // NCORES
NST = 16
DTR = 128
DFF = 11008
NJ = DFF // 128
ND = D // 128
SC = 512
NSC = T // SC
NDN = DSL * NST // 128
EPS = 1e-5

NF4 = [-1.0, -0.6961928009986877, -0.5250730514526367, -0.39491748809814453,
       -0.28444138169288635, -0.18477343022823334, -0.09105003625154495, 0.0,
       0.07958029955625534, 0.16093020141124725, 0.2461123019456863,
       0.33791524171829224, 0.44070982933044434, 0.5626170039176941,
       0.7229568362236023, 1.0]

ENGS = ("sync", "gpsimd", "vector", "scalar", "tensor")


class Prog:

    def __init__(self, sems):
        self.queues = {e: [] for e in ENGS}
        self.sems = sems
        self.C = {k: 0 for k in sems}
        self.mark = {}

    def emit(self, eng, fn, waits=(), inc=None, inc_n=1):
        if inc is not None:
            self.C[inc] += inc_n
        wl = tuple((self.sems[n], v) for (n, v) in waits if v > 0)
        io = (self.sems[inc], inc_n) if inc is not None else None

        def thunk(e, fn=fn, wl=wl, io=io):
            for (s, v) in wl:
                e.wait_ge(s, v)
            r = fn(e)
            if io is not None:
                s, n = io
                if r is None:
                    e.sem_inc(s, n)
                else:
                    r.then_inc(s, n)
        self.queues[eng].append(thunk)
        return self.C[inc] if inc is not None else None

    def dma(self, eng, dst, src, waits=(), name=None):
        assert name is not None
        return self.emit(eng, lambda e, d=dst, s=src: e.dma_start(out=d, in_=s),
                         waits=waits, inc=name, inc_n=16)

    def snap(self, label, name):
        self.mark[label] = (name, self.C[name])

    def at(self, label):
        if label not in self.mark:
            return ()
        n, v = self.mark[label]
        return ((n, v),)


DEBUG_MAMBA = False


def build_forward(stop_after="full"):
    nc = bass.Bass("TRN2", target_bir_lowering=False, debug=False)

    x_tok = nc.dram_tensor("x_tok", [TC, D], F32, kind="ExternalInput").ap()
    x_ds = nc.dram_tensor("x_ds", [T, DSL], F32, kind="ExternalInput").ap()
    xpw = nc.dram_tensor("xpw", [DTR + 2 * NST, D], F32, kind="ExternalInput").ap()
    dtw = nc.dram_tensor("dtw", [DSL, DTR], F32, kind="ExternalInput").ap()
    dtb = nc.dram_tensor("dtb", [DSL, 1], F32, kind="ExternalInput").ap()
    a_pre = nc.dram_tensor("a_pre", [128, NDN], F32, kind="ExternalInput").ap()
    dvec = nc.dram_tensor("dvec", [1, D], F32, kind="ExternalInput").ap()
    e_cat = nc.dram_tensor("e_cat", [16, 128, 128], BF16, kind="ExternalInput").ap()
    f_cat = nc.dram_tensor("f_cat", [16, 128, 128], BF16, kind="ExternalInput").ap()
    g16 = nc.dram_tensor("g16", [16, 128], BF16, kind="ExternalInput").ap()
    eye_bf = nc.dram_tensor("eye_bf", [128, 128], BF16, kind="ExternalInput").ap()
    eye_f32i = nc.dram_tensor("eye_f32", [128, 128], F32, kind="ExternalInput").ap()
    full = stop_after == "full"
    if full:
        gc = nc.dram_tensor("gc", [NGU_PAD * 128, D], I8, kind="ExternalInput").ap()
        ga = nc.dram_tensor("ga", [NGU_PAD * 128, D // 64], F32, kind="ExternalInput").ap()
        uc = nc.dram_tensor("uc", [NGU_PAD * 128, D], I8, kind="ExternalInput").ap()
        ua = nc.dram_tensor("ua", [NGU_PAD * 128, D // 64], F32, kind="ExternalInput").ap()
        dcod = nc.dram_tensor("dcod", [4 * 128, DFF], I8, kind="ExternalInput").ap()
        dam = nc.dram_tensor("dam", [4 * 128, DFF // 64], F32, kind="ExternalInput").ap()
        gla = nc.dram_tensor("gla", [D, 16], F32, kind="ExternalInput").ap()
        glb = nc.dram_tensor("glb", [16, DFF], F32, kind="ExternalInput").ap()
        ula = nc.dram_tensor("ula", [D, 16], F32, kind="ExternalInput").ap()
        ulb = nc.dram_tensor("ulb", [16, DFF], F32, kind="ExternalInput").ap()
        dla = nc.dram_tensor("dla", [DFF, 16], F32, kind="ExternalInput").ap()
        dlb = nc.dram_tensor("dlb", [16, D], F32, kind="ExternalInput").ap()

    x_out = nc.dram_tensor("x_out", [TC, D], F32, kind="ExternalOutput").ap()

    if DEBUG_MAMBA:
        dbg_dtin = nc.dram_tensor("dbg_dtin", [128, T], F32, kind="ExternalOutput").ap()
        dbg_b = nc.dram_tensor("dbg_b", [16, T], F32, kind="ExternalOutput").ap()
        dbg_c = nc.dram_tensor("dbg_c", [16, T], F32, kind="ExternalOutput").ap()
        dbg_m = nc.dram_tensor("dbg_m", [128, 64], F32, kind="ExternalOutput").ap()
        dbg_r = nc.dram_tensor("dbg_r", [128, 64], F32, kind="ExternalOutput").ap()
        dbg_dt = nc.dram_tensor("dbg_dt", [128, 4 * SC], F32, kind="ExternalOutput").ap()
        dbg_carry = nc.dram_tensor("dbg_carry", [128, NDN], F32, kind="ExternalOutput").ap()
        dbg_da = nc.dram_tensor("dbg_da", [128, SC], F32, kind="ExternalOutput").ap()
        dbg_dbx = nc.dram_tensor("dbg_dbx", [128, SC], F32, kind="ExternalOutput").ap()
        dbg_u = nc.dram_tensor("dbg_u", [128, 4 * SC], F32, kind="ExternalOutput").ap()
        dbg_stats = nc.dram_tensor("dbg_stats", [128, 16], F32, kind="ExternalOutput").ap()
        dbg_stt = nc.dram_tensor("dbg_stt", [16, 128], F32, kind="ExternalOutput").ap()
        dbg_xp = nc.dram_tensor("dbg_xp", [128, 8 * 128], F32, kind="ExternalOutput").ap()
        dbg_yloc = nc.dram_tensor("dbg_yloc", [DSL, T], F32, kind="ExternalOutput").ap()
        dbg_h = nc.dram_tensor("dbg_h", [128, SC], F32, kind="ExternalOutput").ap()
        dbg_crep = nc.dram_tensor("dbg_crep", [128, SC], F32, kind="ExternalOutput").ap()
        dbg_cin = nc.dram_tensor("dbg_cin", [128, 1], F32, kind="ExternalOutput").ap()
        dbg_row = nc.dram_tensor("dbg_row", [128, NDN * SC], F32, kind="ExternalOutput").ap()
        dbg_row2 = nc.dram_tensor("dbg_row2", [128, NDN * SC], F32, kind="ExternalOutput").ap()
        dbg_u13 = nc.dram_tensor("dbg_u13", [128, 4 * SC], F32, kind="ExternalOutput").ap()
        dbg_brep = nc.dram_tensor("dbg_brep", [128, SC], F32, kind="ExternalOutput").ap()
        dbg_bsnap = nc.dram_tensor("dbg_bsnap", [8, 16, T], F32, kind="ExternalOutput").ap()
        dbg_crep2 = nc.dram_tensor("dbg_crep2", [128, SC], F32, kind="ExternalOutput").ap()
    if full:
        wsh_gu0 = nc.dram_tensor("wsh_gu0", [4 * 64 * 128, 128], BF16, kind="Internal").ap()
        wsh_gu1 = nc.dram_tensor("wsh_gu1", [7 * 64 * 128, 128], BF16, kind="Internal").ap()
        wsh_d = nc.dram_tensor("wsh_d", [4 * 86 * 128, 128], BF16, kind="Internal").ap()
        wfull_gu0 = nc.dram_tensor("wfull_gu0", [NCORES * 4 * 64 * 128, 128], BF16,
                                   kind="Internal").ap()
        wfull_gu1 = nc.dram_tensor("wfull_gu1", [NCORES * 7 * 64 * 128, 128], BF16,
                                   kind="Internal").ap()
        wfull_d = nc.dram_tensor("wfull_d", [NCORES * 4 * 86 * 128, 128], BF16,
                                 kind="Internal").ap()
    cc1_in = nc.dram_tensor("cc1_in", [162, TC], F32, kind="Internal").ap()
    cc1_out = nc.dram_tensor("cc1_out", [162 * NCORES, TC], F32, kind="Internal").ap()
    cc2_in = nc.dram_tensor("cc2_in", [DSL, T], BF16, kind="Internal").ap()
    cc2_out = nc.dram_tensor("cc2_out", [D, T], BF16, kind="Internal").ap()
    xmid_hbm = nc.dram_tensor("xmid_hbm", [TC, D], F32, kind="Internal").ap()

    with ExitStack() as _ctx:
        g_sb = _ctx.enter_context(nc.sbuf_tensor("g_sb", [16, 128], BF16))
        eyeb = _ctx.enter_context(nc.sbuf_tensor("eyeb", [128, 128], BF16))
        eyef = _ctx.enter_context(nc.sbuf_tensor("eyef", [128, 128], F32))
        stats = _ctx.enter_context(nc.sbuf_tensor("stats", [128, 16], F32))
        st3 = _ctx.enter_context(nc.sbuf_tensor("st3", [128, 16], F32))
        cst = _ctx.enter_context(nc.sbuf_tensor("cst", [128, 2], F32))
        if full:
            dq_cod0 = _ctx.enter_context(nc.sbuf_tensor("dq_cod0", [128, 512], I8))
            dq_cod1 = _ctx.enter_context(nc.sbuf_tensor("dq_cod1", [128, 512], I8))
            dq_amx0 = _ctx.enter_context(nc.sbuf_tensor("dq_amx0", [128, 8], F32))
            dq_amx1 = _ctx.enter_context(nc.sbuf_tensor("dq_amx1", [128, 8], F32))
            dq_acc = _ctx.enter_context(nc.sbuf_tensor("dq_acc", [128, 512], BF16))
            dq_tmp = _ctx.enter_context(nc.sbuf_tensor("dq_tmp", [128, 512], BF16))
            dq_wq = _ctx.enter_context(nc.sbuf_tensor("dq_wq", [128, 512], BF16))
            dq_ob = _ctx.enter_context(nc.sbuf_tensor("dq_ob", [128, 512], BF16))
        DMA_SEMS = ["LDC", "LX0", "LX1", "LP", "LS0", "LS1", "LT0", "LT1",
                    "LY0", "LY1", "LLA", "LW0", "LW1", "LXM", "LMA0", "LMA1",
                    "SY0", "SY1", "SXM0", "SXM1", "SXO0", "SXO1", "SPK",
                    "LR0", "LR1", "LQ0", "LQ1", "SQ0", "SQ1"]
        dma_sems = {n: _ctx.enter_context(nc.semaphore("s" + n)) for n in DMA_SEMS}
        sCC = _ctx.enter_context(nc.semaphore("sCC"))
        sPE1 = _ctx.enter_context(nc.semaphore("sPE1"))
        sPE3 = _ctx.enter_context(nc.semaphore("sPE3"))
        sPE4 = _ctx.enter_context(nc.semaphore("sPE4"))
        sAC = _ctx.enter_context(nc.semaphore("sAC"))
        sACz = _ctx.enter_context(nc.semaphore("sACz"))
        sDV1 = _ctx.enter_context(nc.semaphore("sDV1"))
        sDV2 = _ctx.enter_context(nc.semaphore("sDV2"))
        sDV3 = _ctx.enter_context(nc.semaphore("sDV3"))
        sDV4 = _ctx.enter_context(nc.semaphore("sDV4"))
        sDV5 = _ctx.enter_context(nc.semaphore("sDV5"))
        blk = _ctx.enter_context(nc.Block())
        sGP = _ctx.enter_context(nc.semaphore("sGP"))
        sem_map = dict(CC=sCC, GP=sGP, PE1=sPE1, PE3=sPE3, PE4=sPE4,
                       AC=sAC, ACz=sACz, DV1=sDV1, DV2=sDV2, DV3=sDV3,
                       DV4=sDV4, DV5=sDV5)
        sem_map.update(dma_sems)
        P = Prog(sem_map)
        emit, dma, snap, at, C, mark = P.emit, P.dma, P.snap, P.at, P.C, P.mark

        emit("vector", lambda e: e.memset(cst[:, 0:1], EPS))
        emit("vector", lambda e: e.memset(cst[:, 1:2], -0.5), inc="DV5")
        snap("cst", "DV5")

        dq_state = {"i": 0}
        dq_work = []
        if full:
            v_gu0 = wsh_gu0.rearrange("(a p) f -> a p f", p=128)
            v_gu1 = wsh_gu1.rearrange("(a p) f -> a p f", p=128)
            v_d = wsh_d.rearrange("(a p) f -> a p f", p=128)
            for jc in range(NGU_PAD):
                for mat, codes_t, am_t, uoff in ((0, gc, ga, 0), (1, uc, ua, 32)):
                    for ib in range(8):
                        dst = (v_gu0[(jc * 64 + uoff + ib * 4):
                                     (jc * 64 + uoff + ib * 4) + 4]
                               if jc < 4 else
                               v_gu1[((jc - 4) * 64 + uoff + ib * 4):
                                     ((jc - 4) * 64 + uoff + ib * 4) + 4])
                        dq_work.append((codes_t, am_t, jc, ib, 512, dst))
            for q in range(4):
                for ib in range(22):
                    iw = min(512, DFF - ib * 512)
                    dst = v_d[(q * 86 + ib * 4):(q * 86 + ib * 4) + iw // 128]
                    dq_work.append((dcod, dam, q, ib, iw, dst))

        def emit_dq(n, ptb, eng="vector"):
            for _ in range(n):
                it = dq_state["i"]
                if it >= len(dq_work):
                    return
                dq_state["i"] += 1
                codes_t, am_t, jc, ib, iw, dst = dq_work[it]
                nt = iw // 128
                nb = iw // 64
                lqs = f"LQ{it % 2}"
                sqs = f"SQ{it % 2}"
                lsem = "GP" if eng == "gpsimd" else "DV5"
                dq_cod = (dq_cod0, dq_cod1)[it % 2]
                dq_amx = (dq_amx0, dq_amx1)[it % 2]
                dma("sync", dq_cod[:, 0:iw],
                    codes_t[jc * 128:(jc + 1) * 128, ib * 512:ib * 512 + iw],
                    waits=at(f"dq_lut{it - 2}"), name=lqs)
                w_a = dma("sync", dq_amx[:, 0:nb],
                          am_t[jc * 128:(jc + 1) * 128, ib * 8:ib * 8 + nb],
                          name=lqs)
                csl = dq_cod[:, 0:iw]
                emit(eng, lambda e, csl=csl, iw=iw: e.tensor_scalar(
                    dq_acc[:, 0:iw], csl, 0.0, float(NF4[0]),
                    OP.is_equal, OP.mult),
                    waits=((lqs, w_a),))
                for k in range(1, 16):
                    emit(eng, lambda e, csl=csl, iw=iw, k=k: e.tensor_scalar(
                        dq_tmp[:, 0:iw], csl, float(k), float(NF4[k]),
                        OP.is_equal, OP.mult))
                    emit(eng, lambda e, iw=iw: e.tensor_add(
                        dq_acc[:, 0:iw], dq_acc[:, 0:iw], dq_tmp[:, 0:iw]))
                lut = emit(eng, lambda e, iw=iw, nb=nb, am=dq_amx: e.tensor_tensor(
                    dq_wq[:, 0:iw].rearrange("p (b z) -> p b z", z=64),
                    dq_acc[:, 0:iw].rearrange("p (b z) -> p b z", z=64),
                    am[:, 0:nb].unsqueeze(2)
                    .broadcast_to([128, nb, 64]), OP.mult),
                    waits=(("PE4", dq_state.get("tpm", 0)),), inc=lsem)
                snap(f"dq_lut{it}", lsem)
                tpm = None
                for s in range(nt):
                    tpm = emit("tensor", lambda e, s=s, ptb=ptb: e.transpose(
                        ptb[:, s * 128:(s + 1) * 128],
                        dq_wq[:, s * 128:(s + 1) * 128],
                        eyeb[:]),
                        waits=(((lsem, lut), ("DV4", C["DV4"])) if s == 0 else ()),
                        inc="PE4" if s == nt - 1 else None)
                dq_state["tpm"] = tpm
                ev = emit("vector", lambda e, iw=iw, ptb=ptb: e.tensor_copy(
                    dq_ob[:, 0:iw], ptb[:, 0:iw]),
                    waits=(("PE4", tpm),) + at(f"dq_st{it - 1}"), inc="DV4")
                dma("gpsimd", dst.transpose([1, 0, 2]),
                    dq_ob[:, 0:iw].rearrange("p (c f) -> p c f", f=128),
                    waits=(("DV4", ev),), name=sqs)
                snap(f"dq_st{it}", sqs)
                if dq_state["i"] == 64:
                    snap("dq_b0_0", "SQ0")
                    snap("dq_b0_1", "SQ1")
                if dq_state["i"] == 176:
                    snap("dq_b1_0", "SQ0")
                    snap("dq_b1_1", "SQ1")
                if dq_state["i"] == len(dq_work):
                    snap("dq_b2_0", "SQ0")
                    snap("dq_b2_1", "SQ1")

        with ExitStack() as _ctx:
            xpwT = _ctx.enter_context(nc.sbuf_tensor("xpwT", [128, 32 * 160], BF16))
            wstg = _ctx.enter_context(nc.sbuf_tensor("wstg", [128, 4096], F32))
            wstg2 = _ctx.enter_context(nc.sbuf_tensor("wstg2", [32, 4096], F32))
            wstgb = _ctx.enter_context(nc.sbuf_tensor("wstgb", [128, 4096], BF16))
            wstgb2 = _ctx.enter_context(nc.sbuf_tensor("wstgb2", [32, 4096], BF16))
            xt = _ctx.enter_context(nc.sbuf_tensor("xt", [128, 2 * D], F32))
            xlnb = _ctx.enter_context(nc.sbuf_tensor("xlnb", [128, 2 * D], BF16))
            xlnT1 = _ctx.enter_context(nc.sbuf_tensor("xlnT1", [128, D], BF16))
            bnbuf = _ctx.enter_context(nc.sbuf_tensor("bnbuf", [128, 48], F32))
            mv = _ctx.enter_context(nc.sbuf_tensor("mv", [128, 16], F32))
            xp_sb = _ctx.enter_context(nc.sbuf_tensor("xp_sb", [128, 8 * 128], F32))
            bxp_sb = _ctx.enter_context(nc.sbuf_tensor("bxp_sb", [16, 8 * 128], F32))
            cxp_sb = _ctx.enter_context(nc.sbuf_tensor("cxp_sb", [16, 8 * 128], F32))
            st_t = _ctx.enter_context(nc.sbuf_tensor("st_t", [16, 128], F32))
            p1t0 = _ctx.enter_context(nc.psum_tensor("p1t0", [128, 512], BF16))
            p1t1 = _ctx.enter_context(nc.psum_tensor("p1t1", [128, 512], BF16))
            pxd = _ctx.enter_context(nc.psum_tensor("pxd", [128, 128], F32))
            pxb = _ctx.enter_context(nc.psum_tensor("pxb", [16, 128], F32))
            pxc = _ctx.enter_context(nc.psum_tensor("pxc", [16, 128], F32))
            pst = _ctx.enter_context(nc.psum_tensor("pst", [16, 128], F32))
            p1t = (p1t0, p1t1)
            dma("sync", eyeb[:], eye_bf, name="LDC")
            dma("sync", eyef[:], eye_f32i, name="LDC")
            dma("sync", g_sb[:], g16, name="LDC")
            dma("sync", wstg[:], xpw[0:128, :], name="LDC")
            ldc_all = dma("sync", wstg2[:], xpw[128:160, :], name="LDC")
            emit("vector", lambda e: e.tensor_copy(wstgb[:], wstg[:]),
                 waits=(("LDC", ldc_all),), inc="DV5")
            c2 = emit("vector", lambda e: e.tensor_copy(wstgb2[:], wstg2[:]),
                      inc="DV5")
            for dc in range(32):
                emit("tensor", lambda e, dc=dc: e.transpose(
                    p1t0[:, 0:128], wstgb[:, dc * 128:(dc + 1) * 128], eyeb[:]),
                    waits=(("DV5", c2), ("DV4", C["DV4"])), inc="PE4")
                t2 = emit("tensor", lambda e, dc=dc: e.transpose(
                    p1t0[:, 128:160], wstgb2[:, dc * 128:(dc + 1) * 128],
                    eyeb[0:32, 0:32]),
                    inc="PE4")
                emit("vector", lambda e, dc=dc: e.tensor_copy(
                    xpwT[:, dc * 160:dc * 160 + 128], p1t0[:, 0:128]),
                    waits=(("PE4", t2),), inc="DV4")
                emit("vector", lambda e, dc=dc: e.tensor_copy(
                    xpwT[:, dc * 160 + 128:dc * 160 + 160], p1t0[:, 128:160]),
                    inc="DV4")

            for c in range(8):
                par = c % 2
                lxs = f"LX{c % 2}"
                w_x = dma("sync", xt[:, par * D:(par + 1) * D],
                          x_tok[c * 128:(c + 1) * 128, :],
                          waits=at(f"ph1_ln{c - 2}"), name=lxs)
                bnm = None
                for b in range(8):
                    bnm = emit("vector", lambda e, b=b, par=par: e.bn_stats(
                        bnbuf[:, 6 * b:6 * b + 6],
                        xt[:, par * D + 512 * b:par * D + 512 * (b + 1)]),
                        waits=((lxs, w_x),) if b == 0 else (),
                        inc="DV2" if b == 7 else None)
                ag = emit("vector", lambda e, c=c: e.bn_aggr(
                    mv[:, 2 * c:2 * c + 2], bnbuf[:]),
                    waits=(("DV2", bnm),), inc="DV2")
                m1 = emit("vector", lambda e, c=c: e.tensor_copy(
                    stats[:, c:c + 1], mv[:, 2 * c:2 * c + 1]),
                    waits=(("DV2", ag),), inc="DV5")
                emit("scalar", lambda e, c=c: e.activation(
                    mv[:, 2 * c + 1:2 * c + 2], mv[:, 2 * c + 1:2 * c + 2],
                    AF.Ln, bias=cst[:, 0:1]),
                    waits=(("DV5", m1),) + at("cst"))
                r1 = emit("scalar", lambda e, c=c: e.activation(
                    stats[:, 8 + c:9 + c], mv[:, 2 * c + 1:2 * c + 2],
                    AF.Exp, scale=cst[:, 1:2]),
                    inc="AC")
                lnm = emit("vector", lambda e, c=c, par=par: e.tensor_scalar(
                    xlnb[:, par * D:(par + 1) * D], xt[:, par * D:(par + 1) * D],
                    stats[:, c:c + 1], stats[:, 8 + c:9 + c], OP.subtract, OP.mult),
                    waits=(("AC", r1),) + at(f"ph1_tp{c - 2}"), inc="DV5")
                snap(f"ph1_ln{c}", "DV5")
                last_tp = None
                for b in range(8):
                    tb = b % 2
                    for s in range(4):
                        last_tp = emit("tensor", lambda e, b=b, s=s, par=par, tb=tb:
                                       e.transpose(
                            p1t[tb][:, s * 128:(s + 1) * 128],
                            xlnb[:, par * D + (4 * b + s) * 128:
                                 par * D + (4 * b + s + 1) * 128], eyeb[:]),
                            waits=((("DV5", lnm),) if (b == 0 and s == 0) else ()) +
                                  ((("DV4", C["DV4"]),) if s == 0 else ()),
                            inc="PE4" if s == 3 else None)
                    emit("vector", lambda e, b=b, tb=tb: e.tensor_copy(
                        xlnT1[:, b * 512:(b + 1) * 512], p1t[tb][:]),
                        waits=(("PE4", last_tp),), inc="DV4")
                snap(f"ph1_tp{c}", "PE4")
                ev = C["DV4"]
                mmx = None
                for dc in range(32):
                    emit("tensor", lambda e, dc=dc: e.matmul(
                        pxd[:], xpwT[:, dc * 160:dc * 160 + 128],
                        xlnT1[:, dc * 128:(dc + 1) * 128],
                        start=(dc == 0), stop=(dc == 31)),
                        waits=((("DV4", ev),) + at(f"ph1_xpe{c - 1}")) if dc == 0 else ())
                    emit("tensor", lambda e, dc=dc: e.matmul(
                        pxb[:], xpwT[:, dc * 160 + 128:dc * 160 + 144],
                        xlnT1[:, dc * 128:(dc + 1) * 128],
                        start=(dc == 0), stop=(dc == 31)))
                    mmx = emit("tensor", lambda e, dc=dc: e.matmul(
                        pxc[:], xpwT[:, dc * 160 + 144:dc * 160 + 160],
                        xlnT1[:, dc * 128:(dc + 1) * 128],
                        start=(dc == 0), stop=(dc == 31)),
                        inc="PE4" if dc == 31 else None)
                emit("vector", lambda e, c=c: e.tensor_copy(
                    xp_sb[:, c * 128:(c + 1) * 128], pxd[:]),
                    waits=(("PE4", mmx),), inc="DV4")
                emit("vector", lambda e, c=c: e.tensor_copy(
                    bxp_sb[:, c * 128:(c + 1) * 128], pxb[:]), inc="DV4")
                emit("vector", lambda e, c=c: e.tensor_copy(
                    cxp_sb[:, c * 128:(c + 1) * 128], pxc[:]), inc="DV4")
                snap(f"ph1_xpe{c}", "DV4")

            if DEBUG_MAMBA:
                dma("gpsimd", dbg_stats, stats[:],
                    waits=(("DV4", C["DV4"]), ("AC", C["AC"])), name="SPK")
                dma("gpsimd", dbg_xp, xp_sb[:], waits=(("DV4", C["DV4"]),), name="SPK")
            stt = emit("tensor", lambda e: e.transpose(pst[:], stats[:], eyef[:]),
                       waits=(("DV4", C["DV4"]), ("AC", C["AC"])), inc="PE4")
            stev = emit("vector", lambda e: e.tensor_copy(st_t[:], pst[:]),
                 waits=(("PE4", stt),), inc="DV4")
            if DEBUG_MAMBA:
                dma("gpsimd", dbg_stt, st_t[:], waits=(("DV4", stev),), name="SPK")
            dma("gpsimd", cc1_in[0:128, :], xp_sb[:], waits=(("DV4", C["DV4"]),), name="SPK")
            dma("gpsimd", cc1_in[128:144, :], bxp_sb[:], name="SPK")
            dma("gpsimd", cc1_in[144:160, :], cxp_sb[:], name="SPK")
            dma("gpsimd", cc1_in[160:161, :].rearrange("o (c f) -> (o c) f", f=128),
                st_t[0:8, :], name="SPK")
            dma("gpsimd", cc1_in[161:162, :].rearrange("o (c f) -> (o c) f", f=128),
                st_t[8:16, :], name="SPK")
            cc1_done = emit("gpsimd", lambda e: e.collective_compute(
                "AllGather", OP.bypass, replica_groups=[list(range(NCORES))],
                ins=[cc1_in.opt()], outs=[cc1_out.opt()]),
                waits=(("SPK", C["SPK"]),), inc="CC")

        with ExitStack() as _ctx:
            dtwT = _ctx.enter_context(nc.sbuf_tensor("dtwT", [128, 4 * 128], BF16))
            dtb_sb = _ctx.enter_context(nc.sbuf_tensor("dtb_sb", [128, 4], F32))
            acol = _ctx.enter_context(nc.sbuf_tensor("acol", [128, NDN], F32))
            e_sb = _ctx.enter_context(nc.sbuf_tensor("e_sb", [128, 16 * 128], BF16))
            f_sb = _ctx.enter_context(nc.sbuf_tensor("f_sb", [128, 16 * 128], BF16))
            mfull = _ctx.enter_context(nc.sbuf_tensor("mfull", [128, 64], F32))
            rfull = _ctx.enter_context(nc.sbuf_tensor("rfull", [128, 64], F32))
            carry = _ctx.enter_context(nc.sbuf_tensor("carry", [128, NDN], F32))
            dtin = _ctx.enter_context(nc.sbuf_tensor("dtin", [128, T], BF16))
            b_sb = _ctx.enter_context(nc.sbuf_tensor("b_sb", [16, T], BF16))
            c_sb = _ctx.enter_context(nc.sbuf_tensor("c_sb", [16, T], BF16))
            ldstg = _ctx.enter_context(nc.sbuf_tensor("ldstg", [128, 1024], F32))
            ldstg2 = _ctx.enter_context(nc.sbuf_tensor("ldstg2", [16, 2048], F32))
            xds = _ctx.enter_context(nc.sbuf_tensor("xds", [128, 8 * SC], F32))
            xlnds = _ctx.enter_context(nc.sbuf_tensor("xlnds", [128, 8 * SC], BF16))
            xlnT = _ctx.enter_context(nc.sbuf_tensor("xlnT", [128, 8 * SC], BF16))
            dt_sb = _ctx.enter_context(nc.sbuf_tensor("dt_sb", [128, 8 * SC], F32))
            dtbf = _ctx.enter_context(nc.sbuf_tensor("dtbf", [128, 8 * SC], BF16))
            u_sb = _ctx.enter_context(nc.sbuf_tensor("u_sb", [128, 8 * SC], BF16))
            brep = _ctx.enter_context(nc.sbuf_tensor("brep", [128, 2 * SC], BF16))
            crep = _ctx.enter_context(nc.sbuf_tensor("crep", [128, 2 * SC], BF16))
            zb_sb = _ctx.enter_context(nc.sbuf_tensor("zb_sb", [128, 2 * SC], F32))
            sp_r = _ctx.enter_context(nc.sbuf_tensor("sp_r", [128, 2 * SC], F32))
            sp_l = _ctx.enter_context(nc.sbuf_tensor("sp_l", [128, 2 * SC], F32))
            dA = _ctx.enter_context(nc.sbuf_tensor("dA", [128, 4 * SC], F32))
            dbx = _ctx.enter_context(nc.sbuf_tensor("dbx", [128, 4 * SC], BF16))
            h_sb = _ctx.enter_context(nc.sbuf_tensor("h_sb", [128, 4 * SC], BF16))
            hc_sb = _ctx.enter_context(nc.sbuf_tensor("hc_sb", [128, 4 * SC], BF16))
            ysb = _ctx.enter_context(nc.sbuf_tensor("ysb", [128, 2 * SC], BF16))
            ub_sb = _ctx.enter_context(nc.sbuf_tensor("ub_sb", [128, 4 * SC], BF16))
            pAw0 = _ctx.enter_context(nc.psum_tensor("pAw0", [128, 2 * SC], F32))
            pAw1 = _ctx.enter_context(nc.psum_tensor("pAw1", [128, 2 * SC], F32))
            pBw = _ctx.enter_context(nc.psum_tensor("pBw", [128, 2 * SC], F32))
            pY0 = _ctx.enter_context(nc.psum_tensor("pY0", [128, SC], F32))
            pTb0 = _ctx.enter_context(nc.psum_tensor("pTb0", [128, SC], BF16))
            pTb = (pTb0, pTb0)
            for v in range(16):
                dma("sync", e_sb[:, v * 128:(v + 1) * 128], e_cat[v], name="LP",
                    waits=(("CC", cc1_done),) if v == 0 else ())
                dma("sync", f_sb[:, v * 128:(v + 1) * 128], f_cat[v], name="LP")
            dma("sync", acol[:], a_pre, name="LP")
            for dc in range(4):
                dma("sync", dtb_sb[:, dc:dc + 1], dtb[dc * 128:(dc + 1) * 128, :],
                    name="LP")
            prep_efa = C["LP"]
            a1 = emit("scalar", lambda e: e.activation(acol[:], acol[:], AF.Exp),
                      waits=(("LP", prep_efa),), inc="AC")
            acneg = emit("vector", lambda e: e.tensor_scalar_mul(acol[:], acol[:], -1.0),
                         waits=(("AC", a1),), inc="DV5")
            wd1 = dma("sync", ldstg[:, 0:512].rearrange("p (c f) -> p c f", f=128),
                      dtw.rearrange("(c p) f -> p c f", p=128), name="LP")
            cd1 = emit("vector", lambda e: e.tensor_copy(dtbf[:, 0:512], ldstg[:, 0:512]),
                       waits=(("LP", wd1),), inc="DV5")
            for dc in range(4):
                td = emit("tensor", lambda e, dc=dc: e.transpose(
                    pTb0[:, 0:128], dtbf[:, dc * 128:(dc + 1) * 128], eyeb[:]),
                    waits=(("DV5", cd1), ("DV4", C["DV4"])), inc="PE4")
                emit("vector", lambda e, dc=dc: e.tensor_copy(
                    dtwT[:, dc * 128:(dc + 1) * 128], pTb0[:, 0:128]),
                    waits=(("PE4", td),), inc="DV4")
            for j in range(NCORES):
                dma("sync", ldstg[:, 0:1024], cc1_out[162 * j:162 * j + 128, :],
                    waits=(("CC", cc1_done), ("DV5", cd1)) if j == 0
                    else at("prep_cast2"), name="LP")
                dma("sync", ldstg2[:, 0:1024], cc1_out[162 * j + 128:162 * j + 144, :],
                    name="LP")
                dma("sync", ldstg2[:, 1024:2048],
                    cc1_out[162 * j + 144:162 * j + 160, :], name="LP")
                for q in range(8):
                    dma("sync", mfull[:, 8 * j + q:8 * j + q + 1],
                        cc1_out[162 * j + 160:162 * j + 161, 128 * q:128 * (q + 1)]
                        .rearrange("o f -> (o f)").unsqueeze(1), name="LP")
                    wj = dma("sync", rfull[:, 8 * j + q:8 * j + q + 1],
                             cc1_out[162 * j + 161:162 * j + 162, 128 * q:128 * (q + 1)]
                             .rearrange("o f -> (o f)").unsqueeze(1), name="LP")
                emit("vector", lambda e, j=j: e.tensor_copy(
                    dtin[:, j * 1024:(j + 1) * 1024], ldstg[:, 0:1024]),
                    waits=(("LP", wj),), inc="DV5")
                emit("vector", lambda e, j=j: e.tensor_copy(
                    b_sb[:, j * 1024:(j + 1) * 1024], ldstg2[:, 0:1024]),
                    inc="DV5")
                emit("vector", lambda e, j=j: e.tensor_copy(
                    c_sb[:, j * 1024:(j + 1) * 1024], ldstg2[:, 1024:2048]),
                    inc="DV5")
                snap("prep_cast2", "DV5")
            prep_ld = C["LP"]
            if DEBUG_MAMBA:
                dma("gpsimd", dbg_dtin, dtin[:], waits=(("LP", prep_ld), ("DV5", C["DV5"])), name="SPK")
                dma("gpsimd", dbg_m, mfull[:], name="SPK")
                dma("gpsimd", dbg_r, rfull[:], name="SPK")

            for tc in range(NSC):
                par = tc % 2
                lss = f"LS{tc % 2}"
                w_x = None
                for q in range(4):
                    w_x = dma("sync", xds[:, (par * 4 + q) * SC:(par * 4 + q + 1) * SC],
                              x_ds.rearrange("(a p) f -> a p f", p=128)[tc * 4 + q],
                              waits=at(f"lnq{tc - 2}") if q == 0 else (), name=lss)
                lnq = None
                for q in range(4):
                    lnq = emit("vector", lambda e, q=q, par=par, tc=tc: e.tensor_scalar(
                        xlnds[:, (par * 4 + q) * SC:(par * 4 + q + 1) * SC],
                        xds[:, (par * 4 + q) * SC:(par * 4 + q + 1) * SC],
                        mfull[:, tc * 4 + q:tc * 4 + q + 1],
                        rfull[:, tc * 4 + q:tc * 4 + q + 1], OP.subtract, OP.mult),
                        waits=((lss, w_x),) +
                              ((("LP", prep_ld),) if tc == 0 else ()) +
                              (at(f"scan_tp{tc - 2}") if q == 0 else ()),
                        inc="DV5")
                snap(f"lnq{tc}", "DV5")
                last_tp = None
                for dsub in range(4):
                    tb = 0
                    for q in range(4):
                        last_tp = emit("tensor",
                                       lambda e, q=q, dsub=dsub, par=par, tb=tb:
                                       e.transpose(
                            pTb[tb][:, q * 128:(q + 1) * 128],
                            xlnds[:, (par * 4 + q) * SC + dsub * 128:
                                  (par * 4 + q) * SC + (dsub + 1) * 128], eyeb[:]),
                            waits=((("DV5", lnq),) if (dsub == 0 and q == 0) else ()) +
                                  ((("DV4", C["DV4"]),) if q == 0 else ()),
                            inc="PE4" if q == 3 else None)
                    emit("vector", lambda e, dsub=dsub, par=par, tb=tb: e.tensor_copy(
                        xlnT[:, (par * 4 + dsub) * SC:(par * 4 + dsub + 1) * SC],
                        pTb[tb][:]),
                        waits=(("PE4", last_tp),), inc="DV4")
                snap(f"scan_tp{tc}", "PE4")
                if DEBUG_MAMBA and tc % 2 == 0:
                    dma("gpsimd", dbg_bsnap[tc // 2], b_sb[:], name="SPK")
                lrs = f"LR{tc % 2}"
                rep = None
                for kk in range(8):
                    dma("gpsimd", brep[16 * kk:16 * (kk + 1), par * SC:(par + 1) * SC],
                        b_sb[:, tc * SC:(tc + 1) * SC],
                        waits=(at(f"hcl{tc - 2}") + ((("LP", prep_ld),) if tc == 0 else ()))
                        if kk == 0 else (), name=lrs)
                    rep = dma("gpsimd",
                              crep[16 * kk:16 * (kk + 1), par * SC:(par + 1) * SC],
                              c_sb[:, tc * SC:(tc + 1) * SC], name=lrs)
                dtu = None
                for dc in range(4):
                    bpar = dc % 2
                    mmdt = emit("tensor", lambda e, dc=dc, tc=tc, bpar=bpar: e.matmul(
                        pAw0[:, bpar * SC:(bpar + 1) * SC],
                        dtwT[:, dc * 128:(dc + 1) * 128],
                        dtin[:, tc * SC:(tc + 1) * SC], start=True, stop=True),
                        waits=(at(f"dA_last{tc - 1}") if dc < 2 else ()) +
                              ((("ACz", C["ACz"] - 1),) if dc >= 2 else ()) +
                              ((("DV4", C["DV4"]),) if dc == 0 else ()),
                        inc="PE4")
                    zz = emit("scalar", lambda e, dc=dc, par=par, bpar=bpar: e.activation(
                        zb_sb[:, par * SC:(par + 1) * SC],
                        pAw0[:, bpar * SC:(bpar + 1) * SC],
                        AF.Identity, bias=dtb_sb[:, dc:dc + 1]),
                        waits=(("PE4", mmdt),) +
                              (at(f"dtu{tc}:{dc - 1}") if dc > 0 else
                               at(f"dtu{tc - 1}:3")),
                        inc="ACz")
                    emit("scalar", lambda e, par=par: e.activation(
                        sp_r[:, par * SC:(par + 1) * SC],
                        zb_sb[:, par * SC:(par + 1) * SC], AF.Relu))
                    emit("scalar", lambda e, par=par: e.activation(
                        sp_l[:, par * SC:(par + 1) * SC],
                        zb_sb[:, par * SC:(par + 1) * SC], AF.Abs))
                    emit("scalar", lambda e, par=par: e.activation(
                        sp_l[:, par * SC:(par + 1) * SC],
                        sp_l[:, par * SC:(par + 1) * SC], AF.Exp, scale=-1.0))
                    spl = emit("scalar", lambda e, par=par: e.activation(
                        sp_l[:, par * SC:(par + 1) * SC],
                        sp_l[:, par * SC:(par + 1) * SC], AF.Ln, bias=1.0), inc="AC")
                    emit("vector", lambda e, dc=dc, par=par: e.tensor_add(
                        dt_sb[:, (par * 4 + dc) * SC:(par * 4 + dc + 1) * SC],
                        sp_r[:, par * SC:(par + 1) * SC],
                        sp_l[:, par * SC:(par + 1) * SC]),
                        waits=(("AC", spl),) +
                              ((("PE1", mark.get(f"mmlast{tc - 2}", ("PE1", 0))[1]),)
                               if dc == 0 else ()))
                    emit("vector", lambda e, dc=dc, par=par: e.tensor_copy(
                        dtbf[:, (par * 4 + dc) * SC:(par * 4 + dc + 1) * SC],
                        dt_sb[:, (par * 4 + dc) * SC:(par * 4 + dc + 1) * SC]))
                    dtu = emit("vector", lambda e, dc=dc, par=par: e.tensor_tensor(
                        u_sb[:, (par * 4 + dc) * SC:(par * 4 + dc + 1) * SC],
                        dt_sb[:, (par * 4 + dc) * SC:(par * 4 + dc + 1) * SC],
                        xlnT[:, (par * 4 + dc) * SC:(par * 4 + dc + 1) * SC],
                        OP.mult), inc="DV5")
                    snap(f"dtu{tc}:{dc}", "DV5")
                if DEBUG_MAMBA and tc == 12:
                    dma("gpsimd", dbg_dt, dt_sb[:, 0:4 * SC], waits=(("DV5", dtu),), name="SPK")
                    dma("gpsimd", dbg_u, u_sb[:, 0:4 * SC], waits=(("DV5", dtu),), name="SPK")

                if DEBUG_MAMBA and tc == 14:
                    dma("gpsimd", dbg_brep, brep[:, par * SC:(par + 1) * SC],
                        waits=((lrs, rep),), name="SPK")
                    dma("gpsimd", dbg_crep2, crep[:, par * SC:(par + 1) * SC],
                        waits=((lrs, rep),), name="SPK")

                if full:
                    emit_dq(2, pTb0)
                for m in range(NDN // 2):
                    dc0, dc1 = 2 * m, 2 * m + 1
                    mp = m % 2
                    v0, v1 = dc0 % 16, dc1 % 16
                    dch = dc0 // 16
                    g_idx = tc * 4 + dch
                    yb = g_idx % 2
                    dsl = slice((par * 4 + dch) * SC, (par * 4 + dch + 1) * SC)
                    ip0 = slice(mp * 2 * SC, mp * 2 * SC + SC)
                    ip1 = slice(mp * 2 * SC + SC, (mp + 1) * 2 * SC)
                    ipp = slice(mp * 2 * SC, (mp + 1) * 2 * SC)
                    pa, pb = (pAw0, pAw1)[mp], pBw

                    emit("tensor", lambda e, v0=v0, dsl=dsl, pa=pa: e.matmul(
                        pa[:, 0:SC], e_sb[:, v0 * 128:(v0 + 1) * 128], dtbf[:, dsl],
                        start=True, stop=True),
                        waits=at(f"dAp{m - 2}") +
                              ((("ACz", C["ACz"]), ("DV5", dtu)) if m == 0 else ()) +
                              ((("LP", prep_efa),) if (tc == 0 and m == 0) else ()))
                    mma = emit("tensor", lambda e, v1=v1, dsl=dsl, pa=pa: e.matmul(
                        pa[:, SC:2 * SC], e_sb[:, v1 * 128:(v1 + 1) * 128],
                        dtbf[:, dsl], start=True, stop=True), inc="PE1")
                    emit("tensor", lambda e, v0=v0, dsl=dsl, pb=pb: e.matmul(
                        pb[:, 0:SC], e_sb[:, v0 * 128:(v0 + 1) * 128], u_sb[:, dsl],
                        start=True, stop=True),
                        waits=at(f"ubp{m - 1}"))
                    mmb = emit("tensor", lambda e, v1=v1, dsl=dsl, pb=pb: e.matmul(
                        pb[:, SC:2 * SC], e_sb[:, v1 * 128:(v1 + 1) * 128],
                        u_sb[:, dsl], start=True, stop=True), inc="PE1")
                    if m == NDN // 2 - 1:
                        snap(f"mmlast{tc}", "PE1")
                    emit("scalar", lambda e, pa=pa, ipx=ip0, dc0=dc0: e.activation(
                        dA[:, ipx], pa[:, 0:SC], AF.Exp, scale=acol[:, dc0:dc0 + 1]),
                        waits=(("PE1", mma),) + at(f"scanp{m - 2}") +
                              ((("DV5", acneg),) if (tc == 0 and m == 0) else ()))
                    da_m = emit("scalar", lambda e, pa=pa, ipx=ip1, dc1=dc1:
                                e.activation(
                        dA[:, ipx], pa[:, SC:2 * SC], AF.Exp,
                        scale=acol[:, dc1:dc1 + 1]), inc="AC")
                    snap(f"dAp{m}", "AC")
                    if m == NDN // 2 - 1:
                        snap(f"dA_last{tc}", "AC")
                    ubm = emit("scalar", lambda e, pb=pb, ipx=ipp: e.activation(
                        ub_sb[:, ipx], pb[:], AF.Copy),
                        waits=(("PE1", mmb),) + at(f"dbxp{m - 2}"), inc="AC")
                    snap(f"ubp{m}", "AC")
                    dbxm = emit("vector", lambda e, ipx=ipp, par=par: e.tensor_tensor(
                        dbx[:, ipx].rearrange("p (c f) -> p c f", f=SC),
                        ub_sb[:, ipx].rearrange("p (c f) -> p c f", f=SC),
                        brep[:, par * SC:(par + 1) * SC].unsqueeze(1)
                        .broadcast_to([128, 2, SC]), OP.mult),
                        waits=(("AC", ubm),) +
                              (((lrs, rep),) if m == 0 else ()),
                        inc="DV1")
                    snap(f"dbxp{m}", "DV1")
                    for s, (dcx, ipx) in enumerate(((dc0, ip0), (dc1, ip1))):
                        if tc == 0:
                            emit("vector", lambda e, ipx=ipx: e.tensor_tensor_scan(
                                h_sb[:, ipx], dA[:, ipx], dbx[:, ipx], 0.0,
                                OP.mult, OP.add),
                                waits=(("AC", da_m),) if s == 0 else (),
                                inc="DV2" if s == 1 else None)
                        else:
                            emit("vector", lambda e, ipx=ipx, dcx=dcx:
                                 e.tensor_tensor_scan(
                                h_sb[:, ipx], dA[:, ipx], dbx[:, ipx],
                                carry[:, dcx:dcx + 1], OP.mult, OP.add),
                                waits=(("AC", da_m),) if s == 0 else (),
                                inc="DV2" if s == 1 else None)
                    scm = C["DV2"]
                    snap(f"scanp{m}", "DV2")
                    emit("vector", lambda e, ipx=ipp, dc0=dc0: e.tensor_copy(
                        carry[:, dc0:dc0 + 2].unsqueeze(2),
                        h_sb[:, ipx].rearrange("p (c f) -> p c f", f=SC)
                        [:, :, SC - 1:SC]),
                        waits=(("DV2", scm),))
                    hcm = emit("vector", lambda e, ipx=ipp, par=par: e.tensor_tensor(
                        hc_sb[:, ipx].rearrange("p (c f) -> p c f", f=SC),
                        h_sb[:, ipx].rearrange("p (c f) -> p c f", f=SC),
                        crep[:, par * SC:(par + 1) * SC].unsqueeze(1)
                        .broadcast_to([128, 2, SC]), OP.mult),
                        waits=at(f"mmFp{m - 2}"), inc="DV3")
                    if m == NDN // 2 - 1:
                        snap(f"hcl{tc}", "DV3")
                    emit("tensor", lambda e, v0=v0, ipx=ip0: e.matmul(
                        pY0[:], f_sb[:, v0 * 128:(v0 + 1) * 128], hc_sb[:, ipx],
                        start=(v0 == 0), stop=False),
                        waits=(("DV3", hcm),) +
                              (at(f"yev{g_idx - 1}") if v0 == 0 else ()))
                    mmf = emit("tensor", lambda e, v1=v1, ipx=ip1: e.matmul(
                        pY0[:], f_sb[:, v1 * 128:(v1 + 1) * 128], hc_sb[:, ipx],
                        start=False, stop=(v1 == 15)),
                        inc="PE3")
                    snap(f"mmFp{m}", "PE3")
                    if v1 == 15:
                        yev = emit("vector", lambda e, yb=yb: e.tensor_copy(
                            ysb[:, yb * SC:(yb + 1) * SC], pY0[:]),
                            waits=(("PE3", mmf),) + at(f"yst{g_idx - 2}"),
                            inc="DV4")
                        snap(f"yev{g_idx}", "DV4")
                        dma("gpsimd",
                            cc2_in[dch * 128:(dch + 1) * 128, tc * SC:(tc + 1) * SC],
                            ysb[:, yb * SC:(yb + 1) * SC],
                            waits=(("DV4", yev),), name=f"SY{g_idx % 2}")
                        snap(f"yst{g_idx}", f"SY{g_idx % 2}")
            if DEBUG_MAMBA:
                dma("gpsimd", dbg_b[:, 0:T], b_sb[:],
                    waits=(("SY0", C["SY0"]), ("SY1", C["SY1"])), name="SPK")
                dma("gpsimd", dbg_c[:, 0:T], c_sb[:], name="SPK")
                dma("gpsimd", dbg_u13, u_sb[:, 0:4 * SC], name="SPK")
                dma("gpsimd", dbg_yloc, cc2_in,
                    waits=(("SY0", C["SY0"]), ("SY1", C["SY1"])), name="SPK")
                dma("gpsimd", dbg_carry, carry[:],
                    waits=(("DV2", mark["scan63"][1] if "scan63" in mark else 0),),
                    name="SPK")
            cc2_done = emit("gpsimd", lambda e: e.collective_compute(
                "AllGather", OP.bypass, replica_groups=[list(range(NCORES))],
                ins=[cc2_in.opt()], outs=[cc2_out.opt()]),
                waits=(("SY0", C["SY0"]), ("SY1", C["SY1"])), inc="CC")
            scan_dv4 = C["DV4"]

        with ExitStack() as _ctx:
            db_sb = _ctx.enter_context(nc.sbuf_tensor("db_sb", [128, D], F32))
            dstg = _ctx.enter_context(nc.sbuf_tensor("dstg", [1, D], F32))
            xt3 = _ctx.enter_context(nc.sbuf_tensor("xt3", [128, 2 * D], F32))
            yfm = _ctx.enter_context(nc.sbuf_tensor("yfm", [128, 2 * D], BF16))
            xmid = _ctx.enter_context(nc.sbuf_tensor("xmid", [128, 2 * D], F32))
            t3buf = _ctx.enter_context(nc.sbuf_tensor("t3buf", [128, D], F32))
            bnbuf3 = _ctx.enter_context(nc.sbuf_tensor("bnbuf3", [128, 48], F32))
            mv3 = _ctx.enter_context(nc.sbuf_tensor("mv3", [128, 16], F32))
            qm0 = _ctx.enter_context(nc.psum_tensor("qm0", [128, 512], BF16))
            qm1 = _ctx.enter_context(nc.psum_tensor("qm1", [128, 512], BF16))
            mdq = _ctx.enter_context(nc.psum_tensor("mdq", [128, 512], BF16))
            qm = (qm0, qm1)
            if full:
                emit_dq(48, mdq)
                cc_gu0 = emit("gpsimd", lambda e: e.collective_compute(
                    "AllGather", OP.bypass, replica_groups=[list(range(NCORES))],
                    ins=[wsh_gu0.opt()], outs=[wfull_gu0.opt()]),
                    waits=at("dq_b0_0") + at("dq_b0_1"), inc="CC")
                assert "dq_b0_0" in mark and "dq_b0_1" in mark
            wdv = dma("sync", db_sb[:], dvec.broadcast_to([128, D]), name="LDC",
                      waits=(("SY0", C["SY0"]), ("SY1", C["SY1"])))
            snap("db", "LDC")

            for c in range(8):
                par = c % 2
                lts = f"LT{c % 2}"
                w_x = dma("sync", xt3[:, par * D:(par + 1) * D],
                          x_tok[c * 128:(c + 1) * 128, :], waits=at(f"t3{c - 2}"),
                          name=lts)

                def load_yfm(e, c=c, par=par):
                    pid = e.partition_id()
                    srcv = cc2_out.rearrange("(a p) t -> p a t", p=128)
                    sl = srcv[0:128, 0:32, 0:128]
                    ap2 = bass.AP(sl.tensor, sl.offset + pid * TC + c * 128, sl.ap)
                    return e.dma_start(
                        out=yfm[:, par * D:(par + 1) * D]
                        .rearrange("p (a f) -> p a f", f=128), in_=ap2)
                w_y = emit("sync", load_yfm,
                           waits=((("CC", cc2_done),) if c == 0 else ()) +
                                 at(f"xm{c - 2}"),
                           inc=f"LY{c % 2}", inc_n=16)
                emit("vector", lambda e, c=c, par=par: e.tensor_scalar(
                    t3buf[:], xt3[:, par * D:(par + 1) * D],
                    stats[:, c:c + 1], stats[:, 8 + c:9 + c], OP.subtract, OP.mult),
                    waits=((lts, w_x),) + at("db"), inc="DV5")
                emit("vector", lambda e: e.tensor_tensor(
                    t3buf[:], t3buf[:], db_sb[:], OP.mult))
                emit("vector", lambda e, c=c, par=par: e.tensor_tensor(
                    t3buf[:], t3buf[:], xt3[:, par * D:(par + 1) * D], OP.add),
                    inc="DV5")
                snap(f"t3{c}", "DV5")
                for b in range(8):
                    mb = b % 2
                    tpm = None
                    for s in range(4):
                        tpm = emit("tensor", lambda e, b=b, s=s, par=par, mb=mb:
                                   e.transpose(
                            qm[mb][:, s * 128:(s + 1) * 128],
                            yfm[:, par * D + (4 * b + s) * 128:
                                par * D + (4 * b + s + 1) * 128], eyeb[:]),
                            waits=(((f"LY{c % 2}", w_y),) if (b == 0 and s == 0) else ()) +
                                  ((("DV4", C["DV4"]),) if s == 0 else ()),
                            inc="PE4" if s == 3 else None)
                    emit("vector", lambda e, b=b, mb=mb, par=par: e.tensor_tensor(
                        xmid[:, par * D + b * 512:par * D + (b + 1) * 512],
                        t3buf[:, b * 512:(b + 1) * 512], qm[mb][:], OP.add),
                        waits=(("PE4", tpm),) +
                              (at(f"xms{c - 2}") if b == 0 else ()), inc="DV4")
                snap(f"xm{c}", "DV4")
                dma("gpsimd", xmid_hbm[c * 128:(c + 1) * 128, :],
                    xmid[:, par * D:(par + 1) * D], waits=(("DV4", C["DV4"]),),
                    name=f"SXM{c % 2}")
                snap(f"xms{c}", f"SXM{c % 2}")
                if stop_after == "mamba":
                    dma("gpsimd", x_out[c * 128:(c + 1) * 128, :],
                        xmid[:, par * D:(par + 1) * D], name=f"SXM{c % 2}")
                bn3 = None
                for b in range(8):
                    bn3 = emit("vector", lambda e, b=b, par=par: e.bn_stats(
                        bnbuf3[:, 6 * b:6 * b + 6],
                        xmid[:, par * D + 512 * b:par * D + 512 * (b + 1)]),
                        inc="DV2" if b == 7 else None)
                ag3 = emit("vector", lambda e, c=c: e.bn_aggr(
                    mv3[:, 2 * c:2 * c + 2], bnbuf3[:]),
                    waits=(("DV2", bn3),), inc="DV2")
                m3 = emit("vector", lambda e, c=c: e.tensor_copy(
                    st3[:, c:c + 1], mv3[:, 2 * c:2 * c + 1]),
                    waits=(("DV2", ag3),), inc="DV5")
                emit("scalar", lambda e, c=c: e.activation(
                    mv3[:, 2 * c + 1:2 * c + 2], mv3[:, 2 * c + 1:2 * c + 2],
                    AF.Ln, bias=cst[:, 0:1]),
                    waits=(("DV5", m3),))
                emit("scalar", lambda e, c=c: e.activation(
                    st3[:, 8 + c:9 + c], mv3[:, 2 * c + 1:2 * c + 2],
                    AF.Exp, scale=cst[:, 1:2]),
                    inc="AC")
                if full:
                    emit_dq(5, mdq)
            snap("merge_done0", "SXM0")
            snap("merge_done1", "SXM1")
            snap("merge_ac", "AC")
            snap("merge_dv4", "DV4")

        if full:
            with ExitStack() as _ctx:
                xln2T = _ctx.enter_context(nc.sbuf_tensor("xln2T", [128, ND * SC], BF16))
                x3 = _ctx.enter_context(nc.sbuf_tensor("x3", [128, NJ * SC], BF16))
                wstr = _ctx.enter_context(nc.sbuf_tensor("wstr", [128, 2 * NJ * 128], BF16))
                lora_a = _ctx.enter_context(nc.sbuf_tensor("lora_a", [128, (2 * ND + NJ) * 16], BF16))
                lora_b = _ctx.enter_context(nc.sbuf_tensor("lora_b", [96, DFF], BF16))
                uvec = _ctx.enter_context(nc.sbuf_tensor("uvec", [96, 512], BF16))
                xshare = _ctx.enter_context(nc.sbuf_tensor("xshare", [128, 3072], F32))
                xshb = xshare[:].bitcast(BF16)
                xmr = xshare[:, 2048:3072]
                xln2b = lambda: xshb[:, 0:4096]
                r1_sb = xshb[:, 0:512]
                xo_sb = xshare[:, 0:1024]
                xma = xshare[:, 1024:2048]
                xob = xshare[:, 2048:3072]
                q1a = _ctx.enter_context(nc.psum_tensor("q1a", [128, 512], F32))
                q1b = _ctx.enter_context(nc.psum_tensor("q1b", [128, 512], F32))
                q2a = _ctx.enter_context(nc.psum_tensor("q2a", [128, 512], F32))
                q2b = _ctx.enter_context(nc.psum_tensor("q2b", [128, 512], F32))
                qu = _ctx.enter_context(nc.psum_tensor("qu", [96, 512], F32))
                qt0 = _ctx.enter_context(nc.psum_tensor("qt0", [128, 512], F32))
                qt1 = _ctx.enter_context(nc.psum_tensor("qt1", [128, 512], F32))
                qtb = _ctx.enter_context(nc.psum_tensor("qtb", [128, 512], BF16))
                q1 = (q1a, q1b)
                q2 = (q2a, q2b)
                QT = (qt0, qt1, q2a, q2b)

                wb = wstr[:].bitcast(F32)
                wl = dma("sync", wb[:, 0:ND * 16].rearrange("p (c f) -> p c f", f=16),
                         gla.rearrange("(c p) f -> p c f", p=128),
                         waits=at("merge_done0") + at("merge_done1"), name="LLA")
                la1 = emit("vector", lambda e, wb=wb: e.tensor_copy(
                    lora_a[:, 0:ND * 16], wb[:, 0:ND * 16]),
                    waits=(("LLA", wl),), inc="DV5")
                wl = dma("sync", wb[:, 0:ND * 16].rearrange("p (c f) -> p c f", f=16),
                         ula.rearrange("(c p) f -> p c f", p=128),
                         waits=(("DV5", la1),), name="LLA")
                la2 = emit("vector", lambda e, wb=wb: e.tensor_copy(
                    lora_a[:, ND * 16:2 * ND * 16], wb[:, 0:ND * 16]),
                    waits=(("LLA", wl),), inc="DV5")
                wl = dma("sync", wb[:, 0:NJ * 16].rearrange("p (c f) -> p c f", f=16),
                         dla.rearrange("(c p) f -> p c f", p=128),
                         waits=(("DV5", la2),), name="LLA")
                la3 = emit("vector", lambda e, wb=wb: e.tensor_copy(
                    lora_a[:, 2 * ND * 16:(2 * ND + NJ) * 16], wb[:, 0:NJ * 16]),
                    waits=(("LLA", wl),), inc="DV5")
                wl = dma("sync", wb[0:16, 0:DFF], glb, waits=(("DV5", la3),), name="LLA")
                lb1 = emit("vector", lambda e, wb=wb: e.tensor_copy(
                    lora_b[0:16, :], wb[0:16, 0:DFF]), waits=(("LLA", wl),), inc="DV5")
                wl = dma("sync", wb[0:16, 0:DFF], ulb, waits=(("DV5", lb1),), name="LLA")
                lb2 = emit("vector", lambda e, wb=wb: e.tensor_copy(
                    lora_b[32:48, :], wb[0:16, 0:DFF]), waits=(("LLA", wl),), inc="DV5")
                wl = dma("sync", wb[0:16, 0:D], dlb, waits=(("DV5", lb2),), name="LLA")
                emit("vector", lambda e, wb=wb: e.tensor_copy(
                    lora_b[64:80, 0:D], wb[0:16, 0:D]),
                    waits=(("LLA", wl),), inc="DV5")
                snap("lora", "DV5")

                def GLA(dc):
                    return lora_a[:, dc * 16:(dc + 1) * 16]

                def ULA(dc):
                    return lora_a[:, (ND + dc) * 16:(ND + dc + 1) * 16]

                def DLA(ic):
                    return lora_a[:, (2 * ND + ic) * 16:(2 * ND + ic + 1) * 16]

                for hf in range(2):
                    lx = None
                    for cq in range(4):
                        c = hf * 4 + cq
                        for hh in range(4):
                            wx = dma("sync", xmr,
                                     xmid_hbm[c * 128:(c + 1) * 128,
                                              hh * 1024:(hh + 1) * 1024],
                                     waits=(at("merge_done0") + at("merge_done1") +
                                            at("xobst0") + at("xobst1") +
                                            at(f"ffn_lx{hf}:{cq - 1}"))
                                     if hh == 0 else
                                     ((("DV5", lx),) if lx else ()),
                                     name="LXM")
                            lx = emit("vector", lambda e, c=c, hh=hh: e.tensor_scalar(
                                xln2b()[:, hh * 1024:(hh + 1) * 1024], xmr,
                                st3[:, c:c + 1], st3[:, 8 + c:9 + c],
                                OP.subtract, OP.mult),
                                waits=(("LXM", wx),) + at("merge_ac") +
                                      (at(f"ffn_tp{hf}:{cq - 1}") if hh == 0 else ()) +
                                      (at(f"relu{NJ - 1}") if (hh == 0 and cq == 0)
                                       else ()),
                                inc="DV5")
                        snap(f"ffn_lx{hf}:{cq}", "DV5")
                        for dc in range(ND):
                            tpm = emit("tensor", lambda e, dc=dc: e.transpose(
                                qtb[:, 0:128], xln2b()[:, dc * 128:(dc + 1) * 128],
                                eyeb[:]),
                                waits=(("DV5", lx), ("DV4", C["DV4"])) +
                                      (at("merge_dv4") if (hf == 0 and cq == 0 and dc == 0)
                                       else ()),
                                inc="PE4")
                            emit("vector", lambda e, dc=dc, cq=cq: e.tensor_copy(
                                xln2T[:, dc * SC + cq * 128:dc * SC + cq * 128 + 128],
                                qtb[:, 0:128]),
                                waits=(("PE4", tpm),), inc="DV4")
                        snap(f"ffn_tp{hf}:{cq}", "DV4")
                    snap(f"xln2T{hf}", "DV4")
                    mmu = None
                    for dc in range(ND):
                        emit("tensor", lambda e, dc=dc: e.matmul(
                            qu[0:16, :], GLA(dc), xln2T[:, dc * SC:(dc + 1) * SC],
                            start=(dc == 0), stop=(dc == ND - 1)),
                            waits=(at(f"xln2T{hf}") + at("lora") + at(f"uev{hf - 1}"))
                            if dc == 0 else ())
                        mmu = emit("tensor", lambda e, dc=dc: e.matmul(
                            qu[32:48, :], ULA(dc), xln2T[:, dc * SC:(dc + 1) * SC],
                            start=(dc == 0), stop=(dc == ND - 1)),
                            inc="PE4" if dc == ND - 1 else None)
                    ue1 = emit("vector", lambda e: e.tensor_copy(
                        uvec[0:48, :], qu[0:48, :]),
                        waits=(("PE4", mmu),), inc="DV4")
                    snap(f"uev{hf}", "DV4")
                    vfull0 = wfull_gu0.rearrange("(a p) f -> a p f", p=128)
                    vfull1 = wfull_gu1.rearrange("(a p) f -> a p f", p=128)
                    j_order = ([(r, jc) for jc in range(4) for r in range(NCORES)
                                if r * NGU_PAD + jc < NJ] +
                               [(r, jc) for jc in range(4, NGU_PAD) for r in range(NCORES)
                                if r * NGU_PAD + jc < NJ])
                    for jj, (r_, jc_) in enumerate(j_order):
                        j = r_ * NGU_PAD + jc_
                        jp = jj % 2
                        wof = jp * NJ * 128
                        lws = f"LW{jj % 2}"
                        if jc_ < 4:
                            gt = vfull0[(r_ * 4 + jc_) * 64:(r_ * 4 + jc_) * 64 + 32]
                            ut = vfull0[(r_ * 4 + jc_) * 64 + 32:(r_ * 4 + jc_) * 64 + 64]
                        else:
                            gt = vfull1[(r_ * 7 + jc_ - 4) * 64:
                                        (r_ * 7 + jc_ - 4) * 64 + 32]
                            ut = vfull1[(r_ * 7 + jc_ - 4) * 64 + 32:
                                        (r_ * 7 + jc_ - 4) * 64 + 64]
                        dma("sync",
                            wstr[:, wof:wof + ND * 128]
                            .rearrange("p (c f) -> p c f", f=128),
                            gt.transpose([1, 0, 2]),
                            waits=at(f"relu{jj - 2}") +
                                  (at("dmmlast") if jj < 2 else ()) +
                                  (at("lora") if jj == 0 else ()) +
                                  ((("CC", cc_gu0),) if (hf == 0 and jj == 0) else ()) +
                                  ((("CC", mark["ccgu1"][1]),)
                                   if (hf == 0 and jj == 32) else ()),
                            name=lws)
                        wu_l = dma("sync",
                                   wstr[:, wof + ND * 128:wof + 2 * ND * 128]
                                   .rearrange("p (c f) -> p c f", f=128),
                                   ut.transpose([1, 0, 2]), name=lws)
                        emit("tensor", lambda e, j=j, jp=jp: e.matmul(
                            q1[jp][:], lora_b[0:16, j * 128:(j + 1) * 128],
                            uvec[0:16, :], start=True, stop=False),
                            waits=(((lws, wu_l), ("DV4", ue1)) +
                                   at(f"relu{jj - 2}") + at(f"x3m{jj - 2}") +
                                   (at("y3evlast") if jj == 0 else ())))
                        for ic in range(ND):
                            emit("tensor", lambda e, ic=ic, wof=wof, jp=jp: e.matmul(
                                q1[jp][:],
                                wstr[:, wof + ic * 128:wof + (ic + 1) * 128],
                                xln2T[:, ic * SC:(ic + 1) * SC],
                                start=False, stop=(ic == ND - 1)))
                        emit("tensor", lambda e, j=j, jp=jp: e.matmul(
                            q2[jp][:], lora_b[32:48, j * 128:(j + 1) * 128],
                            uvec[32:48, :], start=True, stop=False),
                            waits=(at("xoaddlast") if jj == 0 else ()))
                        gmm = None
                        for ic in range(ND):
                            gmm = emit("tensor", lambda e, ic=ic, wof=wof, jp=jp: e.matmul(
                                q2[jp][:],
                                wstr[:, wof + (ND + ic) * 128:wof + (ND + ic + 1) * 128],
                                xln2T[:, ic * SC:(ic + 1) * SC],
                                start=False, stop=(ic == ND - 1)),
                                inc="PE1" if ic == ND - 1 else None)
                        if jj == NJ - 1:
                            snap("gmmlast", "PE1")
                        rel = emit("scalar", lambda e, jp=jp: e.activation(
                            r1_sb, q1[jp][:], AF.Relu),
                            waits=(("PE1", gmm),) + at(f"x3m{jj - 1}") +
                                  (at(f"ffn_lx{hf}:3") if jj == 0 else ()), inc="AC")
                        snap(f"relu{jj}", "AC")
                        emit("vector", lambda e, j=j, jp=jp: e.tensor_tensor(
                            x3[:, j * SC:(j + 1) * SC], r1_sb, q2[jp][:], OP.mult),
                            waits=(("AC", rel),), inc="DV4")
                        snap(f"x3m{jj}", "DV4")
                        if hf == 0:
                            emit_dq(3, qtb)
                            if jj == 18:
                                snap("ccgu1", "CC")
                                emit("gpsimd", lambda e: e.collective_compute(
                                    "AllGather", OP.bypass,
                                    replica_groups=[list(range(NCORES))],
                                    ins=[wsh_gu1.opt()], outs=[wfull_gu1.opt()]),
                                    waits=at("dq_b1_0") + at("dq_b1_1"), inc="CC")
                                assert "dq_b1_0" in mark and "dq_b1_1" in mark
                                snap("ccgu1", "CC")
                            if jj == 48:
                                snap("ccgd", "CC")
                                emit("gpsimd", lambda e: e.collective_compute(
                                    "AllGather", OP.bypass,
                                    replica_groups=[list(range(NCORES))],
                                    ins=[wsh_d.opt()], outs=[wfull_d.opt()]),
                                    waits=at("dq_b2_0") + at("dq_b2_1"), inc="CC")
                                assert "dq_b2_0" in mark and "dq_b2_1" in mark
                                snap("ccgd", "CC")
                    snap(f"x3done{hf}", "DV4")
                    mmd = None
                    for ic in range(NJ):
                        mmd = emit("tensor", lambda e, ic=ic: e.matmul(
                            qu[64:80, :], DLA(ic), x3[:, ic * SC:(ic + 1) * SC],
                            start=(ic == 0), stop=(ic == NJ - 1)),
                            waits=(at(f"x3done{hf}") + at(f"udev{hf - 1}"))
                            if ic == 0 else (),
                            inc="PE4" if ic == NJ - 1 else None)
                    ud1 = emit("vector", lambda e: e.tensor_copy(
                        uvec[64:80, :], qu[64:80, :]),
                        waits=(("PE4", mmd),), inc="DV4")
                    snap(f"udev{hf}", "DV4")
                    vfd = wfull_d.rearrange("(a p) f -> a p f", p=128)
                    for dm in range(ND):
                        dp = dm % 2
                        dg = dm // 4
                        wof = dp * NJ * 128
                        lws = f"LW{dm % 2}"
                        dt_base = (dm // 4) * 4 * 86 + (dm % 4) * 86
                        dma("sync",
                            wstr[:, wof:wof + 43 * 128]
                            .rearrange("p (c f) -> p c f", f=128),
                            vfd[dt_base:dt_base + 43].transpose([1, 0, 2]),
                            waits=at(f"y3ev{dm - 2}") +
                                  (at("gmmlast") if dm < 2 else ()) +
                                  ((("CC", mark["ccgd"][1]),)
                                   if (hf == 0 and dm == 0) else ()), name=lws)
                        wd2_l = dma("sync",
                                    wstr[:, wof + 43 * 128:wof + 86 * 128]
                                    .rearrange("p (c f) -> p c f", f=128),
                                    vfd[dt_base + 43:dt_base + 86].transpose([1, 0, 2]),
                                    name=lws)
                        emit("tensor", lambda e, dm=dm, dp=dp: e.matmul(
                            q1[dp][:], lora_b[64:80, dm * 128:(dm + 1) * 128],
                            uvec[64:80, :], start=True, stop=False),
                            waits=((lws, wd2_l), ("DV4", ud1)) + at(f"y3ev{dm - 2}"))
                        dmm = None
                        for ic in range(NJ):
                            dmm = emit("tensor", lambda e, ic=ic, wof=wof, dp=dp: e.matmul(
                                q1[dp][:],
                                wstr[:, wof + ic * 128:wof + (ic + 1) * 128],
                                x3[:, ic * SC:(ic + 1) * SC],
                                start=False, stop=(ic == NJ - 1)),
                                inc="PE1" if ic == NJ - 1 else None)
                        if dm == ND - 1:
                            snap("dmmlast", "PE1")
                        y3e = emit("vector", lambda e, dp=dp: e.tensor_copy(
                            xo_sb[:, dp * 512:(dp + 1) * 512], q1[dp][:]),
                            waits=(("PE1", dmm),) + at(f"y3tp{dm - 2}") +
                                  ((("AC", C["AC"]),) if dm == 0 else ()), inc="DV4")
                        snap(f"y3ev{dm}", "DV4")
                        if dm == ND - 1:
                            snap("y3evlast", "DV4")
                        for s in range(4):
                            emit("tensor", lambda e, s=s, dp=dp, dm=dm: e.transpose(
                                QT[s][:, (dm % 4) * 128:(dm % 4) * 128 + 128],
                                xo_sb[:, dp * 512 + s * 128:dp * 512 + (s + 1) * 128],
                                eyef[:]),
                                waits=(("DV4", y3e),) +
                                      (at("xoaddlast") if (dm % 4 == 0 and s == 0)
                                       else ()) +
                                      (at(f"x3m{NJ - 1}") if (dm == 0 and s >= 2)
                                       else ()),
                                inc="PE4" if s == 3 else None)
                        snap(f"y3tp{dm}", "PE4")
                        if dm % 4 == 3:
                            tpg = mark[f"y3tp{dm}"][1]
                            for s in range(4):
                                c = hf * 4 + s
                                wxa = dma("sync", xma[:, (s % 2) * 512:(s % 2 + 1) * 512],
                                          xmid_hbm.rearrange("(a p) f -> a p f", p=128)
                                          [c, :, dg * 512:(dg + 1) * 512],
                                          waits=at(f"xmafree{s % 2}"),
                                          name=f"LMA{s % 2}")
                                xoa = emit("vector",
                                           lambda e, s=s, dg=dg: e.tensor_tensor(
                                    xob[:, (s % 2) * 512:(s % 2 + 1) * 512],
                                    xma[:, (s % 2) * 512:(s % 2 + 1) * 512],
                                    QT[s][:], OP.add),
                                    waits=(("PE4", tpg), (f"LMA{s % 2}", wxa)) +
                                          at(f"xobst{s % 2}"),
                                    inc="DV4")
                                snap(f"xmafree{s % 2}", "DV4")
                                xst = dma("gpsimd",
                                          x_out[c * 128:(c + 1) * 128,
                                                dg * 512:(dg + 1) * 512],
                                          xob[:, (s % 2) * 512:(s % 2 + 1) * 512],
                                          waits=(("DV4", xoa),), name=f"SXO{s % 2}")
                                snap(f"xobst{s % 2}", f"SXO{s % 2}")
                            snap("xoaddlast", "DV4")

        store_names = ["SPK", "SY0", "SY1", "SXM0", "SXM1", "SXO0", "SXO1"]
        fin = [(dma_sems[n], C[n]) for n in store_names if C[n] > 0]

        def final_wait(e, fin=fin):
            for s, v in fin:
                e.wait_ge(s, v)
        emit("gpsimd", final_wait)

        @blk.sync
        def _(e):
            for fn in P.queues["sync"]:
                fn(e)

        @blk.gpsimd
        def _(e):
            for fn in P.queues["gpsimd"]:
                fn(e)

        @blk.vector
        def _(e):
            for fn in P.queues["vector"]:
                fn(e)

        @blk.scalar
        def _(e):
            for fn in P.queues["scalar"]:
                fn(e)

        @blk.tensor
        def _(e):
            for fn in P.queues["tensor"]:
                fn(e)

    return nc


def _fwd_inputs(k, x2, A_log, Dv, x_proj_w, dt_proj_w, dt_proj_b, consts,
                wtiles=None):
    a_sl = np.ascontiguousarray(A_log[k * DSL:(k + 1) * DSL]).reshape(-1)
    a_pre = np.ascontiguousarray(a_sl.reshape(NDN, 128).T)
    d = dict(
        x_tok=np.ascontiguousarray(x2[k * TC:(k + 1) * TC]),
        x_ds=np.ascontiguousarray(x2[:, k * DSL:(k + 1) * DSL]),
        xpw=x_proj_w,
        dtw=np.ascontiguousarray(dt_proj_w[k * DSL:(k + 1) * DSL]),
        dtb=np.ascontiguousarray(dt_proj_b[k * DSL:(k + 1) * DSL]).reshape(DSL, 1),
        a_pre=a_pre,
        dvec=Dv.reshape(1, D),
    )
    d.update(consts)
    if wtiles is not None:
        d.update(wtiles)
    return d


def _consts():
    ev = np.zeros((16, 128, 128), dtype=BF16NP)
    fv = np.zeros((16, 128, 128), dtype=BF16NP)
    for v in range(16):
        for p in range(128):
            r = 8 * v + p // 16
            ev[v, r, p] = 1
            fv[v, p, r] = 1
    g = np.zeros((16, 128), dtype=BF16NP)
    for p in range(128):
        g[p % 16, p] = 1
    return dict(
        e_cat=ev, f_cat=fv, g16=g,
        eye_bf=np.eye(128, dtype=BF16NP),
        eye_f32=np.eye(128, dtype=np.float32),
    )


def kernel(**inp):
    x2 = np.ascontiguousarray(np.asarray(inp["L0_attn_x"])[0])
    consts = _consts()
    R = NGU_PAD * 128
    gcod = _pad_rows(np.asarray(inp["L0_ffn_g_codes"]), R * NCORES)
    gam = _pad_rows(np.asarray(inp["L0_ffn_g_absmax"]), R * NCORES)
    ucod = _pad_rows(np.asarray(inp["L0_ffn_u_codes"]), R * NCORES)
    uam = _pad_rows(np.asarray(inp["L0_ffn_u_absmax"]), R * NCORES)
    dcodf = np.asarray(inp["L0_ffn_d_codes"])
    damf = np.asarray(inp["L0_ffn_d_absmax"])

    nc = build_forward("full")
    in_maps = []
    for k in range(NCORES):
        m = _fwd_inputs(k, x2,
                        np.asarray(inp["L0_attn_A_log"]),
                        np.asarray(inp["L0_attn_D"]),
                        np.asarray(inp["L0_attn_x_proj_w"]),
                        np.asarray(inp["L0_attn_dt_proj_w"]),
                        np.asarray(inp["L0_attn_dt_proj_b"]),
                        consts)
        m.update(dict(
            gc=np.ascontiguousarray(gcod[k * R:(k + 1) * R]).astype(np.int8),
            ga=np.ascontiguousarray(gam[k * R:(k + 1) * R]),
            uc=np.ascontiguousarray(ucod[k * R:(k + 1) * R]).astype(np.int8),
            ua=np.ascontiguousarray(uam[k * R:(k + 1) * R]),
            dcod=np.ascontiguousarray(dcodf[k * 512:(k + 1) * 512]).astype(np.int8),
            dam=np.ascontiguousarray(damf[k * 512:(k + 1) * 512]),
            gla=np.asarray(inp["L0_ffn_g_lora_a"]),
            glb=np.asarray(inp["L0_ffn_g_lora_b"]),
            ula=np.asarray(inp["L0_ffn_u_lora_a"]),
            ulb=np.asarray(inp["L0_ffn_u_lora_b"]),
            dla=np.asarray(inp["L0_ffn_d_lora_a"]),
            dlb=np.asarray(inp["L0_ffn_d_lora_b"]),
        ))
        in_maps.append(m)
    res = run_bass_kernel_spmd(nc, in_maps, core_ids=list(range(NCORES)))
    out = np.concatenate([res.results[k]["x_out"] for k in range(NCORES)], axis=0)
    return out[None]


NGU_PAD = 11


def build_dequant():
    nc = bass.Bass("TRN2", target_bir_lowering=False, debug=False)
    gc = nc.dram_tensor("gc", [NGU_PAD * 128, D], I32, kind="ExternalInput").ap()
    ga = nc.dram_tensor("ga", [NGU_PAD * 128, D // 64], F32, kind="ExternalInput").ap()
    uc = nc.dram_tensor("uc", [NGU_PAD * 128, D], I32, kind="ExternalInput").ap()
    ua = nc.dram_tensor("ua", [NGU_PAD * 128, D // 64], F32, kind="ExternalInput").ap()
    dcod = nc.dram_tensor("dcod", [4 * 128, DFF], I32, kind="ExternalInput").ap()
    dam = nc.dram_tensor("dam", [4 * 128, DFF // 64], F32, kind="ExternalInput").ap()
    eye_bf = nc.dram_tensor("eye_bf", [128, 128], BF16, kind="ExternalInput").ap()

    og = nc.dram_tensor("og", [NGU_PAD, ND, 128, 128], BF16, kind="ExternalOutput").ap()
    ou = nc.dram_tensor("ou", [NGU_PAD, ND, 128, 128], BF16, kind="ExternalOutput").ap()
    od = nc.dram_tensor("od", [4, NJ, 128, 128], BF16, kind="ExternalOutput").ap()

    with ExitStack() as _ctx:
        eyeb = _ctx.enter_context(nc.sbuf_tensor("eyeb", [128, 128], BF16))
        cod = _ctx.enter_context(nc.sbuf_tensor("cod", [128, 2 * 512], I32))
        amx = _ctx.enter_context(nc.sbuf_tensor("amx", [128, 2 * 8], F32))
        accs = _ctx.enter_context(nc.sbuf_tensor("accs", [128, 512], BF16))
        tmp = _ctx.enter_context(nc.sbuf_tensor("tmp", [128, 512], BF16))
        wq = _ctx.enter_context(nc.sbuf_tensor("wq", [128, 2 * 512], BF16))
        obuf = _ctx.enter_context(nc.sbuf_tensor("obuf", [128, 2 * 512], BF16))
        ptr0 = _ctx.enter_context(nc.psum_tensor("ptr0", [128, 512], BF16))
        ptr1 = _ctx.enter_context(nc.psum_tensor("ptr1", [128, 512], BF16))
        sLD = _ctx.enter_context(nc.semaphore("sLD"))
        sLQ1 = _ctx.enter_context(nc.semaphore("sLQ1"))
        sST = _ctx.enter_context(nc.semaphore("sST"))
        sSQ1 = _ctx.enter_context(nc.semaphore("sSQ1"))
        sDV = _ctx.enter_context(nc.semaphore("sDV"))
        sPE = _ctx.enter_context(nc.semaphore("sPE"))
        blk = _ctx.enter_context(nc.Block())
        P = Prog(dict(LQ0=sLD, LQ1=sLQ1, SQ0=sST, SQ1=sSQ1, DV=sDV, PE=sPE))
        emit, dma, snap, at, C = P.emit, P.dma, P.snap, P.at, P.C
        ptr = (ptr0, ptr1)
        w_eye = dma("sync", eyeb[:], eye_bf, name="LQ0")
        git = [0]

        def one_matrix(codes, am, out_t, nrow, icols):
            nib = (icols + 511) // 512
            for jc in range(nrow):
                for ib in range(nib):
                    it = git[0]
                    git[0] += 1
                    par = it % 2
                    iw = min(512, icols - ib * 512)
                    nt = iw // 128
                    nb = iw // 64
                    lqs = f"LQ{it % 2}"
                    dma("sync", cod[:, par * 512:par * 512 + iw],
                        codes[jc * 128:(jc + 1) * 128, ib * 512:ib * 512 + iw],
                        waits=at(f"dqlut{it - 2}"), name=lqs)
                    w_a = dma("sync", amx[:, par * 8:par * 8 + nb],
                              am[jc * 128:(jc + 1) * 128, ib * 8:ib * 8 + nb],
                              name=lqs)
                    csl = cod[:, par * 512:par * 512 + iw]
                    emit("vector", lambda e, csl=csl, iw=iw: e.tensor_scalar(
                        accs[:, 0:iw], csl, 0.0, float(NF4[0]),
                        OP.is_equal, OP.mult),
                        waits=((lqs, w_a),) + at(f"dqw{it - 2}"))
                    for k in range(1, 16):
                        emit("vector", lambda e, csl=csl, iw=iw, k=k: e.tensor_scalar(
                            tmp[:, 0:iw], csl, float(k), float(NF4[k]),
                            OP.is_equal, OP.mult))
                        emit("vector", lambda e, iw=iw: e.tensor_add(
                            accs[:, 0:iw], accs[:, 0:iw], tmp[:, 0:iw]))
                    lut = emit("vector", lambda e, par=par, iw=iw, nb=nb: e.tensor_tensor(
                        wq[:, par * 512:par * 512 + iw]
                        .rearrange("p (b z) -> p b z", z=64),
                        accs[:, 0:iw].rearrange("p (b z) -> p b z", z=64),
                        amx[:, par * 8:par * 8 + nb].unsqueeze(2)
                        .broadcast_to([128, nb, 64]),
                        OP.mult), inc="DV")
                    snap(f"dqlut{it}", "DV")
                    tpm = None
                    for s in range(nt):
                        tpm = emit("tensor", lambda e, s=s, par=par, tb=par: e.transpose(
                            ptr[tb][:, s * 128:(s + 1) * 128],
                            wq[:, par * 512 + s * 128:par * 512 + (s + 1) * 128],
                            eyeb[:]),
                            waits=((("DV", lut),) + at(f"dqev{it - 2}") +
                                   ((("LQ0", w_eye),) if it == 0 else ()))
                            if s == 0 else (),
                            inc="PE" if s == nt - 1 else None)
                    ev = emit("vector", lambda e, par=par, iw=iw, tb=par: e.tensor_copy(
                        obuf[:, par * 512:par * 512 + iw], ptr[tb][:, 0:iw]),
                        waits=(("PE", tpm),) + at(f"dqst{it - 2}"), inc="DV")
                    snap(f"dqev{it}", "DV")
                    snap(f"dqw{it}", "DV")
                    dma("gpsimd", out_t[jc, ib * 4:ib * 4 + nt].transpose([1, 0, 2]),
                        obuf[:, par * 512:par * 512 + iw]
                        .rearrange("p (c f) -> p c f", f=128),
                        waits=(("DV", ev),), name=f"SQ{it % 2}")
                    snap(f"dqst{it}", f"SQ{it % 2}")

        one_matrix(gc, ga, og, NGU_PAD, D)
        one_matrix(uc, ua, ou, NGU_PAD, D)
        one_matrix(dcod, dam, od, 4, DFF)
        fin = [(sST, C["SQ0"]), (sSQ1, C["SQ1"])]

        def final_wait(e, fin=fin):
            for s, v in fin:
                e.wait_ge(s, v)
        emit("gpsimd", final_wait)

        @blk.sync
        def _(e):
            for fn in P.queues["sync"]:
                fn(e)

        @blk.gpsimd
        def _(e):
            for fn in P.queues["gpsimd"]:
                fn(e)

        @blk.vector
        def _(e):
            for fn in P.queues["vector"]:
                fn(e)

        @blk.scalar
        def _(e):
            for fn in P.queues["scalar"]:
                fn(e)

        @blk.tensor
        def _(e):
            for fn in P.queues["tensor"]:
                fn(e)
    return nc


def _pad_rows(a, rows):
    if a.shape[0] == rows:
        return a
    pad = np.zeros((rows - a.shape[0],) + a.shape[1:], dtype=a.dtype)
    return np.concatenate([a, pad], axis=0)


def _run_dequant(inp):
    gcod = _pad_rows(np.asarray(inp["L0_ffn_g_codes"]), NGU_PAD * 128 * NCORES)
    gam = _pad_rows(np.asarray(inp["L0_ffn_g_absmax"]), NGU_PAD * 128 * NCORES)
    ucod = _pad_rows(np.asarray(inp["L0_ffn_u_codes"]), NGU_PAD * 128 * NCORES)
    uam = _pad_rows(np.asarray(inp["L0_ffn_u_absmax"]), NGU_PAD * 128 * NCORES)
    dcod = np.asarray(inp["L0_ffn_d_codes"])
    dam = np.asarray(inp["L0_ffn_d_absmax"])
    eye = np.eye(128, dtype=BF16NP)
    R = NGU_PAD * 128
    nc = build_dequant()
    in_maps = []
    for k in range(NCORES):
        in_maps.append(dict(
            gc=np.ascontiguousarray(gcod[k * R:(k + 1) * R]).astype(np.int8),
            ga=np.ascontiguousarray(gam[k * R:(k + 1) * R]),
            uc=np.ascontiguousarray(ucod[k * R:(k + 1) * R]).astype(np.int8),
            ua=np.ascontiguousarray(uam[k * R:(k + 1) * R]),
            dcod=np.ascontiguousarray(dcod[k * 512:(k + 1) * 512]),
            dam=np.ascontiguousarray(dam[k * 512:(k + 1) * 512]),
            eye_bf=eye,
        ))
    res = run_bass_kernel_spmd(nc, in_maps, core_ids=list(range(NCORES)))
    wg = np.concatenate([res.results[k]["og"] for k in range(NCORES)], axis=0)[:NJ]
    wu = np.concatenate([res.results[k]["ou"] for k in range(NCORES)], axis=0)[:NJ]
    wd = np.concatenate([res.results[k]["od"] for k in range(NCORES)], axis=0)
    return wg, wu, wd


# version 09 (final)
# speedup vs baseline: 1.2169x; 1.1550x over previous
import numpy as np
import ml_dtypes
from contextlib import ExitStack
import concourse.bass as bass
import concourse.mybir as mybir
from concourse.bass_utils import run_bass_kernel_spmd

F32 = mybir.dt.float32
BF16 = mybir.dt.bfloat16
I32 = mybir.dt.int32
I8 = mybir.dt.int8
AF = mybir.ActivationFunctionType
OP = mybir.AluOpType
BF16NP = ml_dtypes.bfloat16

NCORES = 8
T = 8192
TC = T // NCORES
D = 4096
DSL = D // NCORES
NST = 16
DTR = 128
DFF = 11008
NJ = DFF // 128
ND = D // 128
SC = 512
NSC = T // SC
NDN = DSL * NST // 128
EPS = 1e-5

NF4 = [-1.0, -0.6961928009986877, -0.5250730514526367, -0.39491748809814453,
       -0.28444138169288635, -0.18477343022823334, -0.09105003625154495, 0.0,
       0.07958029955625534, 0.16093020141124725, 0.2461123019456863,
       0.33791524171829224, 0.44070982933044434, 0.5626170039176941,
       0.7229568362236023, 1.0]

ENGS = ("sync", "gpsimd", "vector", "scalar", "tensor")


class Prog:

    def __init__(self, sems):
        self.queues = {e: [] for e in ENGS}
        self.sems = sems
        self.C = {k: 0 for k in sems}
        self.mark = {}

    def emit(self, eng, fn, waits=(), inc=None, inc_n=1):
        if inc is not None:
            self.C[inc] += inc_n
        wl = tuple((self.sems[n], v) for (n, v) in waits if v > 0)
        io = (self.sems[inc], inc_n) if inc is not None else None

        def thunk(e, fn=fn, wl=wl, io=io):
            for (s, v) in wl:
                e.wait_ge(s, v)
            r = fn(e)
            if io is not None:
                s, n = io
                if r is None:
                    e.sem_inc(s, n)
                else:
                    r.then_inc(s, n)
        self.queues[eng].append(thunk)
        return self.C[inc] if inc is not None else None

    def dma(self, eng, dst, src, waits=(), name=None):
        assert name is not None
        return self.emit(eng, lambda e, d=dst, s=src: e.dma_start(out=d, in_=s),
                         waits=waits, inc=name, inc_n=16)

    def snap(self, label, name):
        self.mark[label] = (name, self.C[name])

    def at(self, label):
        if label not in self.mark:
            return ()
        n, v = self.mark[label]
        return ((n, v),)


DEBUG_MAMBA = False


def build_forward(stop_after="full"):
    nc = bass.Bass("TRN2", target_bir_lowering=False, debug=False)

    x_tok = nc.dram_tensor("x_tok", [TC, D], F32, kind="ExternalInput").ap()
    x_ds = nc.dram_tensor("x_ds", [T, DSL], F32, kind="ExternalInput").ap()
    xpw = nc.dram_tensor("xpw", [DTR + 2 * NST, D], F32, kind="ExternalInput").ap()
    dtw = nc.dram_tensor("dtw", [DSL, DTR], F32, kind="ExternalInput").ap()
    dtb = nc.dram_tensor("dtb", [DSL, 1], F32, kind="ExternalInput").ap()
    a_pre = nc.dram_tensor("a_pre", [128, NDN], F32, kind="ExternalInput").ap()
    dvec = nc.dram_tensor("dvec", [1, D], F32, kind="ExternalInput").ap()
    e_cat = nc.dram_tensor("e_cat", [16, 128, 128], BF16, kind="ExternalInput").ap()
    f_cat = nc.dram_tensor("f_cat", [16, 128, 128], BF16, kind="ExternalInput").ap()
    g16 = nc.dram_tensor("g16", [16, 128], BF16, kind="ExternalInput").ap()
    eye_bf = nc.dram_tensor("eye_bf", [128, 128], BF16, kind="ExternalInput").ap()
    eye_f32i = nc.dram_tensor("eye_f32", [128, 128], F32, kind="ExternalInput").ap()
    full = stop_after == "full"
    if full:
        gc = nc.dram_tensor("gc", [NGU_PAD * 128, D], I8, kind="ExternalInput").ap()
        ga = nc.dram_tensor("ga", [NGU_PAD * 128, D // 64], F32, kind="ExternalInput").ap()
        uc = nc.dram_tensor("uc", [NGU_PAD * 128, D], I8, kind="ExternalInput").ap()
        ua = nc.dram_tensor("ua", [NGU_PAD * 128, D // 64], F32, kind="ExternalInput").ap()
        dcod = nc.dram_tensor("dcod", [4 * 128, DFF], I8, kind="ExternalInput").ap()
        dam = nc.dram_tensor("dam", [4 * 128, DFF // 64], F32, kind="ExternalInput").ap()
        gla = nc.dram_tensor("gla", [D, 16], F32, kind="ExternalInput").ap()
        glb = nc.dram_tensor("glb", [16, DFF], F32, kind="ExternalInput").ap()
        ula = nc.dram_tensor("ula", [D, 16], F32, kind="ExternalInput").ap()
        ulb = nc.dram_tensor("ulb", [16, DFF], F32, kind="ExternalInput").ap()
        dla = nc.dram_tensor("dla", [DFF, 16], F32, kind="ExternalInput").ap()
        dlb = nc.dram_tensor("dlb", [16, D], F32, kind="ExternalInput").ap()

    x_out = nc.dram_tensor("x_out", [TC, D], F32, kind="ExternalOutput").ap()

    if DEBUG_MAMBA:
        dbg_dtin = nc.dram_tensor("dbg_dtin", [128, T], F32, kind="ExternalOutput").ap()
        dbg_b = nc.dram_tensor("dbg_b", [16, T], F32, kind="ExternalOutput").ap()
        dbg_c = nc.dram_tensor("dbg_c", [16, T], F32, kind="ExternalOutput").ap()
        dbg_m = nc.dram_tensor("dbg_m", [128, 64], F32, kind="ExternalOutput").ap()
        dbg_r = nc.dram_tensor("dbg_r", [128, 64], F32, kind="ExternalOutput").ap()
        dbg_dt = nc.dram_tensor("dbg_dt", [128, 4 * SC], F32, kind="ExternalOutput").ap()
        dbg_carry = nc.dram_tensor("dbg_carry", [128, NDN], F32, kind="ExternalOutput").ap()
        dbg_da = nc.dram_tensor("dbg_da", [128, SC], F32, kind="ExternalOutput").ap()
        dbg_dbx = nc.dram_tensor("dbg_dbx", [128, SC], F32, kind="ExternalOutput").ap()
        dbg_u = nc.dram_tensor("dbg_u", [128, 4 * SC], F32, kind="ExternalOutput").ap()
        dbg_stats = nc.dram_tensor("dbg_stats", [128, 16], F32, kind="ExternalOutput").ap()
        dbg_stt = nc.dram_tensor("dbg_stt", [16, 128], F32, kind="ExternalOutput").ap()
        dbg_xp = nc.dram_tensor("dbg_xp", [128, 8 * 128], F32, kind="ExternalOutput").ap()
        dbg_yloc = nc.dram_tensor("dbg_yloc", [DSL, T], F32, kind="ExternalOutput").ap()
        dbg_h = nc.dram_tensor("dbg_h", [128, SC], F32, kind="ExternalOutput").ap()
        dbg_crep = nc.dram_tensor("dbg_crep", [128, SC], F32, kind="ExternalOutput").ap()
        dbg_cin = nc.dram_tensor("dbg_cin", [128, 1], F32, kind="ExternalOutput").ap()
        dbg_row = nc.dram_tensor("dbg_row", [128, NDN * SC], F32, kind="ExternalOutput").ap()
        dbg_row2 = nc.dram_tensor("dbg_row2", [128, NDN * SC], F32, kind="ExternalOutput").ap()
        dbg_u13 = nc.dram_tensor("dbg_u13", [128, 4 * SC], F32, kind="ExternalOutput").ap()
        dbg_brep = nc.dram_tensor("dbg_brep", [128, SC], F32, kind="ExternalOutput").ap()
        dbg_bsnap = nc.dram_tensor("dbg_bsnap", [8, 16, T], F32, kind="ExternalOutput").ap()
        dbg_crep2 = nc.dram_tensor("dbg_crep2", [128, SC], F32, kind="ExternalOutput").ap()
    if full:
        wsh_gu0 = nc.dram_tensor("wsh_gu0", [4 * 64 * 128, 128], BF16, kind="Internal").ap()
        wsh_gu1 = nc.dram_tensor("wsh_gu1", [7 * 64 * 128, 128], BF16, kind="Internal").ap()
        wsh_d = nc.dram_tensor("wsh_d", [4 * 86 * 128, 128], BF16, kind="Internal").ap()
        wfull_gu0 = nc.dram_tensor("wfull_gu0", [NCORES * 4 * 64 * 128, 128], BF16,
                                   kind="Internal").ap()
        wfull_gu1 = nc.dram_tensor("wfull_gu1", [NCORES * 7 * 64 * 128, 128], BF16,
                                   kind="Internal").ap()
        wfull_d = nc.dram_tensor("wfull_d", [NCORES * 4 * 86 * 128, 128], BF16,
                                 kind="Internal").ap()
    cc1_in = nc.dram_tensor("cc1_in", [162, TC], F32, kind="Internal").ap()
    cc1_out = nc.dram_tensor("cc1_out", [162 * NCORES, TC], F32, kind="Internal").ap()
    cc2_in = nc.dram_tensor("cc2_in", [DSL, T], BF16, kind="Internal").ap()
    cc2_out = nc.dram_tensor("cc2_out", [D, T], BF16, kind="Internal").ap()
    xmid_hbm = nc.dram_tensor("xmid_hbm", [TC, D], F32, kind="Internal").ap()

    with ExitStack() as _ctx:
        g_sb = _ctx.enter_context(nc.sbuf_tensor("g_sb", [16, 128], BF16))
        eyeb = _ctx.enter_context(nc.sbuf_tensor("eyeb", [128, 128], BF16))
        eyef = _ctx.enter_context(nc.sbuf_tensor("eyef", [128, 128], F32))
        stats = _ctx.enter_context(nc.sbuf_tensor("stats", [128, 16], F32))
        st3 = _ctx.enter_context(nc.sbuf_tensor("st3", [128, 16], F32))
        cst = _ctx.enter_context(nc.sbuf_tensor("cst", [128, 2], F32))
        if full:
            dq_cod0 = _ctx.enter_context(nc.sbuf_tensor("dq_cod0", [128, 512], I8))
            dq_cod1 = _ctx.enter_context(nc.sbuf_tensor("dq_cod1", [128, 512], I8))
            dq_amx0 = _ctx.enter_context(nc.sbuf_tensor("dq_amx0", [128, 8], F32))
            dq_amx1 = _ctx.enter_context(nc.sbuf_tensor("dq_amx1", [128, 8], F32))
            dq_acc = _ctx.enter_context(nc.sbuf_tensor("dq_acc", [128, 512], BF16))
            dq_tmp = _ctx.enter_context(nc.sbuf_tensor("dq_tmp", [128, 512], BF16))
            dq_wq = _ctx.enter_context(nc.sbuf_tensor("dq_wq", [128, 512], BF16))
            dq_ob = _ctx.enter_context(nc.sbuf_tensor("dq_ob", [128, 512], BF16))
        DMA_SEMS = ["LDC", "LX0", "LX1", "LP", "LS0", "LS1", "LT0", "LT1",
                    "LY0", "LY1", "LLA", "LW0", "LW1", "LXM", "LMA0", "LMA1",
                    "SY0", "SY1", "SXM0", "SXM1", "SXO0", "SXO1", "SPK",
                    "LR0", "LR1", "LQ0", "LQ1", "SQ0", "SQ1"]
        dma_sems = {n: _ctx.enter_context(nc.semaphore("s" + n)) for n in DMA_SEMS}
        sCC = _ctx.enter_context(nc.semaphore("sCC"))
        sPE1 = _ctx.enter_context(nc.semaphore("sPE1"))
        sPE3 = _ctx.enter_context(nc.semaphore("sPE3"))
        sPE4 = _ctx.enter_context(nc.semaphore("sPE4"))
        sAC = _ctx.enter_context(nc.semaphore("sAC"))
        sACz = _ctx.enter_context(nc.semaphore("sACz"))
        sDV1 = _ctx.enter_context(nc.semaphore("sDV1"))
        sDV2 = _ctx.enter_context(nc.semaphore("sDV2"))
        sDV3 = _ctx.enter_context(nc.semaphore("sDV3"))
        sDV4 = _ctx.enter_context(nc.semaphore("sDV4"))
        sDV5 = _ctx.enter_context(nc.semaphore("sDV5"))
        blk = _ctx.enter_context(nc.Block())
        sGP = _ctx.enter_context(nc.semaphore("sGP"))
        sem_map = dict(CC=sCC, GP=sGP, PE1=sPE1, PE3=sPE3, PE4=sPE4,
                       AC=sAC, ACz=sACz, DV1=sDV1, DV2=sDV2, DV3=sDV3,
                       DV4=sDV4, DV5=sDV5)
        sem_map.update(dma_sems)
        P = Prog(sem_map)
        emit, dma, snap, at, C, mark = P.emit, P.dma, P.snap, P.at, P.C, P.mark

        emit("vector", lambda e: e.memset(cst[:, 0:1], EPS))
        emit("vector", lambda e: e.memset(cst[:, 1:2], -0.5), inc="DV5")
        snap("cst", "DV5")

        dq_state = {"i": 0}
        dq_work = []
        if full:
            v_gu0 = wsh_gu0.rearrange("(a p) f -> a p f", p=128)
            v_gu1 = wsh_gu1.rearrange("(a p) f -> a p f", p=128)
            v_d = wsh_d.rearrange("(a p) f -> a p f", p=128)
            for jc in range(NGU_PAD):
                for mat, codes_t, am_t, uoff in ((0, gc, ga, 0), (1, uc, ua, 32)):
                    for ib in range(8):
                        dst = (v_gu0[(jc * 64 + uoff + ib * 4):
                                     (jc * 64 + uoff + ib * 4) + 4]
                               if jc < 4 else
                               v_gu1[((jc - 4) * 64 + uoff + ib * 4):
                                     ((jc - 4) * 64 + uoff + ib * 4) + 4])
                        dq_work.append((codes_t, am_t, jc, ib, 512, dst))
            for q in range(4):
                for ib in range(22):
                    iw = min(512, DFF - ib * 512)
                    dst = v_d[(q * 86 + ib * 4):(q * 86 + ib * 4) + iw // 128]
                    dq_work.append((dcod, dam, q, ib, iw, dst))

        def emit_dq(n, ptb, eng="vector"):
            for _ in range(n):
                it = dq_state["i"]
                if it >= len(dq_work):
                    return
                dq_state["i"] += 1
                codes_t, am_t, jc, ib, iw, dst = dq_work[it]
                nt = iw // 128
                nb = iw // 64
                lqs = f"LQ{it % 2}"
                sqs = f"SQ{it % 2}"
                lsem = "GP" if eng == "gpsimd" else "DV5"
                dq_cod = (dq_cod0, dq_cod1)[it % 2]
                dq_amx = (dq_amx0, dq_amx1)[it % 2]
                dma("sync", dq_cod[:, 0:iw],
                    codes_t[jc * 128:(jc + 1) * 128, ib * 512:ib * 512 + iw],
                    waits=at(f"dq_lut{it - 2}"), name=lqs)
                w_a = dma("sync", dq_amx[:, 0:nb],
                          am_t[jc * 128:(jc + 1) * 128, ib * 8:ib * 8 + nb],
                          name=lqs)
                csl = dq_cod[:, 0:iw]
                emit(eng, lambda e, csl=csl, iw=iw: e.tensor_scalar(
                    dq_acc[:, 0:iw], csl, 0.0, float(NF4[0]),
                    OP.is_equal, OP.mult),
                    waits=((lqs, w_a),))
                for k in range(1, 16):
                    emit(eng, lambda e, csl=csl, iw=iw, k=k: e.tensor_scalar(
                        dq_tmp[:, 0:iw], csl, float(k), float(NF4[k]),
                        OP.is_equal, OP.mult))
                    emit(eng, lambda e, iw=iw: e.tensor_add(
                        dq_acc[:, 0:iw], dq_acc[:, 0:iw], dq_tmp[:, 0:iw]))
                lut = emit(eng, lambda e, iw=iw, nb=nb, am=dq_amx: e.tensor_tensor(
                    dq_wq[:, 0:iw].rearrange("p (b z) -> p b z", z=64),
                    dq_acc[:, 0:iw].rearrange("p (b z) -> p b z", z=64),
                    am[:, 0:nb].unsqueeze(2)
                    .broadcast_to([128, nb, 64]), OP.mult),
                    waits=(("PE4", dq_state.get("tpm", 0)),), inc=lsem)
                snap(f"dq_lut{it}", lsem)
                tpm = None
                for s in range(nt):
                    tpm = emit("tensor", lambda e, s=s, ptb=ptb: e.transpose(
                        ptb[:, s * 128:(s + 1) * 128],
                        dq_wq[:, s * 128:(s + 1) * 128],
                        eyeb[:]),
                        waits=(((lsem, lut), ("DV4", C["DV4"])) if s == 0 else ()),
                        inc="PE4" if s == nt - 1 else None)
                dq_state["tpm"] = tpm
                ev = emit("vector", lambda e, iw=iw, ptb=ptb: e.tensor_copy(
                    dq_ob[:, 0:iw], ptb[:, 0:iw]),
                    waits=(("PE4", tpm),) + at(f"dq_st{it - 1}"), inc="DV4")
                dma("gpsimd", dst.transpose([1, 0, 2]),
                    dq_ob[:, 0:iw].rearrange("p (c f) -> p c f", f=128),
                    waits=(("DV4", ev),), name=sqs)
                snap(f"dq_st{it}", sqs)
                if dq_state["i"] == 64:
                    snap("dq_b0_0", "SQ0")
                    snap("dq_b0_1", "SQ1")
                if dq_state["i"] == 176:
                    snap("dq_b1_0", "SQ0")
                    snap("dq_b1_1", "SQ1")
                if dq_state["i"] == len(dq_work):
                    snap("dq_b2_0", "SQ0")
                    snap("dq_b2_1", "SQ1")

        with ExitStack() as _ctx:
            xpwT = _ctx.enter_context(nc.sbuf_tensor("xpwT", [128, 32 * 160], BF16))
            wstg = _ctx.enter_context(nc.sbuf_tensor("wstg", [128, 4096], F32))
            wstg2 = _ctx.enter_context(nc.sbuf_tensor("wstg2", [32, 4096], F32))
            wstgb = _ctx.enter_context(nc.sbuf_tensor("wstgb", [128, 4096], BF16))
            wstgb2 = _ctx.enter_context(nc.sbuf_tensor("wstgb2", [32, 4096], BF16))
            xt = _ctx.enter_context(nc.sbuf_tensor("xt", [128, 2 * D], F32))
            xlnb = _ctx.enter_context(nc.sbuf_tensor("xlnb", [128, 2 * D], BF16))
            xlnT1 = _ctx.enter_context(nc.sbuf_tensor("xlnT1", [128, D], BF16))
            bnbuf = _ctx.enter_context(nc.sbuf_tensor("bnbuf", [128, 48], F32))
            mv = _ctx.enter_context(nc.sbuf_tensor("mv", [128, 16], F32))
            xp_sb = _ctx.enter_context(nc.sbuf_tensor("xp_sb", [128, 8 * 128], F32))
            bxp_sb = _ctx.enter_context(nc.sbuf_tensor("bxp_sb", [16, 8 * 128], F32))
            cxp_sb = _ctx.enter_context(nc.sbuf_tensor("cxp_sb", [16, 8 * 128], F32))
            st_t = _ctx.enter_context(nc.sbuf_tensor("st_t", [16, 128], F32))
            p1t0 = _ctx.enter_context(nc.psum_tensor("p1t0", [128, 512], BF16))
            p1t1 = _ctx.enter_context(nc.psum_tensor("p1t1", [128, 512], BF16))
            pxd = _ctx.enter_context(nc.psum_tensor("pxd", [128, 128], F32))
            pxb = _ctx.enter_context(nc.psum_tensor("pxb", [16, 128], F32))
            pxc = _ctx.enter_context(nc.psum_tensor("pxc", [16, 128], F32))
            pst = _ctx.enter_context(nc.psum_tensor("pst", [16, 128], F32))
            p1t = (p1t0, p1t1)
            dma("sync", eyeb[:], eye_bf, name="LDC")
            dma("sync", eyef[:], eye_f32i, name="LDC")
            dma("sync", g_sb[:], g16, name="LDC")
            dma("sync", wstg[:], xpw[0:128, :], name="LDC")
            ldc_all = dma("sync", wstg2[:], xpw[128:160, :], name="LDC")
            emit("vector", lambda e: e.tensor_copy(wstgb[:], wstg[:]),
                 waits=(("LDC", ldc_all),), inc="DV5")
            c2 = emit("vector", lambda e: e.tensor_copy(wstgb2[:], wstg2[:]),
                      inc="DV5")
            for dc in range(32):
                emit("tensor", lambda e, dc=dc: e.transpose(
                    p1t0[:, 0:128], wstgb[:, dc * 128:(dc + 1) * 128], eyeb[:]),
                    waits=(("DV5", c2), ("DV4", C["DV4"])), inc="PE4")
                t2 = emit("tensor", lambda e, dc=dc: e.transpose(
                    p1t0[:, 128:160], wstgb2[:, dc * 128:(dc + 1) * 128],
                    eyeb[0:32, 0:32]),
                    inc="PE4")
                emit("vector", lambda e, dc=dc: e.tensor_copy(
                    xpwT[:, dc * 160:dc * 160 + 128], p1t0[:, 0:128]),
                    waits=(("PE4", t2),), inc="DV4")
                emit("vector", lambda e, dc=dc: e.tensor_copy(
                    xpwT[:, dc * 160 + 128:dc * 160 + 160], p1t0[:, 128:160]),
                    inc="DV4")

            for c in range(8):
                par = c % 2
                lxs = f"LX{c % 2}"
                w_x = dma("sync", xt[:, par * D:(par + 1) * D],
                          x_tok[c * 128:(c + 1) * 128, :],
                          waits=at(f"ph1_ln{c - 2}"), name=lxs)
                bnm = None
                for b in range(8):
                    bnm = emit("vector", lambda e, b=b, par=par: e.bn_stats(
                        bnbuf[:, 6 * b:6 * b + 6],
                        xt[:, par * D + 512 * b:par * D + 512 * (b + 1)]),
                        waits=((lxs, w_x),) if b == 0 else (),
                        inc="DV2" if b == 7 else None)
                ag = emit("vector", lambda e, c=c: e.bn_aggr(
                    mv[:, 2 * c:2 * c + 2], bnbuf[:]),
                    waits=(("DV2", bnm),), inc="DV2")
                m1 = emit("vector", lambda e, c=c: e.tensor_copy(
                    stats[:, c:c + 1], mv[:, 2 * c:2 * c + 1]),
                    waits=(("DV2", ag),), inc="DV5")
                emit("scalar", lambda e, c=c: e.activation(
                    mv[:, 2 * c + 1:2 * c + 2], mv[:, 2 * c + 1:2 * c + 2],
                    AF.Ln, bias=cst[:, 0:1]),
                    waits=(("DV5", m1),) + at("cst"))
                r1 = emit("scalar", lambda e, c=c: e.activation(
                    stats[:, 8 + c:9 + c], mv[:, 2 * c + 1:2 * c + 2],
                    AF.Exp, scale=cst[:, 1:2]),
                    inc="AC")
                lnm = emit("vector", lambda e, c=c, par=par: e.tensor_scalar(
                    xlnb[:, par * D:(par + 1) * D], xt[:, par * D:(par + 1) * D],
                    stats[:, c:c + 1], stats[:, 8 + c:9 + c], OP.subtract, OP.mult),
                    waits=(("AC", r1),) + at(f"ph1_tp{c - 2}"), inc="DV5")
                snap(f"ph1_ln{c}", "DV5")
                last_tp = None
                for b in range(8):
                    tb = b % 2
                    for s in range(4):
                        last_tp = emit("tensor", lambda e, b=b, s=s, par=par, tb=tb:
                                       e.transpose(
                            p1t[tb][:, s * 128:(s + 1) * 128],
                            xlnb[:, par * D + (4 * b + s) * 128:
                                 par * D + (4 * b + s + 1) * 128], eyeb[:]),
                            waits=((("DV5", lnm),) if (b == 0 and s == 0) else ()) +
                                  ((("DV4", C["DV4"]),) if s == 0 else ()),
                            inc="PE4" if s == 3 else None)
                    emit("vector", lambda e, b=b, tb=tb: e.tensor_copy(
                        xlnT1[:, b * 512:(b + 1) * 512], p1t[tb][:]),
                        waits=(("PE4", last_tp),), inc="DV4")
                snap(f"ph1_tp{c}", "PE4")
                ev = C["DV4"]
                mmx = None
                for dc in range(32):
                    emit("tensor", lambda e, dc=dc: e.matmul(
                        pxd[:], xpwT[:, dc * 160:dc * 160 + 128],
                        xlnT1[:, dc * 128:(dc + 1) * 128],
                        start=(dc == 0), stop=(dc == 31)),
                        waits=((("DV4", ev),) + at(f"ph1_xpe{c - 1}")) if dc == 0 else ())
                    emit("tensor", lambda e, dc=dc: e.matmul(
                        pxb[:], xpwT[:, dc * 160 + 128:dc * 160 + 144],
                        xlnT1[:, dc * 128:(dc + 1) * 128],
                        start=(dc == 0), stop=(dc == 31)))
                    mmx = emit("tensor", lambda e, dc=dc: e.matmul(
                        pxc[:], xpwT[:, dc * 160 + 144:dc * 160 + 160],
                        xlnT1[:, dc * 128:(dc + 1) * 128],
                        start=(dc == 0), stop=(dc == 31)),
                        inc="PE4" if dc == 31 else None)
                emit("vector", lambda e, c=c: e.tensor_copy(
                    xp_sb[:, c * 128:(c + 1) * 128], pxd[:]),
                    waits=(("PE4", mmx),), inc="DV4")
                emit("vector", lambda e, c=c: e.tensor_copy(
                    bxp_sb[:, c * 128:(c + 1) * 128], pxb[:]), inc="DV4")
                emit("vector", lambda e, c=c: e.tensor_copy(
                    cxp_sb[:, c * 128:(c + 1) * 128], pxc[:]), inc="DV4")
                snap(f"ph1_xpe{c}", "DV4")

            if DEBUG_MAMBA:
                dma("gpsimd", dbg_stats, stats[:],
                    waits=(("DV4", C["DV4"]), ("AC", C["AC"])), name="SPK")
                dma("gpsimd", dbg_xp, xp_sb[:], waits=(("DV4", C["DV4"]),), name="SPK")
            stt = emit("tensor", lambda e: e.transpose(pst[:], stats[:], eyef[:]),
                       waits=(("DV4", C["DV4"]), ("AC", C["AC"])), inc="PE4")
            stev = emit("vector", lambda e: e.tensor_copy(st_t[:], pst[:]),
                 waits=(("PE4", stt),), inc="DV4")
            if DEBUG_MAMBA:
                dma("gpsimd", dbg_stt, st_t[:], waits=(("DV4", stev),), name="SPK")
            dma("gpsimd", cc1_in[0:128, :], xp_sb[:], waits=(("DV4", C["DV4"]),), name="SPK")
            dma("gpsimd", cc1_in[128:144, :], bxp_sb[:], name="SPK")
            dma("gpsimd", cc1_in[144:160, :], cxp_sb[:], name="SPK")
            dma("gpsimd", cc1_in[160:161, :].rearrange("o (c f) -> (o c) f", f=128),
                st_t[0:8, :], name="SPK")
            dma("gpsimd", cc1_in[161:162, :].rearrange("o (c f) -> (o c) f", f=128),
                st_t[8:16, :], name="SPK")
            cc1_done = emit("gpsimd", lambda e: e.collective_compute(
                "AllGather", OP.bypass, replica_groups=[list(range(NCORES))],
                ins=[cc1_in.opt()], outs=[cc1_out.opt()]),
                waits=(("SPK", C["SPK"]),), inc="CC")

        with ExitStack() as _ctx:
            dtwT = _ctx.enter_context(nc.sbuf_tensor("dtwT", [128, 4 * 128], BF16))
            dtb_sb = _ctx.enter_context(nc.sbuf_tensor("dtb_sb", [128, 4], F32))
            acol = _ctx.enter_context(nc.sbuf_tensor("acol", [128, NDN], F32))
            e_sb = _ctx.enter_context(nc.sbuf_tensor("e_sb", [128, 16 * 128], BF16))
            f_sb = _ctx.enter_context(nc.sbuf_tensor("f_sb", [128, 16 * 128], BF16))
            mfull = _ctx.enter_context(nc.sbuf_tensor("mfull", [128, 64], F32))
            rfull = _ctx.enter_context(nc.sbuf_tensor("rfull", [128, 64], F32))
            carry = _ctx.enter_context(nc.sbuf_tensor("carry", [128, NDN], F32))
            dtin = _ctx.enter_context(nc.sbuf_tensor("dtin", [128, T], BF16))
            b_sb = _ctx.enter_context(nc.sbuf_tensor("b_sb", [16, T], BF16))
            c_sb = _ctx.enter_context(nc.sbuf_tensor("c_sb", [16, T], BF16))
            ldstg = _ctx.enter_context(nc.sbuf_tensor("ldstg", [128, 1024], F32))
            ldstg2 = _ctx.enter_context(nc.sbuf_tensor("ldstg2", [16, 2048], F32))
            xds = _ctx.enter_context(nc.sbuf_tensor("xds", [128, 8 * SC], F32))
            xlnds = _ctx.enter_context(nc.sbuf_tensor("xlnds", [128, 8 * SC], BF16))
            xlnT = _ctx.enter_context(nc.sbuf_tensor("xlnT", [128, 8 * SC], BF16))
            dt_sb = _ctx.enter_context(nc.sbuf_tensor("dt_sb", [128, 8 * SC], F32))
            dtbf = _ctx.enter_context(nc.sbuf_tensor("dtbf", [128, 8 * SC], BF16))
            u_sb = _ctx.enter_context(nc.sbuf_tensor("u_sb", [128, 8 * SC], BF16))
            brep = _ctx.enter_context(nc.sbuf_tensor("brep", [128, 2 * SC], BF16))
            crep = _ctx.enter_context(nc.sbuf_tensor("crep", [128, 2 * SC], BF16))
            zb_sb = _ctx.enter_context(nc.sbuf_tensor("zb_sb", [128, 2 * SC], F32))
            sp_r = _ctx.enter_context(nc.sbuf_tensor("sp_r", [128, 2 * SC], F32))
            sp_l = _ctx.enter_context(nc.sbuf_tensor("sp_l", [128, 2 * SC], F32))
            dA = _ctx.enter_context(nc.sbuf_tensor("dA", [128, 4 * SC], F32))
            dbx = _ctx.enter_context(nc.sbuf_tensor("dbx", [128, 4 * SC], BF16))
            h_sb = _ctx.enter_context(nc.sbuf_tensor("h_sb", [128, 4 * SC], BF16))
            hc_sb = _ctx.enter_context(nc.sbuf_tensor("hc_sb", [128, 4 * SC], BF16))
            ysb = _ctx.enter_context(nc.sbuf_tensor("ysb", [128, 2 * SC], BF16))
            ub_sb = _ctx.enter_context(nc.sbuf_tensor("ub_sb", [128, 4 * SC], BF16))
            pAw0 = _ctx.enter_context(nc.psum_tensor("pAw0", [128, 2 * SC], F32))
            pAw1 = _ctx.enter_context(nc.psum_tensor("pAw1", [128, 2 * SC], F32))
            pBw = _ctx.enter_context(nc.psum_tensor("pBw", [128, 2 * SC], F32))
            pY0 = _ctx.enter_context(nc.psum_tensor("pY0", [128, SC], F32))
            pTb0 = _ctx.enter_context(nc.psum_tensor("pTb0", [128, SC], BF16))
            pTb = (pTb0, pTb0)
            for v in range(16):
                dma("sync", e_sb[:, v * 128:(v + 1) * 128], e_cat[v], name="LP",
                    waits=(("CC", cc1_done),) if v == 0 else ())
                dma("sync", f_sb[:, v * 128:(v + 1) * 128], f_cat[v], name="LP")
            dma("sync", acol[:], a_pre, name="LP")
            for dc in range(4):
                dma("sync", dtb_sb[:, dc:dc + 1], dtb[dc * 128:(dc + 1) * 128, :],
                    name="LP")
            prep_efa = C["LP"]
            a1 = emit("scalar", lambda e: e.activation(acol[:], acol[:], AF.Exp),
                      waits=(("LP", prep_efa),), inc="AC")
            acneg = emit("vector", lambda e: e.tensor_scalar_mul(acol[:], acol[:], -1.0),
                         waits=(("AC", a1),), inc="DV5")
            wd1 = dma("sync", ldstg[:, 0:512].rearrange("p (c f) -> p c f", f=128),
                      dtw.rearrange("(c p) f -> p c f", p=128), name="LP")
            cd1 = emit("vector", lambda e: e.tensor_copy(dtbf[:, 0:512], ldstg[:, 0:512]),
                       waits=(("LP", wd1),), inc="DV5")
            for dc in range(4):
                td = emit("tensor", lambda e, dc=dc: e.transpose(
                    pTb0[:, 0:128], dtbf[:, dc * 128:(dc + 1) * 128], eyeb[:]),
                    waits=(("DV5", cd1), ("DV4", C["DV4"])), inc="PE4")
                emit("vector", lambda e, dc=dc: e.tensor_copy(
                    dtwT[:, dc * 128:(dc + 1) * 128], pTb0[:, 0:128]),
                    waits=(("PE4", td),), inc="DV4")
            for j in range(NCORES):
                dma("sync", ldstg[:, 0:1024], cc1_out[162 * j:162 * j + 128, :],
                    waits=(("CC", cc1_done), ("DV5", cd1)) if j == 0
                    else at("prep_cast2"), name="LP")
                dma("sync", ldstg2[:, 0:1024], cc1_out[162 * j + 128:162 * j + 144, :],
                    name="LP")
                dma("sync", ldstg2[:, 1024:2048],
                    cc1_out[162 * j + 144:162 * j + 160, :], name="LP")
                for q in range(8):
                    dma("sync", mfull[:, 8 * j + q:8 * j + q + 1],
                        cc1_out[162 * j + 160:162 * j + 161, 128 * q:128 * (q + 1)]
                        .rearrange("o f -> (o f)").unsqueeze(1), name="LP")
                    wj = dma("sync", rfull[:, 8 * j + q:8 * j + q + 1],
                             cc1_out[162 * j + 161:162 * j + 162, 128 * q:128 * (q + 1)]
                             .rearrange("o f -> (o f)").unsqueeze(1), name="LP")
                emit("vector", lambda e, j=j: e.tensor_copy(
                    dtin[:, j * 1024:(j + 1) * 1024], ldstg[:, 0:1024]),
                    waits=(("LP", wj),), inc="DV5")
                emit("vector", lambda e, j=j: e.tensor_copy(
                    b_sb[:, j * 1024:(j + 1) * 1024], ldstg2[:, 0:1024]),
                    inc="DV5")
                emit("vector", lambda e, j=j: e.tensor_copy(
                    c_sb[:, j * 1024:(j + 1) * 1024], ldstg2[:, 1024:2048]),
                    inc="DV5")
                snap("prep_cast2", "DV5")
            prep_ld = C["LP"]
            if DEBUG_MAMBA:
                dma("gpsimd", dbg_dtin, dtin[:], waits=(("LP", prep_ld), ("DV5", C["DV5"])), name="SPK")
                dma("gpsimd", dbg_m, mfull[:], name="SPK")
                dma("gpsimd", dbg_r, rfull[:], name="SPK")

            for tc in range(NSC):
                par = tc % 2
                lss = f"LS{tc % 2}"
                w_x = None
                for q in range(4):
                    w_x = dma("sync", xds[:, (par * 4 + q) * SC:(par * 4 + q + 1) * SC],
                              x_ds.rearrange("(a p) f -> a p f", p=128)[tc * 4 + q],
                              waits=at(f"lnq{tc - 2}") if q == 0 else (), name=lss)
                lnq = None
                for q in range(4):
                    lnq = emit("vector", lambda e, q=q, par=par, tc=tc: e.tensor_scalar(
                        xlnds[:, (par * 4 + q) * SC:(par * 4 + q + 1) * SC],
                        xds[:, (par * 4 + q) * SC:(par * 4 + q + 1) * SC],
                        mfull[:, tc * 4 + q:tc * 4 + q + 1],
                        rfull[:, tc * 4 + q:tc * 4 + q + 1], OP.subtract, OP.mult),
                        waits=((lss, w_x),) +
                              ((("LP", prep_ld),) if tc == 0 else ()) +
                              (at(f"scan_tp{tc - 2}") if q == 0 else ()),
                        inc="DV5")
                snap(f"lnq{tc}", "DV5")
                last_tp = None
                for dsub in range(4):
                    tb = 0
                    for q in range(4):
                        last_tp = emit("tensor",
                                       lambda e, q=q, dsub=dsub, par=par, tb=tb:
                                       e.transpose(
                            pTb[tb][:, q * 128:(q + 1) * 128],
                            xlnds[:, (par * 4 + q) * SC + dsub * 128:
                                  (par * 4 + q) * SC + (dsub + 1) * 128], eyeb[:]),
                            waits=((("DV5", lnq),) if (dsub == 0 and q == 0) else ()) +
                                  ((("DV4", C["DV4"]),) if q == 0 else ()),
                            inc="PE4" if q == 3 else None)
                    emit("vector", lambda e, dsub=dsub, par=par, tb=tb: e.tensor_copy(
                        xlnT[:, (par * 4 + dsub) * SC:(par * 4 + dsub + 1) * SC],
                        pTb[tb][:]),
                        waits=(("PE4", last_tp),), inc="DV4")
                snap(f"scan_tp{tc}", "PE4")
                if DEBUG_MAMBA and tc % 2 == 0:
                    dma("gpsimd", dbg_bsnap[tc // 2], b_sb[:], name="SPK")
                lrs = f"LR{tc % 2}"
                rep = None
                for kk in range(8):
                    dma("gpsimd", brep[16 * kk:16 * (kk + 1), par * SC:(par + 1) * SC],
                        b_sb[:, tc * SC:(tc + 1) * SC],
                        waits=(at(f"hcl{tc - 2}") + ((("LP", prep_ld),) if tc == 0 else ()))
                        if kk == 0 else (), name=lrs)
                    rep = dma("gpsimd",
                              crep[16 * kk:16 * (kk + 1), par * SC:(par + 1) * SC],
                              c_sb[:, tc * SC:(tc + 1) * SC], name=lrs)
                dtu = None
                for dc in range(4):
                    bpar = dc % 2
                    mmdt = emit("tensor", lambda e, dc=dc, tc=tc, bpar=bpar: e.matmul(
                        pAw0[:, bpar * SC:(bpar + 1) * SC],
                        dtwT[:, dc * 128:(dc + 1) * 128],
                        dtin[:, tc * SC:(tc + 1) * SC], start=True, stop=True),
                        waits=(at(f"dA_last{tc - 1}") if dc < 2 else ()) +
                              ((("ACz", C["ACz"] - 1),) if dc >= 2 else ()) +
                              ((("DV4", C["DV4"]),) if dc == 0 else ()),
                        inc="PE4")
                    zz = emit("scalar", lambda e, dc=dc, par=par, bpar=bpar: e.activation(
                        zb_sb[:, par * SC:(par + 1) * SC],
                        pAw0[:, bpar * SC:(bpar + 1) * SC],
                        AF.Identity, bias=dtb_sb[:, dc:dc + 1]),
                        waits=(("PE4", mmdt),) +
                              (at(f"dtu{tc}:{dc - 1}") if dc > 0 else
                               at(f"dtu{tc - 1}:3")),
                        inc="ACz")
                    emit("scalar", lambda e, par=par: e.activation(
                        sp_r[:, par * SC:(par + 1) * SC],
                        zb_sb[:, par * SC:(par + 1) * SC], AF.Relu))
                    emit("scalar", lambda e, par=par: e.activation(
                        sp_l[:, par * SC:(par + 1) * SC],
                        zb_sb[:, par * SC:(par + 1) * SC], AF.Abs))
                    emit("scalar", lambda e, par=par: e.activation(
                        sp_l[:, par * SC:(par + 1) * SC],
                        sp_l[:, par * SC:(par + 1) * SC], AF.Exp, scale=-1.0))
                    spl = emit("scalar", lambda e, par=par: e.activation(
                        sp_l[:, par * SC:(par + 1) * SC],
                        sp_l[:, par * SC:(par + 1) * SC], AF.Ln, bias=1.0), inc="AC")
                    emit("vector", lambda e, dc=dc, par=par: e.tensor_add(
                        dt_sb[:, (par * 4 + dc) * SC:(par * 4 + dc + 1) * SC],
                        sp_r[:, par * SC:(par + 1) * SC],
                        sp_l[:, par * SC:(par + 1) * SC]),
                        waits=(("AC", spl),) +
                              ((("PE1", mark.get(f"mmlast{tc - 2}", ("PE1", 0))[1]),)
                               if dc == 0 else ()))
                    emit("vector", lambda e, dc=dc, par=par: e.tensor_copy(
                        dtbf[:, (par * 4 + dc) * SC:(par * 4 + dc + 1) * SC],
                        dt_sb[:, (par * 4 + dc) * SC:(par * 4 + dc + 1) * SC]))
                    dtu = emit("vector", lambda e, dc=dc, par=par: e.tensor_tensor(
                        u_sb[:, (par * 4 + dc) * SC:(par * 4 + dc + 1) * SC],
                        dt_sb[:, (par * 4 + dc) * SC:(par * 4 + dc + 1) * SC],
                        xlnT[:, (par * 4 + dc) * SC:(par * 4 + dc + 1) * SC],
                        OP.mult), inc="DV5")
                    snap(f"dtu{tc}:{dc}", "DV5")
                if DEBUG_MAMBA and tc == 12:
                    dma("gpsimd", dbg_dt, dt_sb[:, 0:4 * SC], waits=(("DV5", dtu),), name="SPK")
                    dma("gpsimd", dbg_u, u_sb[:, 0:4 * SC], waits=(("DV5", dtu),), name="SPK")

                if DEBUG_MAMBA and tc == 14:
                    dma("gpsimd", dbg_brep, brep[:, par * SC:(par + 1) * SC],
                        waits=((lrs, rep),), name="SPK")
                    dma("gpsimd", dbg_crep2, crep[:, par * SC:(par + 1) * SC],
                        waits=((lrs, rep),), name="SPK")

                if full:
                    emit_dq(2, pTb0)
                NP = NDN // 2

                def produce(m, tc=tc, par=par, lrs=lrs, rep=rep, dtu=dtu):
                    mg = tc * NP + m
                    dc0, dc1 = 2 * m, 2 * m + 1
                    mp = m % 2
                    v0, v1 = dc0 % 16, dc1 % 16
                    dch = dc0 // 16
                    dsl = slice((par * 4 + dch) * SC, (par * 4 + dch + 1) * SC)
                    ip0 = slice(mp * 2 * SC, mp * 2 * SC + SC)
                    ip1 = slice(mp * 2 * SC + SC, (mp + 1) * 2 * SC)
                    ipp = slice(mp * 2 * SC, (mp + 1) * 2 * SC)
                    pa = (pAw0, pAw1)[mp]
                    emit("tensor", lambda e, v0=v0, dsl=dsl, pa=pa: e.matmul(
                        pa[:, 0:SC], e_sb[:, v0 * 128:(v0 + 1) * 128], dtbf[:, dsl],
                        start=True, stop=True),
                        waits=at(f"dAp{mg - 2}") +
                              ((("ACz", C["ACz"]), ("DV5", dtu)) if m == 0 else ()) +
                              ((("LP", prep_efa),) if (tc == 0 and m == 0) else ()))
                    mma = emit("tensor", lambda e, v1=v1, dsl=dsl, pa=pa: e.matmul(
                        pa[:, SC:2 * SC], e_sb[:, v1 * 128:(v1 + 1) * 128],
                        dtbf[:, dsl], start=True, stop=True), inc="PE1")
                    emit("tensor", lambda e, v0=v0, dsl=dsl: e.matmul(
                        pBw[:, 0:SC], e_sb[:, v0 * 128:(v0 + 1) * 128], u_sb[:, dsl],
                        start=True, stop=True),
                        waits=at(f"ubp{mg - 1}"))
                    mmb = emit("tensor", lambda e, v1=v1, dsl=dsl: e.matmul(
                        pBw[:, SC:2 * SC], e_sb[:, v1 * 128:(v1 + 1) * 128],
                        u_sb[:, dsl], start=True, stop=True), inc="PE1")
                    if m == NP - 1:
                        snap(f"mmlast{tc}", "PE1")
                    emit("scalar", lambda e, pa=pa, ipx=ip0, dc0=dc0: e.activation(
                        dA[:, ipx], pa[:, 0:SC], AF.Exp, scale=acol[:, dc0:dc0 + 1]),
                        waits=(("PE1", mma),) + at(f"scanp{mg - 2}") +
                              ((("DV5", acneg),) if (tc == 0 and m == 0) else ()))
                    emit("scalar", lambda e, pa=pa, ipx=ip1, dc1=dc1: e.activation(
                        dA[:, ipx], pa[:, SC:2 * SC], AF.Exp,
                        scale=acol[:, dc1:dc1 + 1]), inc="AC")
                    snap(f"dAp{mg}", "AC")
                    if m == NP - 1:
                        snap(f"dA_last{tc}", "AC")
                    emit("scalar", lambda e, ipx=ipp: e.activation(
                        ub_sb[:, ipx], pBw[:], AF.Copy),
                        waits=(("PE1", mmb),) + at(f"dbxp{mg - 2}"), inc="AC")
                    snap(f"ubp{mg}", "AC")

                def produce_dbx(m, tc=tc, par=par, lrs=lrs, rep=rep):
                    mg = tc * NP + m
                    mp = m % 2
                    ipp = slice(mp * 2 * SC, (mp + 1) * 2 * SC)
                    emit("vector", lambda e, ipx=ipp, par=par: e.tensor_tensor(
                        dbx[:, ipx].rearrange("p (c f) -> p c f", f=SC),
                        ub_sb[:, ipx].rearrange("p (c f) -> p c f", f=SC),
                        brep[:, par * SC:(par + 1) * SC].unsqueeze(1)
                        .broadcast_to([128, 2, SC]), OP.mult),
                        waits=at(f"ubp{mg}") +
                              (((lrs, rep),) if m == 0 else ()),
                        inc="DV1")
                    snap(f"dbxp{mg}", "DV1")

                def consume(m, tc=tc, par=par):
                    mg = tc * NP + m
                    dc0, dc1 = 2 * m, 2 * m + 1
                    mp = m % 2
                    v0, v1 = dc0 % 16, dc1 % 16
                    dch = dc0 // 16
                    g_idx = tc * 4 + dch
                    yb = g_idx % 2
                    ip0 = slice(mp * 2 * SC, mp * 2 * SC + SC)
                    ip1 = slice(mp * 2 * SC + SC, (mp + 1) * 2 * SC)
                    ipp = slice(mp * 2 * SC, (mp + 1) * 2 * SC)
                    for s, (dcx, ipx) in enumerate(((dc0, ip0), (dc1, ip1))):
                        if tc == 0:
                            emit("vector", lambda e, ipx=ipx: e.tensor_tensor_scan(
                                h_sb[:, ipx], dA[:, ipx], dbx[:, ipx], 0.0,
                                OP.mult, OP.add),
                                waits=at(f"dAp{mg}") if s == 0 else (),
                                inc="DV2" if s == 1 else None)
                        else:
                            emit("vector", lambda e, ipx=ipx, dcx=dcx:
                                 e.tensor_tensor_scan(
                                h_sb[:, ipx], dA[:, ipx], dbx[:, ipx],
                                carry[:, dcx:dcx + 1], OP.mult, OP.add),
                                waits=at(f"dAp{mg}") if s == 0 else (),
                                inc="DV2" if s == 1 else None)
                    scm = C["DV2"]
                    snap(f"scanp{mg}", "DV2")
                    emit("vector", lambda e, ipx=ipp, dc0=dc0: e.tensor_copy(
                        carry[:, dc0:dc0 + 2].unsqueeze(2),
                        h_sb[:, ipx].rearrange("p (c f) -> p c f", f=SC)
                        [:, :, SC - 1:SC]),
                        waits=(("DV2", scm),))
                    hcm = emit("vector", lambda e, ipx=ipp, par=par: e.tensor_tensor(
                        hc_sb[:, ipx].rearrange("p (c f) -> p c f", f=SC),
                        h_sb[:, ipx].rearrange("p (c f) -> p c f", f=SC),
                        crep[:, par * SC:(par + 1) * SC].unsqueeze(1)
                        .broadcast_to([128, 2, SC]), OP.mult),
                        waits=at(f"mmFp{mg - 2}"), inc="DV3")
                    if m == NP - 1:
                        snap(f"hcl{tc}", "DV3")
                    emit("tensor", lambda e, v0=v0, ipx=ip0: e.matmul(
                        pY0[:], f_sb[:, v0 * 128:(v0 + 1) * 128], hc_sb[:, ipx],
                        start=(v0 == 0), stop=False),
                        waits=(("DV3", hcm),) +
                              (at(f"yev{g_idx - 1}") if v0 == 0 else ()))
                    mmf = emit("tensor", lambda e, v1=v1, ipx=ip1: e.matmul(
                        pY0[:], f_sb[:, v1 * 128:(v1 + 1) * 128], hc_sb[:, ipx],
                        start=False, stop=(v1 == 15)),
                        inc="PE3")
                    snap(f"mmFp{mg}", "PE3")
                    if v1 == 15:
                        yev = emit("vector", lambda e, yb=yb: e.tensor_copy(
                            ysb[:, yb * SC:(yb + 1) * SC], pY0[:]),
                            waits=(("PE3", mmf),) + at(f"yst{g_idx - 2}"),
                            inc="DV4")
                        snap(f"yev{g_idx}", "DV4")
                        dma("gpsimd",
                            cc2_in[dch * 128:(dch + 1) * 128, tc * SC:(tc + 1) * SC],
                            ysb[:, yb * SC:(yb + 1) * SC],
                            waits=(("DV4", yev),), name=f"SY{g_idx % 2}")
                        snap(f"yst{g_idx}", f"SY{g_idx % 2}")

                for mm in range(NP + 1):
                    if mm < NP:
                        produce(mm)
                    if mm >= 1:
                        consume(mm - 1)
                    if mm < NP:
                        produce_dbx(mm)
            if DEBUG_MAMBA:
                dma("gpsimd", dbg_b[:, 0:T], b_sb[:],
                    waits=(("SY0", C["SY0"]), ("SY1", C["SY1"])), name="SPK")
                dma("gpsimd", dbg_c[:, 0:T], c_sb[:], name="SPK")
                dma("gpsimd", dbg_u13, u_sb[:, 0:4 * SC], name="SPK")
                dma("gpsimd", dbg_yloc, cc2_in,
                    waits=(("SY0", C["SY0"]), ("SY1", C["SY1"])), name="SPK")
                dma("gpsimd", dbg_carry, carry[:],
                    waits=(("DV2", mark["scan63"][1] if "scan63" in mark else 0),),
                    name="SPK")
            cc2_done = emit("gpsimd", lambda e: e.collective_compute(
                "AllGather", OP.bypass, replica_groups=[list(range(NCORES))],
                ins=[cc2_in.opt()], outs=[cc2_out.opt()]),
                waits=(("SY0", C["SY0"]), ("SY1", C["SY1"])), inc="CC")
            scan_dv4 = C["DV4"]

        with ExitStack() as _ctx:
            db_sb = _ctx.enter_context(nc.sbuf_tensor("db_sb", [128, D], F32))
            dstg = _ctx.enter_context(nc.sbuf_tensor("dstg", [1, D], F32))
            xt3 = _ctx.enter_context(nc.sbuf_tensor("xt3", [128, 2 * D], F32))
            yfm = _ctx.enter_context(nc.sbuf_tensor("yfm", [128, 2 * D], BF16))
            xmid = _ctx.enter_context(nc.sbuf_tensor("xmid", [128, 2 * D], F32))
            t3buf = _ctx.enter_context(nc.sbuf_tensor("t3buf", [128, D], F32))
            bnbuf3 = _ctx.enter_context(nc.sbuf_tensor("bnbuf3", [128, 48], F32))
            mv3 = _ctx.enter_context(nc.sbuf_tensor("mv3", [128, 16], F32))
            qm0 = _ctx.enter_context(nc.psum_tensor("qm0", [128, 512], BF16))
            qm1 = _ctx.enter_context(nc.psum_tensor("qm1", [128, 512], BF16))
            mdq = _ctx.enter_context(nc.psum_tensor("mdq", [128, 512], BF16))
            qm = (qm0, qm1)
            if full:
                emit_dq(48, mdq)
                cc_gu0 = emit("gpsimd", lambda e: e.collective_compute(
                    "AllGather", OP.bypass, replica_groups=[list(range(NCORES))],
                    ins=[wsh_gu0.opt()], outs=[wfull_gu0.opt()]),
                    waits=at("dq_b0_0") + at("dq_b0_1"), inc="CC")
                assert "dq_b0_0" in mark and "dq_b0_1" in mark
            wdv = dma("sync", db_sb[:], dvec.broadcast_to([128, D]), name="LDC",
                      waits=(("SY0", C["SY0"]), ("SY1", C["SY1"])))
            snap("db", "LDC")

            for c in range(8):
                par = c % 2
                lts = f"LT{c % 2}"
                w_x = dma("sync", xt3[:, par * D:(par + 1) * D],
                          x_tok[c * 128:(c + 1) * 128, :], waits=at(f"t3{c - 2}"),
                          name=lts)

                def load_yfm(e, c=c, par=par):
                    pid = e.partition_id()
                    srcv = cc2_out.rearrange("(a p) t -> p a t", p=128)
                    sl = srcv[0:128, 0:32, 0:128]
                    ap2 = bass.AP(sl.tensor, sl.offset + pid * TC + c * 128, sl.ap)
                    return e.dma_start(
                        out=yfm[:, par * D:(par + 1) * D]
                        .rearrange("p (a f) -> p a f", f=128), in_=ap2)
                w_y = emit("sync", load_yfm,
                           waits=((("CC", cc2_done),) if c == 0 else ()) +
                                 at(f"xm{c - 2}"),
                           inc=f"LY{c % 2}", inc_n=16)
                emit("vector", lambda e, c=c, par=par: e.tensor_scalar(
                    t3buf[:], xt3[:, par * D:(par + 1) * D],
                    stats[:, c:c + 1], stats[:, 8 + c:9 + c], OP.subtract, OP.mult),
                    waits=((lts, w_x),) + at("db"), inc="DV5")
                emit("vector", lambda e: e.tensor_tensor(
                    t3buf[:], t3buf[:], db_sb[:], OP.mult))
                emit("vector", lambda e, c=c, par=par: e.tensor_tensor(
                    t3buf[:], t3buf[:], xt3[:, par * D:(par + 1) * D], OP.add),
                    inc="DV5")
                snap(f"t3{c}", "DV5")
                for b in range(8):
                    mb = b % 2
                    tpm = None
                    for s in range(4):
                        tpm = emit("tensor", lambda e, b=b, s=s, par=par, mb=mb:
                                   e.transpose(
                            qm[mb][:, s * 128:(s + 1) * 128],
                            yfm[:, par * D + (4 * b + s) * 128:
                                par * D + (4 * b + s + 1) * 128], eyeb[:]),
                            waits=(((f"LY{c % 2}", w_y),) if (b == 0 and s == 0) else ()) +
                                  ((("DV4", C["DV4"]),) if s == 0 else ()),
                            inc="PE4" if s == 3 else None)
                    emit("vector", lambda e, b=b, mb=mb, par=par: e.tensor_tensor(
                        xmid[:, par * D + b * 512:par * D + (b + 1) * 512],
                        t3buf[:, b * 512:(b + 1) * 512], qm[mb][:], OP.add),
                        waits=(("PE4", tpm),) +
                              (at(f"xms{c - 2}") if b == 0 else ()), inc="DV4")
                snap(f"xm{c}", "DV4")
                dma("gpsimd", xmid_hbm[c * 128:(c + 1) * 128, :],
                    xmid[:, par * D:(par + 1) * D], waits=(("DV4", C["DV4"]),),
                    name=f"SXM{c % 2}")
                snap(f"xms{c}", f"SXM{c % 2}")
                if stop_after == "mamba":
                    dma("gpsimd", x_out[c * 128:(c + 1) * 128, :],
                        xmid[:, par * D:(par + 1) * D], name=f"SXM{c % 2}")
                bn3 = None
                for b in range(8):
                    bn3 = emit("vector", lambda e, b=b, par=par: e.bn_stats(
                        bnbuf3[:, 6 * b:6 * b + 6],
                        xmid[:, par * D + 512 * b:par * D + 512 * (b + 1)]),
                        inc="DV2" if b == 7 else None)
                ag3 = emit("vector", lambda e, c=c: e.bn_aggr(
                    mv3[:, 2 * c:2 * c + 2], bnbuf3[:]),
                    waits=(("DV2", bn3),), inc="DV2")
                m3 = emit("vector", lambda e, c=c: e.tensor_copy(
                    st3[:, c:c + 1], mv3[:, 2 * c:2 * c + 1]),
                    waits=(("DV2", ag3),), inc="DV5")
                emit("scalar", lambda e, c=c: e.activation(
                    mv3[:, 2 * c + 1:2 * c + 2], mv3[:, 2 * c + 1:2 * c + 2],
                    AF.Ln, bias=cst[:, 0:1]),
                    waits=(("DV5", m3),))
                emit("scalar", lambda e, c=c: e.activation(
                    st3[:, 8 + c:9 + c], mv3[:, 2 * c + 1:2 * c + 2],
                    AF.Exp, scale=cst[:, 1:2]),
                    inc="AC")
                if full:
                    emit_dq(5, mdq)
            snap("merge_done0", "SXM0")
            snap("merge_done1", "SXM1")
            snap("merge_ac", "AC")
            snap("merge_dv4", "DV4")

        if full:
            with ExitStack() as _ctx:
                xln2T = _ctx.enter_context(nc.sbuf_tensor("xln2T", [128, ND * SC], BF16))
                x3 = _ctx.enter_context(nc.sbuf_tensor("x3", [128, NJ * SC], BF16))
                wstr = _ctx.enter_context(nc.sbuf_tensor("wstr", [128, 2 * NJ * 128], BF16))
                lora_a = _ctx.enter_context(nc.sbuf_tensor("lora_a", [128, (2 * ND + NJ) * 16], BF16))
                lora_b = _ctx.enter_context(nc.sbuf_tensor("lora_b", [96, DFF], BF16))
                uvec = _ctx.enter_context(nc.sbuf_tensor("uvec", [96, 512], BF16))
                xshare = _ctx.enter_context(nc.sbuf_tensor("xshare", [128, 3072], F32))
                xshb = xshare[:].bitcast(BF16)
                xmr = xshare[:, 2048:3072]
                xln2b = lambda: xshb[:, 0:4096]
                r1_sb = xshb[:, 0:512]
                xo_sb = xshare[:, 0:1024]
                xma = xshare[:, 1024:2048]
                xob = xshare[:, 2048:3072]
                q1a = _ctx.enter_context(nc.psum_tensor("q1a", [128, 512], F32))
                q1b = _ctx.enter_context(nc.psum_tensor("q1b", [128, 512], F32))
                q2a = _ctx.enter_context(nc.psum_tensor("q2a", [128, 512], F32))
                q2b = _ctx.enter_context(nc.psum_tensor("q2b", [128, 512], F32))
                qu = _ctx.enter_context(nc.psum_tensor("qu", [96, 512], F32))
                qt0 = _ctx.enter_context(nc.psum_tensor("qt0", [128, 512], F32))
                qt1 = _ctx.enter_context(nc.psum_tensor("qt1", [128, 512], F32))
                qtb = _ctx.enter_context(nc.psum_tensor("qtb", [128, 512], BF16))
                q1 = (q1a, q1b)
                q2 = (q2a, q2b)
                QT = (qt0, qt1, q2a, q2b)

                wb = wstr[:].bitcast(F32)
                wl = dma("sync", wb[:, 0:ND * 16].rearrange("p (c f) -> p c f", f=16),
                         gla.rearrange("(c p) f -> p c f", p=128),
                         waits=at("merge_done0") + at("merge_done1"), name="LLA")
                la1 = emit("vector", lambda e, wb=wb: e.tensor_copy(
                    lora_a[:, 0:ND * 16], wb[:, 0:ND * 16]),
                    waits=(("LLA", wl),), inc="DV5")
                wl = dma("sync", wb[:, 0:ND * 16].rearrange("p (c f) -> p c f", f=16),
                         ula.rearrange("(c p) f -> p c f", p=128),
                         waits=(("DV5", la1),), name="LLA")
                la2 = emit("vector", lambda e, wb=wb: e.tensor_copy(
                    lora_a[:, ND * 16:2 * ND * 16], wb[:, 0:ND * 16]),
                    waits=(("LLA", wl),), inc="DV5")
                wl = dma("sync", wb[:, 0:NJ * 16].rearrange("p (c f) -> p c f", f=16),
                         dla.rearrange("(c p) f -> p c f", p=128),
                         waits=(("DV5", la2),), name="LLA")
                la3 = emit("vector", lambda e, wb=wb: e.tensor_copy(
                    lora_a[:, 2 * ND * 16:(2 * ND + NJ) * 16], wb[:, 0:NJ * 16]),
                    waits=(("LLA", wl),), inc="DV5")
                wl = dma("sync", wb[0:16, 0:DFF], glb, waits=(("DV5", la3),), name="LLA")
                lb1 = emit("vector", lambda e, wb=wb: e.tensor_copy(
                    lora_b[0:16, :], wb[0:16, 0:DFF]), waits=(("LLA", wl),), inc="DV5")
                wl = dma("sync", wb[0:16, 0:DFF], ulb, waits=(("DV5", lb1),), name="LLA")
                lb2 = emit("vector", lambda e, wb=wb: e.tensor_copy(
                    lora_b[32:48, :], wb[0:16, 0:DFF]), waits=(("LLA", wl),), inc="DV5")
                wl = dma("sync", wb[0:16, 0:D], dlb, waits=(("DV5", lb2),), name="LLA")
                emit("vector", lambda e, wb=wb: e.tensor_copy(
                    lora_b[64:80, 0:D], wb[0:16, 0:D]),
                    waits=(("LLA", wl),), inc="DV5")
                snap("lora", "DV5")

                def GLA(dc):
                    return lora_a[:, dc * 16:(dc + 1) * 16]

                def ULA(dc):
                    return lora_a[:, (ND + dc) * 16:(ND + dc + 1) * 16]

                def DLA(ic):
                    return lora_a[:, (2 * ND + ic) * 16:(2 * ND + ic + 1) * 16]

                for hf in range(2):
                    lx = None
                    for cq in range(4):
                        c = hf * 4 + cq
                        for hh in range(4):
                            wx = dma("sync", xmr,
                                     xmid_hbm[c * 128:(c + 1) * 128,
                                              hh * 1024:(hh + 1) * 1024],
                                     waits=(at("merge_done0") + at("merge_done1") +
                                            at("xobst0") + at("xobst1") +
                                            at(f"ffn_lx{hf}:{cq - 1}"))
                                     if hh == 0 else
                                     ((("DV5", lx),) if lx else ()),
                                     name="LXM")
                            lx = emit("vector", lambda e, c=c, hh=hh: e.tensor_scalar(
                                xln2b()[:, hh * 1024:(hh + 1) * 1024], xmr,
                                st3[:, c:c + 1], st3[:, 8 + c:9 + c],
                                OP.subtract, OP.mult),
                                waits=(("LXM", wx),) + at("merge_ac") +
                                      (at(f"ffn_tp{hf}:{cq - 1}") if hh == 0 else ()) +
                                      (at(f"relu{NJ - 1}") if (hh == 0 and cq == 0)
                                       else ()),
                                inc="DV5")
                        snap(f"ffn_lx{hf}:{cq}", "DV5")
                        for dc in range(ND):
                            tpm = emit("tensor", lambda e, dc=dc: e.transpose(
                                qtb[:, 0:128], xln2b()[:, dc * 128:(dc + 1) * 128],
                                eyeb[:]),
                                waits=(("DV5", lx), ("DV4", C["DV4"])) +
                                      (at("merge_dv4") if (hf == 0 and cq == 0 and dc == 0)
                                       else ()),
                                inc="PE4")
                            emit("vector", lambda e, dc=dc, cq=cq: e.tensor_copy(
                                xln2T[:, dc * SC + cq * 128:dc * SC + cq * 128 + 128],
                                qtb[:, 0:128]),
                                waits=(("PE4", tpm),), inc="DV4")
                        snap(f"ffn_tp{hf}:{cq}", "DV4")
                    snap(f"xln2T{hf}", "DV4")
                    mmu = None
                    for dc in range(ND):
                        emit("tensor", lambda e, dc=dc: e.matmul(
                            qu[0:16, :], GLA(dc), xln2T[:, dc * SC:(dc + 1) * SC],
                            start=(dc == 0), stop=(dc == ND - 1)),
                            waits=(at(f"xln2T{hf}") + at("lora") + at(f"uev{hf - 1}"))
                            if dc == 0 else ())
                        mmu = emit("tensor", lambda e, dc=dc: e.matmul(
                            qu[32:48, :], ULA(dc), xln2T[:, dc * SC:(dc + 1) * SC],
                            start=(dc == 0), stop=(dc == ND - 1)),
                            inc="PE4" if dc == ND - 1 else None)
                    ue1 = emit("vector", lambda e: e.tensor_copy(
                        uvec[0:48, :], qu[0:48, :]),
                        waits=(("PE4", mmu),), inc="DV4")
                    snap(f"uev{hf}", "DV4")
                    vfull0 = wfull_gu0.rearrange("(a p) f -> a p f", p=128)
                    vfull1 = wfull_gu1.rearrange("(a p) f -> a p f", p=128)
                    j_order = ([(r, jc) for jc in range(4) for r in range(NCORES)
                                if r * NGU_PAD + jc < NJ] +
                               [(r, jc) for jc in range(4, NGU_PAD) for r in range(NCORES)
                                if r * NGU_PAD + jc < NJ])
                    for jj, (r_, jc_) in enumerate(j_order):
                        j = r_ * NGU_PAD + jc_
                        jp = jj % 2
                        wof = jp * NJ * 128
                        lws = f"LW{jj % 2}"
                        if jc_ < 4:
                            gt = vfull0[(r_ * 4 + jc_) * 64:(r_ * 4 + jc_) * 64 + 32]
                            ut = vfull0[(r_ * 4 + jc_) * 64 + 32:(r_ * 4 + jc_) * 64 + 64]
                        else:
                            gt = vfull1[(r_ * 7 + jc_ - 4) * 64:
                                        (r_ * 7 + jc_ - 4) * 64 + 32]
                            ut = vfull1[(r_ * 7 + jc_ - 4) * 64 + 32:
                                        (r_ * 7 + jc_ - 4) * 64 + 64]
                        dma("sync",
                            wstr[:, wof:wof + ND * 128]
                            .rearrange("p (c f) -> p c f", f=128),
                            gt.transpose([1, 0, 2]),
                            waits=at(f"relu{jj - 2}") +
                                  (at("dmmlast") if jj < 2 else ()) +
                                  (at("lora") if jj == 0 else ()) +
                                  ((("CC", cc_gu0),) if (hf == 0 and jj == 0) else ()) +
                                  ((("CC", mark["ccgu1"][1]),)
                                   if (hf == 0 and jj == 32) else ()),
                            name=lws)
                        wu_l = dma("sync",
                                   wstr[:, wof + ND * 128:wof + 2 * ND * 128]
                                   .rearrange("p (c f) -> p c f", f=128),
                                   ut.transpose([1, 0, 2]), name=lws)
                        emit("tensor", lambda e, j=j, jp=jp: e.matmul(
                            q1[jp][:], lora_b[0:16, j * 128:(j + 1) * 128],
                            uvec[0:16, :], start=True, stop=False),
                            waits=(((lws, wu_l), ("DV4", ue1)) +
                                   at(f"relu{jj - 2}") + at(f"x3m{jj - 2}") +
                                   (at("y3evlast") if jj == 0 else ())))
                        for ic in range(ND):
                            emit("tensor", lambda e, ic=ic, wof=wof, jp=jp: e.matmul(
                                q1[jp][:],
                                wstr[:, wof + ic * 128:wof + (ic + 1) * 128],
                                xln2T[:, ic * SC:(ic + 1) * SC],
                                start=False, stop=(ic == ND - 1)))
                        emit("tensor", lambda e, j=j, jp=jp: e.matmul(
                            q2[jp][:], lora_b[32:48, j * 128:(j + 1) * 128],
                            uvec[32:48, :], start=True, stop=False),
                            waits=(at("xoaddlast") if jj == 0 else ()))
                        gmm = None
                        for ic in range(ND):
                            gmm = emit("tensor", lambda e, ic=ic, wof=wof, jp=jp: e.matmul(
                                q2[jp][:],
                                wstr[:, wof + (ND + ic) * 128:wof + (ND + ic + 1) * 128],
                                xln2T[:, ic * SC:(ic + 1) * SC],
                                start=False, stop=(ic == ND - 1)),
                                inc="PE1" if ic == ND - 1 else None)
                        if jj == NJ - 1:
                            snap("gmmlast", "PE1")
                        rel = emit("scalar", lambda e, jp=jp: e.activation(
                            r1_sb, q1[jp][:], AF.Relu),
                            waits=(("PE1", gmm),) + at(f"x3m{jj - 1}") +
                                  (at(f"ffn_lx{hf}:3") if jj == 0 else ()), inc="AC")
                        snap(f"relu{jj}", "AC")
                        emit("vector", lambda e, j=j, jp=jp: e.tensor_tensor(
                            x3[:, j * SC:(j + 1) * SC], r1_sb, q2[jp][:], OP.mult),
                            waits=(("AC", rel),), inc="DV4")
                        snap(f"x3m{jj}", "DV4")
                        if hf == 0:
                            emit_dq(3, qtb)
                            if jj == 18:
                                snap("ccgu1", "CC")
                                emit("gpsimd", lambda e: e.collective_compute(
                                    "AllGather", OP.bypass,
                                    replica_groups=[list(range(NCORES))],
                                    ins=[wsh_gu1.opt()], outs=[wfull_gu1.opt()]),
                                    waits=at("dq_b1_0") + at("dq_b1_1"), inc="CC")
                                assert "dq_b1_0" in mark and "dq_b1_1" in mark
                                snap("ccgu1", "CC")
                            if jj == 48:
                                snap("ccgd", "CC")
                                emit("gpsimd", lambda e: e.collective_compute(
                                    "AllGather", OP.bypass,
                                    replica_groups=[list(range(NCORES))],
                                    ins=[wsh_d.opt()], outs=[wfull_d.opt()]),
                                    waits=at("dq_b2_0") + at("dq_b2_1"), inc="CC")
                                assert "dq_b2_0" in mark and "dq_b2_1" in mark
                                snap("ccgd", "CC")
                    snap(f"x3done{hf}", "DV4")
                    mmd = None
                    for ic in range(NJ):
                        mmd = emit("tensor", lambda e, ic=ic: e.matmul(
                            qu[64:80, :], DLA(ic), x3[:, ic * SC:(ic + 1) * SC],
                            start=(ic == 0), stop=(ic == NJ - 1)),
                            waits=(at(f"x3done{hf}") + at(f"udev{hf - 1}"))
                            if ic == 0 else (),
                            inc="PE4" if ic == NJ - 1 else None)
                    ud1 = emit("vector", lambda e: e.tensor_copy(
                        uvec[64:80, :], qu[64:80, :]),
                        waits=(("PE4", mmd),), inc="DV4")
                    snap(f"udev{hf}", "DV4")
                    vfd = wfull_d.rearrange("(a p) f -> a p f", p=128)
                    for dm in range(ND):
                        dp = dm % 2
                        dg = dm // 4
                        wof = dp * NJ * 128
                        lws = f"LW{dm % 2}"
                        dt_base = (dm // 4) * 4 * 86 + (dm % 4) * 86
                        dma("sync",
                            wstr[:, wof:wof + 43 * 128]
                            .rearrange("p (c f) -> p c f", f=128),
                            vfd[dt_base:dt_base + 43].transpose([1, 0, 2]),
                            waits=at(f"y3ev{dm - 2}") +
                                  (at("gmmlast") if dm < 2 else ()) +
                                  ((("CC", mark["ccgd"][1]),)
                                   if (hf == 0 and dm == 0) else ()), name=lws)
                        wd2_l = dma("sync",
                                    wstr[:, wof + 43 * 128:wof + 86 * 128]
                                    .rearrange("p (c f) -> p c f", f=128),
                                    vfd[dt_base + 43:dt_base + 86].transpose([1, 0, 2]),
                                    name=lws)
                        emit("tensor", lambda e, dm=dm, dp=dp: e.matmul(
                            q1[dp][:], lora_b[64:80, dm * 128:(dm + 1) * 128],
                            uvec[64:80, :], start=True, stop=False),
                            waits=((lws, wd2_l), ("DV4", ud1)) + at(f"y3ev{dm - 2}"))
                        dmm = None
                        for ic in range(NJ):
                            dmm = emit("tensor", lambda e, ic=ic, wof=wof, dp=dp: e.matmul(
                                q1[dp][:],
                                wstr[:, wof + ic * 128:wof + (ic + 1) * 128],
                                x3[:, ic * SC:(ic + 1) * SC],
                                start=False, stop=(ic == NJ - 1)),
                                inc="PE1" if ic == NJ - 1 else None)
                        if dm == ND - 1:
                            snap("dmmlast", "PE1")
                        y3e = emit("vector", lambda e, dp=dp: e.tensor_copy(
                            xo_sb[:, dp * 512:(dp + 1) * 512], q1[dp][:]),
                            waits=(("PE1", dmm),) + at(f"y3tp{dm - 2}") +
                                  ((("AC", C["AC"]),) if dm == 0 else ()), inc="DV4")
                        snap(f"y3ev{dm}", "DV4")
                        if dm == ND - 1:
                            snap("y3evlast", "DV4")
                        for s in range(4):
                            emit("tensor", lambda e, s=s, dp=dp, dm=dm: e.transpose(
                                QT[s][:, (dm % 4) * 128:(dm % 4) * 128 + 128],
                                xo_sb[:, dp * 512 + s * 128:dp * 512 + (s + 1) * 128],
                                eyef[:]),
                                waits=(("DV4", y3e),) +
                                      (at("xoaddlast") if (dm % 4 == 0 and s == 0)
                                       else ()) +
                                      (at(f"x3m{NJ - 1}") if (dm == 0 and s >= 2)
                                       else ()),
                                inc="PE4" if s == 3 else None)
                        snap(f"y3tp{dm}", "PE4")
                        if dm % 4 == 3:
                            tpg = mark[f"y3tp{dm}"][1]
                            for s in range(4):
                                c = hf * 4 + s
                                wxa = dma("sync", xma[:, (s % 2) * 512:(s % 2 + 1) * 512],
                                          xmid_hbm.rearrange("(a p) f -> a p f", p=128)
                                          [c, :, dg * 512:(dg + 1) * 512],
                                          waits=at(f"xmafree{s % 2}"),
                                          name=f"LMA{s % 2}")
                                xoa = emit("vector",
                                           lambda e, s=s, dg=dg: e.tensor_tensor(
                                    xob[:, (s % 2) * 512:(s % 2 + 1) * 512],
                                    xma[:, (s % 2) * 512:(s % 2 + 1) * 512],
                                    QT[s][:], OP.add),
                                    waits=(("PE4", tpg), (f"LMA{s % 2}", wxa)) +
                                          at(f"xobst{s % 2}"),
                                    inc="DV4")
                                snap(f"xmafree{s % 2}", "DV4")
                                xst = dma("gpsimd",
                                          x_out[c * 128:(c + 1) * 128,
                                                dg * 512:(dg + 1) * 512],
                                          xob[:, (s % 2) * 512:(s % 2 + 1) * 512],
                                          waits=(("DV4", xoa),), name=f"SXO{s % 2}")
                                snap(f"xobst{s % 2}", f"SXO{s % 2}")
                            snap("xoaddlast", "DV4")

        store_names = ["SPK", "SY0", "SY1", "SXM0", "SXM1", "SXO0", "SXO1"]
        fin = [(dma_sems[n], C[n]) for n in store_names if C[n] > 0]

        def final_wait(e, fin=fin):
            for s, v in fin:
                e.wait_ge(s, v)
        emit("gpsimd", final_wait)

        @blk.sync
        def _(e):
            for fn in P.queues["sync"]:
                fn(e)

        @blk.gpsimd
        def _(e):
            for fn in P.queues["gpsimd"]:
                fn(e)

        @blk.vector
        def _(e):
            for fn in P.queues["vector"]:
                fn(e)

        @blk.scalar
        def _(e):
            for fn in P.queues["scalar"]:
                fn(e)

        @blk.tensor
        def _(e):
            for fn in P.queues["tensor"]:
                fn(e)

    return nc


def _fwd_inputs(k, x2, A_log, Dv, x_proj_w, dt_proj_w, dt_proj_b, consts,
                wtiles=None):
    a_sl = np.ascontiguousarray(A_log[k * DSL:(k + 1) * DSL]).reshape(-1)
    a_pre = np.ascontiguousarray(a_sl.reshape(NDN, 128).T)
    d = dict(
        x_tok=np.ascontiguousarray(x2[k * TC:(k + 1) * TC]),
        x_ds=np.ascontiguousarray(x2[:, k * DSL:(k + 1) * DSL]),
        xpw=x_proj_w,
        dtw=np.ascontiguousarray(dt_proj_w[k * DSL:(k + 1) * DSL]),
        dtb=np.ascontiguousarray(dt_proj_b[k * DSL:(k + 1) * DSL]).reshape(DSL, 1),
        a_pre=a_pre,
        dvec=Dv.reshape(1, D),
    )
    d.update(consts)
    if wtiles is not None:
        d.update(wtiles)
    return d


def _consts():
    ev = np.zeros((16, 128, 128), dtype=BF16NP)
    fv = np.zeros((16, 128, 128), dtype=BF16NP)
    for v in range(16):
        for p in range(128):
            r = 8 * v + p // 16
            ev[v, r, p] = 1
            fv[v, p, r] = 1
    g = np.zeros((16, 128), dtype=BF16NP)
    for p in range(128):
        g[p % 16, p] = 1
    return dict(
        e_cat=ev, f_cat=fv, g16=g,
        eye_bf=np.eye(128, dtype=BF16NP),
        eye_f32=np.eye(128, dtype=np.float32),
    )


def kernel(**inp):
    x2 = np.ascontiguousarray(np.asarray(inp["L0_attn_x"])[0])
    consts = _consts()
    R = NGU_PAD * 128
    gcod = _pad_rows(np.asarray(inp["L0_ffn_g_codes"]), R * NCORES)
    gam = _pad_rows(np.asarray(inp["L0_ffn_g_absmax"]), R * NCORES)
    ucod = _pad_rows(np.asarray(inp["L0_ffn_u_codes"]), R * NCORES)
    uam = _pad_rows(np.asarray(inp["L0_ffn_u_absmax"]), R * NCORES)
    dcodf = np.asarray(inp["L0_ffn_d_codes"])
    damf = np.asarray(inp["L0_ffn_d_absmax"])

    nc = build_forward("full")
    in_maps = []
    for k in range(NCORES):
        m = _fwd_inputs(k, x2,
                        np.asarray(inp["L0_attn_A_log"]),
                        np.asarray(inp["L0_attn_D"]),
                        np.asarray(inp["L0_attn_x_proj_w"]),
                        np.asarray(inp["L0_attn_dt_proj_w"]),
                        np.asarray(inp["L0_attn_dt_proj_b"]),
                        consts)
        m.update(dict(
            gc=np.ascontiguousarray(gcod[k * R:(k + 1) * R]).astype(np.int8),
            ga=np.ascontiguousarray(gam[k * R:(k + 1) * R]),
            uc=np.ascontiguousarray(ucod[k * R:(k + 1) * R]).astype(np.int8),
            ua=np.ascontiguousarray(uam[k * R:(k + 1) * R]),
            dcod=np.ascontiguousarray(dcodf[k * 512:(k + 1) * 512]).astype(np.int8),
            dam=np.ascontiguousarray(damf[k * 512:(k + 1) * 512]),
            gla=np.asarray(inp["L0_ffn_g_lora_a"]),
            glb=np.asarray(inp["L0_ffn_g_lora_b"]),
            ula=np.asarray(inp["L0_ffn_u_lora_a"]),
            ulb=np.asarray(inp["L0_ffn_u_lora_b"]),
            dla=np.asarray(inp["L0_ffn_d_lora_a"]),
            dlb=np.asarray(inp["L0_ffn_d_lora_b"]),
        ))
        in_maps.append(m)
    res = run_bass_kernel_spmd(nc, in_maps, core_ids=list(range(NCORES)))
    out = np.concatenate([res.results[k]["x_out"] for k in range(NCORES)], axis=0)
    return out[None]


NGU_PAD = 11


def build_dequant():
    nc = bass.Bass("TRN2", target_bir_lowering=False, debug=False)
    gc = nc.dram_tensor("gc", [NGU_PAD * 128, D], I32, kind="ExternalInput").ap()
    ga = nc.dram_tensor("ga", [NGU_PAD * 128, D // 64], F32, kind="ExternalInput").ap()
    uc = nc.dram_tensor("uc", [NGU_PAD * 128, D], I32, kind="ExternalInput").ap()
    ua = nc.dram_tensor("ua", [NGU_PAD * 128, D // 64], F32, kind="ExternalInput").ap()
    dcod = nc.dram_tensor("dcod", [4 * 128, DFF], I32, kind="ExternalInput").ap()
    dam = nc.dram_tensor("dam", [4 * 128, DFF // 64], F32, kind="ExternalInput").ap()
    eye_bf = nc.dram_tensor("eye_bf", [128, 128], BF16, kind="ExternalInput").ap()

    og = nc.dram_tensor("og", [NGU_PAD, ND, 128, 128], BF16, kind="ExternalOutput").ap()
    ou = nc.dram_tensor("ou", [NGU_PAD, ND, 128, 128], BF16, kind="ExternalOutput").ap()
    od = nc.dram_tensor("od", [4, NJ, 128, 128], BF16, kind="ExternalOutput").ap()

    with ExitStack() as _ctx:
        eyeb = _ctx.enter_context(nc.sbuf_tensor("eyeb", [128, 128], BF16))
        cod = _ctx.enter_context(nc.sbuf_tensor("cod", [128, 2 * 512], I32))
        amx = _ctx.enter_context(nc.sbuf_tensor("amx", [128, 2 * 8], F32))
        accs = _ctx.enter_context(nc.sbuf_tensor("accs", [128, 512], BF16))
        tmp = _ctx.enter_context(nc.sbuf_tensor("tmp", [128, 512], BF16))
        wq = _ctx.enter_context(nc.sbuf_tensor("wq", [128, 2 * 512], BF16))
        obuf = _ctx.enter_context(nc.sbuf_tensor("obuf", [128, 2 * 512], BF16))
        ptr0 = _ctx.enter_context(nc.psum_tensor("ptr0", [128, 512], BF16))
        ptr1 = _ctx.enter_context(nc.psum_tensor("ptr1", [128, 512], BF16))
        sLD = _ctx.enter_context(nc.semaphore("sLD"))
        sLQ1 = _ctx.enter_context(nc.semaphore("sLQ1"))
        sST = _ctx.enter_context(nc.semaphore("sST"))
        sSQ1 = _ctx.enter_context(nc.semaphore("sSQ1"))
        sDV = _ctx.enter_context(nc.semaphore("sDV"))
        sPE = _ctx.enter_context(nc.semaphore("sPE"))
        blk = _ctx.enter_context(nc.Block())
        P = Prog(dict(LQ0=sLD, LQ1=sLQ1, SQ0=sST, SQ1=sSQ1, DV=sDV, PE=sPE))
        emit, dma, snap, at, C = P.emit, P.dma, P.snap, P.at, P.C
        ptr = (ptr0, ptr1)
        w_eye = dma("sync", eyeb[:], eye_bf, name="LQ0")
        git = [0]

        def one_matrix(codes, am, out_t, nrow, icols):
            nib = (icols + 511) // 512
            for jc in range(nrow):
                for ib in range(nib):
                    it = git[0]
                    git[0] += 1
                    par = it % 2
                    iw = min(512, icols - ib * 512)
                    nt = iw // 128
                    nb = iw // 64
                    lqs = f"LQ{it % 2}"
                    dma("sync", cod[:, par * 512:par * 512 + iw],
                        codes[jc * 128:(jc + 1) * 128, ib * 512:ib * 512 + iw],
                        waits=at(f"dqlut{it - 2}"), name=lqs)
                    w_a = dma("sync", amx[:, par * 8:par * 8 + nb],
                              am[jc * 128:(jc + 1) * 128, ib * 8:ib * 8 + nb],
                              name=lqs)
                    csl = cod[:, par * 512:par * 512 + iw]
                    emit("vector", lambda e, csl=csl, iw=iw: e.tensor_scalar(
                        accs[:, 0:iw], csl, 0.0, float(NF4[0]),
                        OP.is_equal, OP.mult),
                        waits=((lqs, w_a),) + at(f"dqw{it - 2}"))
                    for k in range(1, 16):
                        emit("vector", lambda e, csl=csl, iw=iw, k=k: e.tensor_scalar(
                            tmp[:, 0:iw], csl, float(k), float(NF4[k]),
                            OP.is_equal, OP.mult))
                        emit("vector", lambda e, iw=iw: e.tensor_add(
                            accs[:, 0:iw], accs[:, 0:iw], tmp[:, 0:iw]))
                    lut = emit("vector", lambda e, par=par, iw=iw, nb=nb: e.tensor_tensor(
                        wq[:, par * 512:par * 512 + iw]
                        .rearrange("p (b z) -> p b z", z=64),
                        accs[:, 0:iw].rearrange("p (b z) -> p b z", z=64),
                        amx[:, par * 8:par * 8 + nb].unsqueeze(2)
                        .broadcast_to([128, nb, 64]),
                        OP.mult), inc="DV")
                    snap(f"dqlut{it}", "DV")
                    tpm = None
                    for s in range(nt):
                        tpm = emit("tensor", lambda e, s=s, par=par, tb=par: e.transpose(
                            ptr[tb][:, s * 128:(s + 1) * 128],
                            wq[:, par * 512 + s * 128:par * 512 + (s + 1) * 128],
                            eyeb[:]),
                            waits=((("DV", lut),) + at(f"dqev{it - 2}") +
                                   ((("LQ0", w_eye),) if it == 0 else ()))
                            if s == 0 else (),
                            inc="PE" if s == nt - 1 else None)
                    ev = emit("vector", lambda e, par=par, iw=iw, tb=par: e.tensor_copy(
                        obuf[:, par * 512:par * 512 + iw], ptr[tb][:, 0:iw]),
                        waits=(("PE", tpm),) + at(f"dqst{it - 2}"), inc="DV")
                    snap(f"dqev{it}", "DV")
                    snap(f"dqw{it}", "DV")
                    dma("gpsimd", out_t[jc, ib * 4:ib * 4 + nt].transpose([1, 0, 2]),
                        obuf[:, par * 512:par * 512 + iw]
                        .rearrange("p (c f) -> p c f", f=128),
                        waits=(("DV", ev),), name=f"SQ{it % 2}")
                    snap(f"dqst{it}", f"SQ{it % 2}")

        one_matrix(gc, ga, og, NGU_PAD, D)
        one_matrix(uc, ua, ou, NGU_PAD, D)
        one_matrix(dcod, dam, od, 4, DFF)
        fin = [(sST, C["SQ0"]), (sSQ1, C["SQ1"])]

        def final_wait(e, fin=fin):
            for s, v in fin:
                e.wait_ge(s, v)
        emit("gpsimd", final_wait)

        @blk.sync
        def _(e):
            for fn in P.queues["sync"]:
                fn(e)

        @blk.gpsimd
        def _(e):
            for fn in P.queues["gpsimd"]:
                fn(e)

        @blk.vector
        def _(e):
            for fn in P.queues["vector"]:
                fn(e)

        @blk.scalar
        def _(e):
            for fn in P.queues["scalar"]:
                fn(e)

        @blk.tensor
        def _(e):
            for fn in P.queues["tensor"]:
                fn(e)
    return nc


def _pad_rows(a, rows):
    if a.shape[0] == rows:
        return a
    pad = np.zeros((rows - a.shape[0],) + a.shape[1:], dtype=a.dtype)
    return np.concatenate([a, pad], axis=0)


def _run_dequant(inp):
    gcod = _pad_rows(np.asarray(inp["L0_ffn_g_codes"]), NGU_PAD * 128 * NCORES)
    gam = _pad_rows(np.asarray(inp["L0_ffn_g_absmax"]), NGU_PAD * 128 * NCORES)
    ucod = _pad_rows(np.asarray(inp["L0_ffn_u_codes"]), NGU_PAD * 128 * NCORES)
    uam = _pad_rows(np.asarray(inp["L0_ffn_u_absmax"]), NGU_PAD * 128 * NCORES)
    dcod = np.asarray(inp["L0_ffn_d_codes"])
    dam = np.asarray(inp["L0_ffn_d_absmax"])
    eye = np.eye(128, dtype=BF16NP)
    R = NGU_PAD * 128
    nc = build_dequant()
    in_maps = []
    for k in range(NCORES):
        in_maps.append(dict(
            gc=np.ascontiguousarray(gcod[k * R:(k + 1) * R]).astype(np.int8),
            ga=np.ascontiguousarray(gam[k * R:(k + 1) * R]),
            uc=np.ascontiguousarray(ucod[k * R:(k + 1) * R]).astype(np.int8),
            ua=np.ascontiguousarray(uam[k * R:(k + 1) * R]),
            dcod=np.ascontiguousarray(dcod[k * 512:(k + 1) * 512]),
            dam=np.ascontiguousarray(dam[k * 512:(k + 1) * 512]),
            eye_bf=eye,
        ))
    res = run_bass_kernel_spmd(nc, in_maps, core_ids=list(range(NCORES)))
    wg = np.concatenate([res.results[k]["og"] for k in range(NCORES)], axis=0)[:NJ]
    wu = np.concatenate([res.results[k]["ou"] for k in range(NCORES)], axis=0)[:NJ]
    wd = np.concatenate([res.results[k]["od"] for k in range(NCORES)], axis=0)
    return wg, wu, wd
